# version 04
# baseline (speedup 1.0000x reference)
import contextlib
import numpy as np
import ml_dtypes
import concourse.bass as bass
import concourse.mybir as mybir
from concourse.bass_utils import run_bass_kernel_spmd

F32 = mybir.dt.float32
BF16 = mybir.dt.bfloat16
I32 = mybir.dt.int32
AF = mybir.ActivationFunctionType
ALU = mybir.AluOpType
AX = mybir.AxisListType

D = 4096
S = 2048
TOWN = 1024
NCH = 32
NH = 32
NG = 8
SCALE = float(1.0 / np.sqrt(128.0))
EXPB = -30.0
VISLEN = 614
ROPE_BASE = 10000.0
TWO_PI = float(2.0 * np.pi)
INVSQRT2 = float(1.0 / np.sqrt(2.0))


class Ctr:
    def __init__(self, sem, step=1):
        self.sem = sem
        self.n = 0
        self.step = step
        self._last = None

    def inc(self, inst, k=1):
        if inst is self._last:
            return self.n
        inst.then_inc(self.sem, k * self.step)
        self._last = inst
        self.n += k * self.step
        return self.n


_SYNCED_OPS = {
    "tensor_tensor", "tensor_add", "tensor_sub", "tensor_mul", "tensor_max",
    "tensor_scalar", "tensor_single_scalar", "tensor_scalar_add",
    "tensor_scalar_sub", "tensor_scalar_mul", "tensor_scalar_max",
    "tensor_scalar_min", "scalar_tensor_tensor", "tensor_copy", "tensor_reduce",
    "reduce_sum", "reduce_max", "reciprocal", "memset", "memzero", "iota",
    "activation", "copy", "add", "mul", "square", "sqrt", "select",
    "tensor_relu", "affine_select", "sign",
}


class SEng:

    def __init__(self, eng, ctr):
        self._e = eng
        self._c = ctr

    def __getattr__(self, name):
        f = getattr(self._e, name)
        if name in _SYNCED_OPS:
            def wrapped(*a, **k):
                if self._c.n > 0:
                    self._e.wait_ge(self._c.sem, self._c.n)
                inst = f(*a, **k)
                self._c.inc(inst)
                return inst
            return wrapped
        return f


class Prog:
    def __init__(self, debug=0, phases=5, p1_quarters=4, p1_groups=16,
                 p1_tiles=4, p1_skip=""):
        self.phases = phases
        self.p1_quarters = p1_quarters
        self.p1_groups = p1_groups
        self.p1_tiles = p1_tiles
        self.p1_skip = set(p1_skip.split(",")) if p1_skip else set()
        self.debug = debug
        self.nc = bass.Bass("TRN2", target_bir_lowering=False)
        self.es = contextlib.ExitStack()
        self.dbg = {}

    def T(self, fn):
        self.blk.tensor(fn)

    def V(self, fn):
        self.blk.vector(lambda e: fn(SEng(e, self.cV)))

    def C(self, fn):
        self.blk.scalar(lambda e: fn(SEng(e, self.cC)))

    def G(self, fn):
        self.blk.gpsimd(lambda e: fn(SEng(e, self.cG)))

    def S(self, fn):
        self.blk.sync(fn)

    def fence(self):
        marks = [(c, c.n) for c in
                 ([self.cP, self.cV, self.cC, self.cG] + self._alldsems)
                 if c.n > 0]

        def _(e):
            for c, v in marks:
                e.wait_ge(c.sem, v)
        for emit in (self.T, self.V, self.C, self.S, self.G):
            emit(_)

    def sb(self, name, shape, dt, stack=None):
        return (stack or self.es).enter_context(self.nc.sbuf_tensor(name, shape, dt))

    def ps(self, name, shape, dt, stack):
        return stack.enter_context(self.nc.psum_tensor(name, shape, dt))

    def build(self):
        nc = self.nc

        def dram(name, shape, dt, kind):
            return nc.dram_tensor(name, shape, dt, kind=kind).ap()

        self.xf = dram("xf", [S, D], F32, "ExternalInput")
        self.xo = dram("xo", [TOWN, D], F32, "ExternalInput")
        self.wq = dram("wq", [D, D], F32, "ExternalInput")
        self.wk = dram("wk", [D, 1024], F32, "ExternalInput")
        self.wv = dram("wv", [D, 1024], F32, "ExternalInput")
        self.wo = dram("wo", [D, D], F32, "ExternalInput")
        self.w1 = dram("w1", [3, D, D], F32, "ExternalInput")
        self.w2 = dram("w2", [3, D, D], F32, "ExternalInput")
        self.gwpc = dram("gwpc", [128, 96], F32, "ExternalInput")
        self.gwlast = dram("gwlast", [1, 3], F32, "ExternalInput")
        self.gbin = dram("gbin", [1, 3], F32, "ExternalInput")
        self.b1pc = dram("b1pc", [128, 96], F32, "ExternalInput")
        self.b2in = dram("b2in", [3, D], F32, "ExternalInput")
        self.masksin = dram("masksin", [16, 128, 512], BF16, "ExternalInput")
        self.selvtin = dram("selvtin", [128, 32], F32, "ExternalInput")
        self.parcolin = dram("parcolin", [128, 1], F32, "ExternalInput")
        self.idfin = dram("idfin", [128, 128], F32, "ExternalInput")
        self.idbin = dram("idbin", [128, 128], BF16, "ExternalInput")

        self.xnt_hbm = dram("xnt_hbm", [128, NCH * TOWN], BF16, "Internal")
        self.q_hbm = dram("q_hbm", [128, NH * TOWN], BF16, "Internal")
        self.ctx_hbm = dram("ctx_hbm", [128, 8 * D], BF16, "Internal")
        self.scr = dram("scr", [TOWN, D], F32, "Internal")
        self.out = dram("out", [TOWN, D], F32, "ExternalOutput")
        if self.debug >= 1:
            self.dbg["xntq"] = dram("d_xntq", [128, NCH * 512], BF16, "ExternalOutput")
            self.dbg["kfeat"] = dram("d_kfeat", [128, NG * S], BF16, "ExternalOutput")
            self.dbg["vtok"] = dram("d_vtok", [128, 16 * 1024], BF16, "ExternalOutput")
            self.dbg["misc"] = dram("d_misc", [128, 64], F32, "ExternalOutput")
        if self.debug >= 2:
            self.dbg["xnt"] = dram("d_xnt", [128, NCH * TOWN], BF16, "ExternalOutput")
            self.dbg["qh"] = dram("d_qh", [128, NH * TOWN], BF16, "ExternalOutput")
            self.dbg["coef"] = dram("d_coef", [128, 32], F32, "ExternalOutput")
        if self.debug >= 3:
            self.dbg["ctx"] = dram("d_ctx", [128, 8 * D], BF16, "ExternalOutput")
        if self.debug >= 4:
            self.dbg["scr"] = dram("d_scr", [TOWN, D], F32, "ExternalOutput")

        sems = {k: self.es.enter_context(nc.semaphore(k)) for k in
                ["sp", "sv", "sc", "sg"]}
        self.cP = Ctr(sems["sp"])
        self.cV = Ctr(sems["sv"])
        self.cC = Ctr(sems["sc"])
        self.cG = Ctr(sems["sg"])
        self._alldsems = []

        def dsem(name):
            c = Ctr(self.es.enter_context(nc.semaphore(name)), 16)
            self._alldsems.append(c)
            return c
        self.dsem = dsem
        self.sConst = dsem("dconst")
        self.sX = [dsem(f"dx{r}") for r in range(3)]
        self.sWS = [dsem(f"dws{r}") for r in range(2)]
        self.sQS = [dsem(f"dqs{r}") for r in range(2)]
        self.sQB = [dsem(f"dqb{r}") for r in range(3)]
        self.sCS = [dsem(f"dcs{r}") for r in range(2)]
        self.sCX = [dsem(f"dcx{r}") for r in range(2)]
        self.sOS = [dsem(f"dos{r}") for r in range(2)]
        self.sSI = [dsem(f"dsi{r}") for r in range(2)]
        self.sXnt = dsem("dxnt")
        self.sMask = dsem("dmask")
        self.sDbg = dsem("ddbg")

        sb = self.sb
        self.idf = sb("s_idf", [128, 128], F32)
        self.idb = sb("s_idb", [128, 128], BF16)
        self.gw = sb("s_gw", [128, 96], F32)
        self.b1t = sb("s_b1t", [128, 96], F32)
        self.selvt = sb("s_selvt", [128, 32], F32)
        self.parcol = sb("s_parcol", [128, 1], F32)
        self.gwl = sb("s_gwl", [1, 3], F32)
        self.gb = sb("s_gb", [1, 3], F32)
        self.ones_row = sb("s_ones_row", [1, 128], F32)
        self.ones_colf = sb("s_ones_colf", [128, 1], F32)
        self.ones_colb = sb("s_ones_colb", [128, 1], BF16)
        self.iota_col = sb("s_iota_col", [128, 1], F32)
        self.eps_col = sb("s_eps_col", [128, 1], F32)
        self.negb_col = sb("s_negb_col", [128, 1], F32)
        self.invfreq = sb("s_invfreq", [128, 64], F32)
        self.ctabf = sb("s_ctabf", [128, 1024], F32)
        self.stabf = sb("s_stabf", [128, 1024], F32)
        self.ctabo = sb("s_ctabo", [128, 512], F32)
        self.stabo = sb("s_stabo", [128, 512], F32)
        self.absstage = sb("s_absstage", [128, 16], F32)
        self.coefT = sb("s_coefT", [3, TOWN], F32)
        self.ohb = sb("s_ohb", [128, 8], F32)
        self.sel3 = sb("s_sel3", [3, 384], F32)
        self.ohrow = sb("s_ohrow", [1, 8], F32)
        self.ratio = sb("s_ratio", [1, 8], F32)
        self.gcorr = sb("s_gcorr", [1, 4], F32)
        self.stat = sb("s_stat", [128, 16], F32)

        self.blk = self.es.enter_context(nc.Block())
        self._init_consts()

        kvf = contextlib.ExitStack()
        self.kfeat = sb("s_kfeat", [128, NG * S], BF16, kvf)
        self.vtok = sb("s_vtok", [128, 16 * 1024], BF16, kvf)
        if self.phases >= 1:
            self._p1()
            self.fence()
        if self.phases >= 2:
            self._p2()
            self.fence()
        if self.phases >= 3:
            self._p3()
            self.fence()
        kvf.close()
        if self.phases >= 4:
            self._p4()
            self.fence()
        if self.phases >= 5:
            self._p5()
        self._finish()
        self.es.close()
        return self

    def _init_consts(self):
        cA = self.sConst
        cV, cC, cG = self.cV, self.cC, self.cG
        fr = contextlib.ExitStack()
        i32b = self.sb("i_i32b", [128, 1024], I32, fr)
        fb = self.sb("i_fb", [128, 1024], F32, fr)
        ub = self.sb("i_ub", [128, 1024], F32, fr)

        def _(sy):
            for dst, src in [(self.idf, self.idfin), (self.idb, self.idbin),
                             (self.gw, self.gwpc), (self.b1t, self.b1pc),
                             (self.selvt, self.selvtin),
                             (self.parcol, self.parcolin), (self.gwl, self.gwlast),
                             (self.gb, self.gbin)]:
                cA.inc(sy.dma_start(out=dst[:, :], in_=src[:, :]))
        self.S(_)
        w_const = cA.n

        def _(g):
            g.memset(self.ones_row[:, :], 1.0)
            g.memset(self.ones_colf[:, :], 1.0)
            g.memset(self.ones_colb[:, :], 1.0)
            g.memset(self.eps_col[:, :], 1e-5)
            g.memset(self.negb_col[:, :], EXPB)
            g.memset(self.ohb[:, :], 0.0)
            g.memset(self.ohrow[:, :], 0.0)
            g.memset(self.ratio[:, :], 0.0)
            g.iota(self.iota_col[:, :], [[0, 1]], base=0, channel_multiplier=1,
                   allow_small_or_imprecise_dtypes=True)
            g.iota(self.sel3[:, :], [[0, 384]], base=0, channel_multiplier=1,
                   allow_small_or_imprecise_dtypes=True)
            last = g.iota(self.invfreq[:, :], [[1, 64]], base=0, channel_multiplier=0,
                          allow_small_or_imprecise_dtypes=True)
            cG.inc(last)
        self.G(_)
        w_ms = cG.n

        def _(sc):
            sc.wait_ge(cG.sem, w_ms)
            last = sc.activation(self.invfreq[:, :], self.invfreq[:, :], AF.Exp,
                                 scale=float(-np.log(ROPE_BASE) / 64.0))
            cC.inc(last)
        self.C(_)

        def _(v):
            v.wait_ge(cG.sem, w_ms)
            for e in range(3):
                v.tensor_single_scalar(self.sel3[:, 128 * e:128 * (e + 1)],
                                       self.sel3[:, 128 * e:128 * (e + 1)],
                                       float(e), ALU.is_equal)
        self.V(_)
        w_if = cC.n
        waits = {}

        def gen_tab(name, ctab, stab, bases, use_par, nf):
            def _(v):
                v.wait_ge(cC.sem, w_if)
                v.wait_ge(cA.sem, w_const)
                for j, b in enumerate(bases):
                    v.tensor_scalar(fb[:, 1020:1021], self.iota_col[:, 0:1],
                                    float(b), None, ALU.add)
                    if use_par:
                        v.tensor_tensor(fb[:, 1020:1021], fb[:, 1020:1021],
                                        self.parcol[:, 0:1], ALU.add)
                    v.tensor_scalar(ub[:, 64 * j:64 * j + 64], self.invfreq[:, :],
                                    fb[:, 1020:1021], float(1.0 / TWO_PI),
                                    ALU.mult, ALU.mult)
                def frac_pm_half(dst):
                    v.tensor_copy(i32b[:, 0:nf], ub[:, 0:nf])
                    v.tensor_copy(fb[:, 0:nf], i32b[:, 0:nf])
                    v.tensor_tensor(dst[:, 0:nf], ub[:, 0:nf], fb[:, 0:nf],
                                    ALU.subtract)
                    v.tensor_single_scalar(fb[:, 0:nf], dst[:, 0:nf], 0.5,
                                           ALU.is_gt)
                    return v.tensor_tensor(dst[:, 0:nf], dst[:, 0:nf],
                                           fb[:, 0:nf], ALU.subtract)
                cV.inc(frac_pm_half(stab))
                waits[name + "s"] = cV.n
                v.tensor_scalar(ub[:, 0:nf], ub[:, 0:nf], 0.25, None, ALU.add)
                cV.inc(frac_pm_half(ctab))
                waits[name + "c"] = cV.n
            self.V(_)

            def _(sc):
                sc.wait_ge(cV.sem, waits[name + "s"])
                sc.activation(stab[:, 0:nf], stab[:, 0:nf], AF.Sin, scale=TWO_PI)
                sc.wait_ge(cV.sem, waits[name + "c"])
                last = sc.activation(ctab[:, 0:nf], ctab[:, 0:nf], AF.Sin, scale=TWO_PI)
                cC.inc(last)
            self.C(_)

        gen_tab("f", self.ctabf, self.stabf, [128 * t for t in range(16)], False, 1024)
        gen_tab("o", self.ctabo, self.stabo, [256 * t for t in range(8)], True, 512)
        self.w_tabs = cC.n
        self.w_const = w_const
        self.fence()
        fr.close()

    def _ln_tile(self, src_ap, r, xn_dst, abs_col, marks_out):
        cV, cC = self.cV, self.cC
        cA = self.sX[r]
        xbuf, st = self.xbuf, self.stat
        sc0 = 8 * r
        m = {}

        def _(sy):
            c, v0 = self.xb_free[r]
            sy.wait_ge(c.sem, v0)
            cA.inc(sy.dma_start(out=xbuf[r][:, :], in_=src_ap))
            self.xb_ready[r] = cA.n
        self.S(_)

        def _(v):
            v.wait_ge(cA.sem, self.xb_ready[r])
            c, v0 = self.stat_free[r]
            v.wait_ge(c.sem, v0)
            v.reduce_sum(st[:, sc0:sc0 + 1], xbuf[r][:, :], AX.X)
            v.tensor_scalar(st[:, sc0 + 1:sc0 + 2], st[:, sc0:sc0 + 1],
                            float(1.0 / D), None, ALU.mult)
            v.tensor_tensor(st[:, sc0 + 2:sc0 + 3], st[:, sc0 + 1:sc0 + 2],
                            st[:, sc0 + 1:sc0 + 2], ALU.mult)
            c, v0 = self.xn_free[r]
            v.wait_ge(c.sem, v0)
            c, v0 = self.absmark[r]
            if c is not None:
                v.wait_ge(c.sem, v0)
            v.scalar_tensor_tensor(xn_dst[:, :], xbuf[r][:, :], 1.0, xbuf[r][:, :],
                                   ALU.mult, ALU.mult,
                                   accum_out=st[:, sc0 + 3:sc0 + 4])
            last = v.scalar_tensor_tensor(st[:, sc0 + 4:sc0 + 5], st[:, sc0 + 3:sc0 + 4],
                                          float(1.0 / D), st[:, sc0 + 2:sc0 + 3],
                                          ALU.mult, ALU.subtract)
            cV.inc(last)
            m["var"] = cV.n
        self.V(_)

        def _(sc):
            sc.wait_ge(cV.sem, m["var"])
            sc.activation(st[:, sc0 + 5:sc0 + 6], st[:, sc0 + 4:sc0 + 5], AF.Ln,
                          bias=self.eps_col[:, 0:1])
            last = sc.activation(st[:, sc0 + 6:sc0 + 7], st[:, sc0 + 5:sc0 + 6],
                                 AF.Exp, scale=-0.5)
            cC.inc(last)
            m["rstd"] = cC.n
        self.C(_)

        def _(v):
            v.wait_ge(cC.sem, m["rstd"])
            v.scalar_tensor_tensor(st[:, sc0 + 7:sc0 + 8], st[:, sc0 + 1:sc0 + 2],
                                   -1.0, st[:, sc0 + 6:sc0 + 7],
                                   ALU.mult, ALU.mult)
            last = v.tensor_scalar(xn_dst[:, :], xbuf[r][:, :],
                                   st[:, sc0 + 6:sc0 + 7], st[:, sc0 + 7:sc0 + 8],
                                   ALU.mult, ALU.add)
            cV.inc(last)
            marks_out["xn"] = (cV, cV.n)
            self.xb_free[r] = (cV, cV.n)
            self.stat_free[r] = (cV, cV.n)
            if abs_col is not None:
                last = v.reduce_sum(self.absstage[:, abs_col:abs_col + 1],
                                    xn_dst[:, :], AX.X, apply_absolute_value=True)
                cV.inc(last)
                self.absmark[r] = (cV, cV.n)
        self.V(_)

    def _p1(self):
        cP, cV, cC = self.cP, self.cV, self.cC
        fr = contextlib.ExitStack()
        self.xbuf = [self.sb(f"p1_x{r}", [128, D], F32, fr) for r in range(2)]
        xnb = [self.sb(f"p1_xn{r}", [128, D], BF16, fr) for r in range(2)]
        xntq = self.sb("p1_xntq", [128, NCH * 512], BF16, fr)
        wslab = [self.sb(f"p1_w{r}", [128, NCH * 128], BF16, fr) for r in range(2)]
        ktmp = self.sb("p1_ktmp", [128, 512], F32, fr)
        ktok = self.sb("p1_ktok", [128, 512], F32, fr)
        krot = self.sb("p1_krot", [128, 512], BF16, fr)
        vtmp = self.sb("p1_vtmp", [128, 512], BF16, fr)
        rt1 = self.sb("p1_rt1", [128, 256], F32, fr)
        rt2 = self.sb("p1_rt2", [128, 256], F32, fr)
        kb = [self.ps(f"p1kb{r}", [128, 512], F32, fr) for r in range(2)]
        vb = [self.ps(f"p1vb{r}", [128, 512], F32, fr) for r in range(2)]
        ft = self.ps("p1ft", [128, 512], F32, fr)
        bfb = [self.ps(f"p1bf{r}", [128, 1024], BF16, fr) for r in range(2)]

        self.xb_free = [(self.sX[0], 0), (self.sX[1], 0)]
        self.xb_ready = [0, 0]
        self.xn_free = [(cP, 0), (cP, 0)]
        self.stat_free = [(cV, 0), (cV, 0)]
        self.absmark = [(None, 0), (None, 0)]
        bfb_free = [(cV, 0), (cV, 0)]
        slab_free = [(cP, 0), (cP, 0)]
        slab_ready = [0, 0]
        kb_free = [(cV, 0), (cV, 0)]
        vb_free = [(cV, 0), (cV, 0)]
        ft_free = (cV, 0)
        ktmp_free = (cP, 0)
        ktok_free = (cV, 0)
        krot_free = (cP, 0)
        vtmp_free = (cP, 0)
        prev_mm = 0
        pending = None

        def emit_chain(job):
            nonlocal ft_free, ktmp_free, ktok_free, krot_free, vtmp_free
            is_k, g, Q, em = job
            if is_k:
                def _(v):
                    nonlocal ktmp_free
                    v.wait_ge(cP.sem, em["mm"])
                    c, v0 = ktmp_free
                    v.wait_ge(c.sem, v0)
                    last = v.tensor_copy(ktmp[:, :], kb[g % 2][:, :])
                    cV.inc(last)
                    em["ktmp"] = cV.n
                self.V(_)
                kb_free[g % 2] = (cV, cV.n)

                def _(t):
                    nonlocal ft_free
                    t.wait_ge(cV.sem, em["ktmp"])
                    c, v0 = ft_free
                    t.wait_ge(c.sem, v0)
                    for u in range(4):
                        last = t.transpose(ft[:, 128 * u:128 * (u + 1)],
                                           ktmp[:, 128 * u:128 * (u + 1)],
                                           self.idf[:, :])
                    cP.inc(last)
                    em["ftp"] = cP.n
                self.T(_)
                ktmp_free = (cP, cP.n)

                def _(v):
                    nonlocal ktok_free, krot_free
                    v.wait_ge(cP.sem, em["ftp"])
                    c, v0 = ktok_free
                    v.wait_ge(c.sem, v0)
                    last = v.tensor_copy(ktok[:, :], ft[:, :])
                    cV.inc(last)
                    em["ktok"] = cV.n
                    v.wait_ge(cC.sem, self.w_tabs)
                    c, v0 = krot_free
                    v.wait_ge(c.sem, v0)
                    ce = self.ctabf[:, 256 * Q:256 * Q + 256]
                    se = self.stabf[:, 256 * Q:256 * Q + 256]
                    ke = ktok[:, 0:512:2]
                    ko = ktok[:, 1:512:2]
                    v.tensor_tensor(rt1[:, :], ke, ce, ALU.mult)
                    v.tensor_tensor(rt2[:, :], ko, se, ALU.mult)
                    v.tensor_tensor(krot[:, 0:512:2], rt1[:, :], rt2[:, :],
                                    ALU.subtract)
                    v.tensor_tensor(rt1[:, :], ke, se, ALU.mult)
                    v.tensor_tensor(rt2[:, :], ko, ce, ALU.mult)
                    last = v.tensor_tensor(krot[:, 1:512:2], rt1[:, :], rt2[:, :],
                                           ALU.add)
                    cV.inc(last)
                    em["krot"] = cV.n
                self.V(_)
                ft_free = (cV, cV.n)
                ktok_free = (cV, cV.n)

                def _(t):
                    nonlocal krot_free
                    t.wait_ge(cV.sem, em["krot"])
                    c, v0 = bfb_free[0]
                    t.wait_ge(c.sem, v0)
                    for u in range(4):
                        last = t.transpose(bfb[0][:, 128 * u:128 * (u + 1)],
                                           krot[:, 128 * u:128 * (u + 1)],
                                           self.idb[:, :])
                    cP.inc(last)
                    em["kbt"] = cP.n
                self.T(_)
                krot_free = (cP, cP.n)

                def _(v):
                    v.wait_ge(cP.sem, em["kbt"])
                    last = v.tensor_copy(
                        self.kfeat[:, 2048 * g + 512 * Q:2048 * g + 512 * Q + 512],
                        bfb[0][:, 0:512])
                    cV.inc(last)
                    bfb_free[0] = (cV, cV.n)
                self.V(_)
            else:
                def _(v):
                    nonlocal vtmp_free
                    v.wait_ge(cP.sem, em["mm"])
                    c, v0 = vtmp_free
                    v.wait_ge(c.sem, v0)
                    last = v.tensor_copy(vtmp[:, :], vb[g % 2][:, :])
                    cV.inc(last)
                    em["vtmp"] = cV.n
                self.V(_)
                vb_free[g % 2] = (cV, cV.n)

                def _(t):
                    nonlocal vtmp_free
                    t.wait_ge(cV.sem, em["vtmp"])
                    c, v0 = bfb_free[1]
                    t.wait_ge(c.sem, v0)
                    for u in range(4):
                        last = t.transpose(bfb[1][:, 128 * u:128 * (u + 1)],
                                           vtmp[:, 128 * u:128 * (u + 1)],
                                           self.idb[:, :])
                    cP.inc(last)
                    em["vbt"] = cP.n
                self.T(_)
                vtmp_free = (cP, cP.n)

                def _(v):
                    v.wait_ge(cP.sem, em["vbt"])
                    dst = self.vtok[:, :].rearrange("p (t n) -> p t n", t=16)[
                        :, 4 * Q:4 * Q + 4, 128 * g:128 * g + 128]
                    src = bfb[1][:, 0:512].rearrange("p (u n) -> p u n", u=4)
                    last = v.tensor_copy(dst, src)
                    cV.inc(last)
                    bfb_free[1] = (cV, cV.n)
                self.V(_)

        for Q in range(self.p1_quarters):
            for j, tg in enumerate(range(4 * Q, 4 * Q + self.p1_tiles)):
                r = tg % 2
                tq = tg % 4
                lm = {}
                self._ln_tile(self.xf[128 * tg:128 * (tg + 1), :], r, xnb[r],
                              None if "abs" in self.p1_skip else tg, lm)
                if "tp" in self.p1_skip:
                    continue
                tmarks = {}
                for half in range(2):
                    def _(t, r=r, lm=lm, half=half, tmarks=tmarks):
                        if half == 0:
                            c, v0 = lm["xn"]
                            t.wait_ge(c.sem, v0)
                        for bb in range(2):
                            c, v0 = bfb_free[bb]
                            t.wait_ge(c.sem, v0)
                            for u in range(8):
                                ch = 16 * half + 8 * bb + u
                                last = t.transpose(bfb[bb][:, 128 * u:128 * (u + 1)],
                                                   xnb[r][:, 128 * ch:128 * (ch + 1)],
                                                   self.idb[:, :])
                            cP.inc(last)
                            tmarks[(half, bb)] = cP.n
                    self.T(_)
                    if "evac" in self.p1_skip:
                        continue

                    def _(sc, tq=tq, half=half, tmarks=tmarks, pm=prev_mm):
                        if half == 0:
                            sc.wait_ge(cP.sem, pm)
                        for bb in range(2):
                            sc.wait_ge(cP.sem, tmarks[(half, bb)])
                            c0 = 16 * half + 8 * bb
                            dst = xntq[:, :].rearrange("p (c n) -> p c n", c=NCH)[
                                :, c0:c0 + 8, 128 * tq:128 * tq + 128]
                            src = bfb[bb][:, :].rearrange("p (u n) -> p u n", u=8)
                            last = sc.copy(out=dst, in_=src)
                            cC.inc(last)
                            bfb_free[bb] = (cC, cC.n)
                    self.C(_)
                self.xn_free[r] = (cP, cP.n)
            evac_last = cC.n

            for gi in range(self.p1_groups):
                r = gi % 2
                g = gi % 8
                is_k = gi < 8

                def _(gp, r=r, g=g, is_k=is_k):
                    c, v0 = slab_free[r]
                    gp.wait_ge(c.sem, v0)
                    src = (self.wk if is_k else self.wv)[:, 128 * g:128 * (g + 1)]
                    last = gp.dma_start(
                        out=wslab[r][:, :].rearrange("p (c n) -> p c n", c=NCH),
                        in_=src.rearrange("(c p) n -> p c n", p=128))
                    self.sWS[r].inc(last)
                    slab_ready[r] = self.sWS[r].n
                self.G(_)

                em = {}

                def _(t, r=r, g=g, is_k=is_k, el=evac_last, em=em):
                    t.wait_ge(self.sWS[r].sem, slab_ready[r])
                    t.wait_ge(cC.sem, el)
                    bank = (kb if is_k else vb)[g % 2]
                    c, v0 = (kb_free if is_k else vb_free)[g % 2]
                    t.wait_ge(c.sem, v0)
                    for ch in range(NCH):
                        last = t.matmul(bank[:, :], wslab[r][:, 128 * ch:128 * (ch + 1)],
                                        xntq[:, 512 * ch:512 * (ch + 1)],
                                        start=(ch == 0), stop=(ch == NCH - 1),
                                        skip_group_check=True)
                    cP.inc(last)
                    em["mm"] = cP.n
                    slab_free[r] = (cP, cP.n)
                self.T(_)

                if pending is not None:
                    emit_chain(pending)
                pending = (is_k, g, Q, em)
            prev_mm = cP.n

        if pending is not None:
            emit_chain(pending)
            pending = None

        if self.debug >= 1:
            def _(sy):
                sy.wait_ge(cC.sem, cC.n)
                sy.wait_ge(cP.sem, cP.n)
                sy.wait_ge(cV.sem, cV.n)
                self.sDbg.inc(sy.dma_start(out=self.dbg["xntq"][:, :], in_=xntq[:, :]))
                self.sDbg.inc(sy.dma_start(out=self.dbg["kfeat"][:, :], in_=self.kfeat[:, :]))
                self.sDbg.inc(sy.dma_start(out=self.dbg["vtok"][:, :], in_=self.vtok[:, :]))
            self.S(_)
        self.fence()
        fr.close()

    def _p2(self):
        cP, cV, cC = self.cP, self.cV, self.cC
        fr2 = contextlib.ExitStack()
        xnt_own = self.sb("p2_xnt", [128, NCH * TOWN], BF16, fr2)
        fra = contextlib.ExitStack()
        self.xbuf = [self.sb(f"p2_x{r}", [128, D], F32, fra) for r in range(2)]
        xnf = [self.sb("p2_xnf0", [128, D], F32, fra)]
        xnf = [xnf[0], xnf[0]]
        gsc = [self.sb(f"p2_gsc{r}", [128, 512], F32, fra) for r in range(2)]
        g3 = [self.sb(f"p2_g3{r}", [128, 4], F32, fra) for r in range(2)]
        coef3 = [self.sb(f"p2_c3{r}", [128, 4], F32, fra) for r in range(2)]
        sm = self.sb("p2_sm", [128, 24], F32, fra)
        amask = self.sb("p2_amask", [128, 16], F32, fra)
        ft2 = [self.ps(f"p2ft{r}", [128, 512], F32, fra) for r in range(2)]
        gp = self.ps("p2gp", [128, 512], F32, fra)
        cp = self.ps("p2cp", [128, 512], F32, fra)

        self.xb_free = [(self.sX[0], self.sX[0].n), (self.sX[1], self.sX[1].n)]
        self.xb_ready = [0, 0]
        self.xn_free = [(cP, 0), (cP, 0)]
        self.stat_free = [(cV, 0), (cV, 0)]
        self.absmark = [(None, 0), (None, 0)]

        rm = {}

        def _(v):
            v.tensor_tensor(amask[:, :], self.absstage[:, :], self.selvt[:, 0:16],
                            ALU.mult)
            v.reduce_sum(self.stat[:, 0:1], amask[:, :], AX.X)
            v.tensor_tensor(amask[:, :], self.absstage[:, :], self.selvt[:, 16:32],
                            ALU.mult)
            last = v.reduce_sum(self.stat[:, 1:2], amask[:, :], AX.X)
            cV.inc(last)
            rm["cols"] = cV.n
        self.V(_)

        def _(t):
            t.wait_ge(cV.sem, rm["cols"])
            t.matmul(gp[0:1, 0:1], self.stat[:, 0:1], self.ones_colf[:, :],
                     start=True, stop=True, skip_group_check=True)
            last = t.matmul(gp[0:1, 1:2], self.stat[:, 1:2], self.ones_colf[:, :],
                            start=True, stop=True, skip_group_check=True)
            cP.inc(last)
            rm["sums"] = cP.n
        self.T(_)

        def _(v):
            v.wait_ge(cP.sem, rm["sums"])
            r = self.ratio
            v.tensor_scalar(r[0:1, 1:2], gp[0:1, 0:1], float(1.0 / (VISLEN * D)),
                            1e-8, ALU.mult, ALU.add)
            v.reciprocal(r[0:1, 2:3], r[0:1, 1:2])
            v.tensor_scalar(r[0:1, 3:4], gp[0:1, 1:2], float(1.0 / ((S - VISLEN) * D)),
                            None, ALU.mult)
            v.tensor_tensor(r[0:1, 0:1], r[0:1, 3:4], r[0:1, 2:3], ALU.mult)
            v.tensor_scalar(self.gcorr[0:1, 0:3], self.gwl[0:1, 0:3], r[0:1, 0:1],
                            None, ALU.mult)
            last = v.tensor_tensor(self.gcorr[0:1, 0:3], self.gcorr[0:1, 0:3],
                                   self.gb[0:1, 0:3], ALU.add)
            cV.inc(last)
            rm["gcorr"] = cV.n
        self.V(_)
        gcorr_ready = rm["gcorr"]
        gp_free = (cV, rm["gcorr"])
        self.stat_free = [(cV, rm["gcorr"]), (cV, rm["gcorr"])]

        ft_free = [(cV, 0), (cV, 0)]
        gsc_free = [(cP, 0), (cP, 0)]
        xnt_marks = []
        for it in range(9):
            isG = (it == 0)
            i = it - 1
            r = it % 2
            src = self.xf[0:128, :] if isG else self.xo[128 * i:128 * (i + 1), :]
            lm = {}
            self._ln_tile(src, r, xnf[r], None, lm)
            tm = {}
            for grp in range(8):
                bb = grp % 2

                def _(t, r=r, lm=lm, tm=tm, grp=grp, bb=bb):
                    if grp == 0:
                        c, v0 = lm["xn"]
                        t.wait_ge(c.sem, v0)
                        t.wait_ge(cV.sem, gcorr_ready)
                    c, v0 = ft_free[bb]
                    t.wait_ge(c.sem, v0)
                    for u in range(4):
                        ch = 4 * grp + u
                        last = t.transpose(ft2[bb][:, 128 * u:128 * (u + 1)],
                                           xnf[r][:, 128 * ch:128 * (ch + 1)],
                                           self.idf[:, :])
                    cP.inc(last)
                    tm[("tp", grp)] = cP.n
                self.T(_)

                def _(v, tm=tm, grp=grp, bb=bb):
                    v.wait_ge(cP.sem, tm[("tp", grp)])
                    c, v0 = gsc_free[bb]
                    v.wait_ge(c.sem, v0)
                    last = v.tensor_copy(gsc[bb][:, :], ft2[bb][:, :])
                    cV.inc(last)
                    tm[("gsc", grp)] = cV.n
                self.V(_)

                if not isG:
                    def _(sc, i=i, tm=tm, grp=grp, bb=bb):
                        sc.wait_ge(cP.sem, tm[("tp", grp)])
                        dst = xnt_own[:, :].rearrange("p (c n) -> p c n", c=NCH)[
                            :, 4 * grp:4 * grp + 4, 128 * i:128 * i + 128]
                        src2 = ft2[bb][:, 0:512].rearrange("p (u n) -> p u n", u=4)
                        last = sc.copy(out=dst, in_=src2)
                        cC.inc(last)
                        tm[("ev", grp)] = cC.n
                    self.C(_)
                    ft_free[bb] = (cC, tm[("ev", grp)])
                else:
                    ft_free[bb] = (cV, tm[("gsc", grp)])

                def _(t, tm=tm, grp=grp, bb=bb, gpf=gp_free, isG=isG):
                    if grp == 0:
                        c, v0 = gpf
                        t.wait_ge(c.sem, v0)
                    t.wait_ge(cV.sem, tm[("gsc", grp)])
                    for u in range(4):
                        ch = 4 * grp + u
                        last = t.matmul(gp[:, 0:3], gsc[bb][:, 128 * u:128 * (u + 1)],
                                        self.gw[:, 3 * ch:3 * ch + 3],
                                        start=(ch == 0), stop=False,
                                        skip_group_check=True)
                    if grp == 7:
                        last = t.matmul(gp[:, 0:3], self.ones_row[:, :],
                                        self.gcorr[0:1, 0:3],
                                        start=False, stop=True, skip_group_check=True)
                        cP.inc(last)
                        tm["gdone"] = cP.n
                self.T(_)
                gsc_free[bb] = (cP, cP.n)
            self.xn_free[r] = (cP, cP.n)
            if not isG:
                xnt_marks.append(tm[("ev", 7)])

            smm = {}

            def _(v, r=r, tm=tm):
                v.wait_ge(cP.sem, tm["gdone"])
                last = v.reduce_max(sm[:, 12 * r:12 * r + 1], gp[:, 0:3], AX.X,
                                    negate=True)
                cV.inc(last)
                smm["negmax"] = cV.n
            self.V(_)

            def _(sc, r=r):
                sc.wait_ge(cV.sem, smm["negmax"])
                last = sc.activation(g3[r][:, 0:3], gp[:, 0:3], AF.Exp,
                                     bias=sm[:, 12 * r:12 * r + 1])
                cC.inc(last)
                smm["exp"] = cC.n
            self.C(_)
            gp_free = (cC, cC.n)

            def _(v, r=r, isG=isG, i=i):
                b = 12 * r
                v.wait_ge(cC.sem, smm["exp"])
                v.reduce_sum(sm[:, b + 1:b + 2], g3[r][:, 0:3], AX.X)
                v.reciprocal(sm[:, b + 2:b + 3], sm[:, b + 1:b + 2])
                v.tensor_scalar(g3[r][:, 0:3], g3[r][:, 0:3], sm[:, b + 2:b + 3],
                                None, ALU.mult)
                v.reduce_max(sm[:, b + 3:b + 4], g3[r][:, 0:3], AX.X)
                v.tensor_reduce(sm[:, b + 4:b + 5], g3[r][:, 0:3], AX.X, ALU.min)
                v.reduce_sum(sm[:, b + 5:b + 6], g3[r][:, 0:3], AX.X)
                v.tensor_tensor(sm[:, b + 6:b + 7], sm[:, b + 5:b + 6],
                                sm[:, b + 3:b + 4], ALU.subtract)
                v.tensor_tensor(sm[:, b + 6:b + 7], sm[:, b + 6:b + 7],
                                sm[:, b + 4:b + 5], ALU.subtract)
                v.tensor_tensor(sm[:, b + 7:b + 8], sm[:, b + 3:b + 4],
                                sm[:, b + 6:b + 7], ALU.add)
                v.tensor_scalar(sm[:, b + 7:b + 8], sm[:, b + 7:b + 8], 1e-8,
                                None, ALU.add)
                v.reciprocal(sm[:, b + 8:b + 9], sm[:, b + 7:b + 8])
                v.tensor_tensor(sm[:, b + 9:b + 10], sm[:, b + 3:b + 4],
                                sm[:, b + 8:b + 9], ALU.mult)
                last = v.tensor_tensor(sm[:, b + 10:b + 11], sm[:, b + 6:b + 7],
                                       sm[:, b + 8:b + 9], ALU.mult)
                cV.inc(last)
                if isG:
                    p3 = g3[r][0:1, 0:3]
                    oh = self.ohrow
                    v.reduce_max(oh[0:1, 6:7], p3, AX.X)
                    v.tensor_scalar(oh[0:1, 0:3], p3, oh[0:1, 6:7], None, ALU.is_ge)
                    v.tensor_scalar(oh[0:1, 7:8], oh[0:1, 0:1], -1.0, 1.0,
                                    ALU.mult, ALU.add)
                    v.tensor_tensor(oh[0:1, 1:2], oh[0:1, 1:2], oh[0:1, 7:8], ALU.mult)
                    v.tensor_scalar(oh[0:1, 6:7], oh[0:1, 1:2], -1.0, 1.0,
                                    ALU.mult, ALU.add)
                    v.tensor_tensor(oh[0:1, 7:8], oh[0:1, 7:8], oh[0:1, 6:7], ALU.mult)
                    v.tensor_tensor(oh[0:1, 2:3], oh[0:1, 2:3], oh[0:1, 7:8], ALU.mult)
                    v.scalar_tensor_tensor(self.ratio[0:1, 4:7], oh[0:1, 0:3], -1e30,
                                           p3, ALU.mult, ALU.add)
                    v.reduce_max(oh[0:1, 6:7], self.ratio[0:1, 4:7], AX.X)
                    v.tensor_scalar(oh[0:1, 3:6], self.ratio[0:1, 4:7], oh[0:1, 6:7],
                                    None, ALU.is_ge)
                    v.tensor_scalar(oh[0:1, 7:8], oh[0:1, 3:4], -1.0, 1.0,
                                    ALU.mult, ALU.add)
                    v.tensor_tensor(oh[0:1, 4:5], oh[0:1, 4:5], oh[0:1, 7:8], ALU.mult)
                    v.tensor_scalar(oh[0:1, 6:7], oh[0:1, 4:5], -1.0, 1.0,
                                    ALU.mult, ALU.add)
                    v.tensor_tensor(oh[0:1, 7:8], oh[0:1, 7:8], oh[0:1, 6:7], ALU.mult)
                    last = v.tensor_tensor(oh[0:1, 5:6], oh[0:1, 5:6], oh[0:1, 7:8],
                                           ALU.mult)
                    cV.inc(last)
                    smm["oh"] = cV.n
                else:
                    for e in range(3):
                        v.tensor_tensor(sm[:, b + 11:b + 12], self.ohb[:, 3 + e:4 + e],
                                        sm[:, b + 10:b + 11], ALU.mult)
                        v.scalar_tensor_tensor(coef3[r][:, e:e + 1],
                                               self.ohb[:, e:e + 1],
                                               sm[:, b + 9:b + 10],
                                               sm[:, b + 11:b + 12],
                                               ALU.mult, ALU.add)
                    last = v.tensor_copy(sm[:, b + 11:b + 12], coef3[r][:, 0:1])
                    cV.inc(last)
                    smm["coef"] = cV.n
            self.V(_)

            if isG:
                def _(t):
                    t.wait_ge(cV.sem, smm["oh"])
                    last = t.matmul(cp[:, 0:6], self.ones_row[:, :],
                                    self.ohrow[0:1, 0:6],
                                    start=True, stop=True, skip_group_check=True)
                    cP.inc(last)
                    smm["ohmm"] = cP.n
                self.T(_)

                def _(v):
                    v.wait_ge(cP.sem, smm["ohmm"])
                    last = v.tensor_copy(self.ohb[:, 0:6], cp[:, 0:6])
                    cV.inc(last)
                self.V(_)
                cp_free = (cV, cV.n)
            else:
                def _(t, i=i):
                    t.wait_ge(cV.sem, smm["coef"])
                    c, v0 = cp_free
                    t.wait_ge(c.sem, v0)
                    last = t.transpose(cp[0:3, 0:128], coef3[r][:, 0:3], self.idf[:, :])
                    cP.inc(last)
                    smm["ctp"] = cP.n
                self.T(_)

                def _(v, i=i):
                    v.wait_ge(cP.sem, smm["ctp"])
                    last = v.tensor_copy(self.coefT[:, 128 * i:128 * (i + 1)],
                                         cp[0:3, 0:128])
                    cV.inc(last)
                self.V(_)
                cp_free = (cV, cV.n)

        xnt_ready = xnt_marks[-1]
        self.fence()
        fra.close()

        frb = contextlib.ExitStack()
        wqslab = [self.sb(f"p2_wq{r}", [128, NCH * 128], BF16, frb) for r in range(2)]
        qtmp = self.sb("p2_qtmp", [128, 1024], F32, frb)
        qtok = self.sb("p2_qtok", [128, 1024], F32, frb)
        qrot = self.sb("p2_qrot", [128, 1024], BF16, frb)
        qstage = [self.sb(f"p2_qs{r}", [128, 1024], BF16, frb) for r in range(2)]
        qa = [self.ps(f"p2qa{r}", [128, 512], F32, frb) for r in range(4)]
        ftq = [self.ps(f"p2ftq{r}", [128, 512], F32, frb) for r in range(2)]
        bfq = self.ps("p2bfq", [128, 1024], BF16, frb)

        slab_free = [(cP, 0), (cP, 0)]
        slab_ready = [0, 0]
        qa_free = [(cV, 0), (cV, 0)]
        qtmp_free = (cP, 0)
        ftq_free = [(cV, 0), (cV, 0)]
        qtok_free = (cV, 0)
        qrot_free = (cP, 0)
        bfq_free = (cV, 0)
        qs_free = [(self.sQS[0], 0), (self.sQS[1], 0)]
        pending = None

        def emit_qchain(job):
            nonlocal qtmp_free, qtok_free, qrot_free, bfq_free
            h, em = job
            p = h % 2

            def _(v):
                nonlocal qtmp_free
                v.wait_ge(cP.sem, em["mm"])
                c, v0 = qtmp_free
                v.wait_ge(c.sem, v0)
                v.tensor_copy(qtmp[:, 0:512], qa[2 * p][:, :])
                last = v.tensor_copy(qtmp[:, 512:1024], qa[2 * p + 1][:, :])
                cV.inc(last)
                em["qtmp"] = cV.n
            self.V(_)
            qa_free[p] = (cV, cV.n)

            def _(t):
                t.wait_ge(cV.sem, em["qtmp"])
                for half in range(2):
                    c, v0 = ftq_free[half]
                    t.wait_ge(c.sem, v0)
                    for u in range(4):
                        last = t.transpose(ftq[half][:, 128 * u:128 * (u + 1)],
                                           qtmp[:, 512 * half + 128 * u:
                                                512 * half + 128 * (u + 1)],
                                           self.idf[:, :])
                    cP.inc(last)
                    em[("ftp", half)] = cP.n
            self.T(_)
            qtmp_free0 = (cP, cP.n)

            def _(v):
                nonlocal qtok_free, qtmp_free, qrot_free
                c, v0 = qtok_free
                v.wait_ge(c.sem, v0)
                for half in range(2):
                    v.wait_ge(cP.sem, em[("ftp", half)])
                    last = v.tensor_copy(qtok[:, 512 * half:512 * (half + 1)],
                                         ftq[half][:, :])
                    cV.inc(last)
                    ftq_free[half] = (cV, cV.n)
                v.wait_ge(cC.sem, self.w_tabs)
                c, v0 = qrot_free
                v.wait_ge(c.sem, v0)
                qe = qtok[:, 0:1024:2]
                qo = qtok[:, 1:1024:2]
                v.tensor_tensor(qtmp[:, 0:512], qe, self.ctabo[:, :], ALU.mult)
                v.tensor_tensor(qtmp[:, 512:1024], qo, self.stabo[:, :], ALU.mult)
                v.tensor_tensor(qrot[:, 0:1024:2], qtmp[:, 0:512], qtmp[:, 512:1024],
                                ALU.subtract)
                v.tensor_tensor(qtmp[:, 0:512], qe, self.stabo[:, :], ALU.mult)
                v.tensor_tensor(qtmp[:, 512:1024], qo, self.ctabo[:, :], ALU.mult)
                last = v.tensor_tensor(qrot[:, 1:1024:2], qtmp[:, 0:512],
                                       qtmp[:, 512:1024], ALU.add)
                cV.inc(last)
                em["rope"] = cV.n
                qtok_free = (cV, cV.n)
                qtmp_free = (cV, cV.n)
            self.V(_)

            def _(t):
                t.wait_ge(cV.sem, em["rope"])
                c, v0 = bfq_free
                t.wait_ge(c.sem, v0)
                for u in range(8):
                    last = t.transpose(bfq[:, 128 * u:128 * (u + 1)],
                                       qrot[:, 128 * u:128 * (u + 1)], self.idb[:, :])
                cP.inc(last)
                em["bft"] = cP.n
            self.T(_)
            qrot_free = (cP, cP.n)

            def _(v):
                nonlocal bfq_free
                v.wait_ge(cP.sem, em["bft"])
                c, v0 = qs_free[p]
                v.wait_ge(c.sem, v0)
                last = v.tensor_copy(qstage[p][:, :], bfq[:, :])
                cV.inc(last)
                em["qs"] = cV.n
                bfq_free = (cV, cV.n)
            self.V(_)

            def _(sy):
                sy.wait_ge(cV.sem, em["qs"])
                last = sy.dma_start(out=self.q_hbm[:, 1024 * h:1024 * (h + 1)],
                                    in_=qstage[p][:, :])
                self.sQS[p].inc(last)
                qs_free[p] = (self.sQS[p], self.sQS[p].n)
            self.S(_)

        for h in range(NH):
            r = h % 2
            p = h % 2

            def _(gp2, h=h, r=r):
                c, v0 = slab_free[r]
                gp2.wait_ge(c.sem, v0)
                last = gp2.dma_start(
                    out=wqslab[r][:, :].rearrange("p (c n) -> p c n", c=NCH),
                    in_=self.wq[:, 128 * h:128 * (h + 1)].rearrange(
                        "(c p) n -> p c n", p=128))
                self.sWS[r].inc(last)
                slab_ready[r] = self.sWS[r].n
            self.G(_)

            hm = {}

            def _(t, h=h, r=r, p=p, hm=hm, xr=xnt_ready):
                t.wait_ge(self.sWS[r].sem, slab_ready[r])
                t.wait_ge(cC.sem, xr)
                c, v0 = qa_free[p]
                t.wait_ge(c.sem, v0)
                for ch in range(NCH):
                    t.matmul(qa[2 * p][:, :], wqslab[r][:, 128 * ch:128 * (ch + 1)],
                             xnt_own[:, 1024 * ch:1024 * ch + 512],
                             start=(ch == 0), stop=(ch == NCH - 1),
                             skip_group_check=True)
                    last = t.matmul(qa[2 * p + 1][:, :],
                                    wqslab[r][:, 128 * ch:128 * (ch + 1)],
                                    xnt_own[:, 1024 * ch + 512:1024 * ch + 1024],
                                    start=(ch == 0), stop=(ch == NCH - 1),
                                    skip_group_check=True)
                cP.inc(last)
                hm["mm"] = cP.n
                slab_free[r] = (cP, cP.n)
            self.T(_)

            if pending is not None:
                emit_qchain(pending)
            pending = (h, hm)
        emit_qchain(pending)
        pending = None

        def _(gp2):
            gp2.wait_ge(cP.sem, cP.n)
            last = gp2.dma_start(out=self.xnt_hbm[:, :], in_=xnt_own[:, :])
            self.sXnt.inc(last)
        self.G(_)
        if self.debug >= 2:
            def _(sy):
                sy.wait_ge(self.sXnt.sem, self.sXnt.n)
                for c in (self.sQS[0], self.sQS[1]):
                    sy.wait_ge(c.sem, c.n)
                self.sDbg.inc(sy.dma_start(out=self.dbg["xnt"][:, :], in_=xnt_own[:, :]))
                self.sDbg.inc(sy.dma_start(out=self.dbg["qh"][:, :], in_=self.q_hbm[:, :]))
            self.S(_)
        if self.debug >= 1:
            def _(sy):
                sy.wait_ge(cV.sem, cV.n)
                self.sDbg.inc(sy.dma_start(out=self.dbg["misc"][:, 0:8], in_=self.ohb[:, :]))
                self.sDbg.inc(sy.dma_start(out=self.dbg["misc"][0:1, 8:16],
                                           in_=self.ratio[0:1, :]))
                self.sDbg.inc(sy.dma_start(out=self.dbg["misc"][0:3, 16:32],
                                           in_=self.coefT[:, 0:16]))
            self.S(_)
        if self.debug >= 2:
            def _(sy):
                sy.wait_ge(cV.sem, cV.n)
                for i in range(8):
                    self.sDbg.inc(sy.dma_start(out=self.dbg["coef"][0:3, 4 * i:4 * i + 4],
                                               in_=self.coefT[:, 128 * i:128 * i + 4]))
            self.S(_)
        self.fence()
        frb.close()
        fr2.close()

    def _p3(self):
        cP, cV, cC = self.cP, self.cV, self.cC
        fr = contextlib.ExitStack()
        masks = self.sb("p3_masks", [128, 16 * 512], BF16, fr)
        qbuf = [self.sb(f"p3_q{r}", [128, 1024], BF16, fr) for r in range(3)]
        probs = self.sb("p3_probs", [128, 24 * 512], BF16, fr)
        denrow = self.sb("p3_den", [1, 1024], F32, fr)
        rbb = [self.sb(f"p3_rb{r}", [128, 512], F32, fr) for r in range(2)]
        cstage = [self.sb(f"p3_cs{r}", [128, 512], BF16, fr) for r in range(2)]
        sbk = self.ps("p3s", [128, 2048], F32, fr)
        cxb = [self.ps(f"p3c{r}", [128, 512], F32, fr) for r in range(2)]
        dnb = [self.ps(f"p3d{r}", [128, 512], F32, fr) for r in range(2)]

        def _(sy):
            last = sy.dma_start(out=masks[:, :].rearrange("p (j n) -> p j n", j=16),
                                in_=self.masksin[:, :, :].rearrange("j p n -> p j n"))
            self.sMask.inc(last)
        self.S(_)
        masks_ready = self.sMask.n

        qb_free = [(cP, 0)] * 3
        qb_ready = [0, 0, 0]
        s_free = [(cC, 0)] * 4
        probs_free = [(cP, 0)] * 24
        cx_free = [(cV, 0), (cV, 0)]
        dn_free = [(cV, 0), (cV, 0)]
        rb_free = (cV, 0)
        rbb_free = [(cV, 0), (cV, 0)]
        cs_free = [(self.sCS[0], 0), (self.sCS[1], 0)]

        pending = None

        def emit_B(job):
            h, qb, kmax, rr, g, pready, em = job
            slot0 = 16 if qb == 0 else 0

            def _(t):
                for kt in range(kmax):
                    c, v0 = pready[kt]
                    t.wait_ge(c.sem, v0)
                    if kt == 0:
                        c, v0 = cx_free[rr]
                        t.wait_ge(c.sem, v0)
                        c, v0 = dn_free[rr]
                        t.wait_ge(c.sem, v0)
                    sl = slot0 + kt
                    t.matmul(cxb[rr][:, :],
                             self.vtok[:, 1024 * kt + 128 * g:
                                       1024 * kt + 128 * (g + 1)],
                             probs[:, 512 * sl:512 * (sl + 1)],
                             start=(kt == 0), stop=(kt == kmax - 1),
                             skip_group_check=True)
                    last = t.matmul(dnb[rr][0:1, :], self.ones_colb[:, :],
                                    probs[:, 512 * sl:512 * (sl + 1)],
                                    start=(kt == 0), stop=(kt == kmax - 1),
                                    skip_group_check=True)
                    cP.inc(last)
                    probs_free[sl] = (cP, cP.n)
                em["cd"] = cP.n
            self.T(_)

            def _(v):
                v.wait_ge(cP.sem, em["cd"])
                v.tensor_copy(denrow[0:1, 512 * rr:512 * (rr + 1)],
                              dnb[rr][0:1, :])
                last = v.reciprocal(denrow[0:1, 512 * rr:512 * (rr + 1)],
                                    denrow[0:1, 512 * rr:512 * (rr + 1)])
                cV.inc(last)
                em["rec"] = cV.n
            self.V(_)

            def _(t):
                t.wait_ge(cV.sem, em["rec"])
                last = t.matmul(dnb[rr][:, :], self.ones_row[:, :],
                                denrow[0:1, 512 * rr:512 * (rr + 1)],
                                start=True, stop=True, skip_group_check=True)
                cP.inc(last)
                em["rbmm"] = cP.n
            self.T(_)

            def _(v):
                v.wait_ge(cP.sem, em["rbmm"])
                c, v0 = rbb_free[rr]
                v.wait_ge(c.sem, v0)
                v.tensor_copy(rbb[rr][:, :], dnb[rr][:, :])
                c, v0 = cs_free[rr]
                v.wait_ge(c.sem, v0)
                last = v.tensor_tensor(cstage[rr][:, :], cxb[rr][:, :],
                                       rbb[rr][:, :], ALU.mult)
                cV.inc(last)
                em["cs"] = cV.n
            self.V(_)
            dn_free[rr] = (cV, cV.n)
            cx_free[rr] = (cV, cV.n)
            rbb_free[rr] = (cV, cV.n)

            def _(sy):
                sy.wait_ge(cV.sem, em["cs"])
                dst = self.ctx_hbm[:, :].rearrange("p (t n) -> p t n", t=8)[
                    :, 4 * qb:4 * qb + 4, 128 * h:128 * (h + 1)]
                src2 = cstage[rr][:, :].rearrange("p (u n) -> p u n", u=4)
                last = sy.dma_start(out=dst, in_=src2)
                self.sCS[rr].inc(last)
                cs_free[rr] = (self.sCS[rr], self.sCS[rr].n)
            self.S(_)

        for h in range(NH):
            qr = h % 3
            g = h // 4

            def _(gq, h=h, qr=qr):
                c, v0 = qb_free[qr]
                gq.wait_ge(c.sem, v0)
                gq.wait_ge(self.sXnt.sem, self.sXnt.n)
                last = gq.dma_start(out=qbuf[qr][:, :],
                                    in_=self.q_hbm[:, 1024 * h:1024 * (h + 1)])
                self.sQB[qr].inc(last)
                qb_ready[qr] = self.sQB[qr].n
            self.G(_)

            for qb in range(2):
                kmax = 8 if qb == 0 else 16
                rr = (2 * h + qb) % 2
                slot0 = 16 if qb == 0 else 0
                expM = [cC.n + (j // 4) + 1 for j in range(kmax)]
                sA = {}

                def _(t, h=h, qb=qb, kmax=kmax, g=g, qr=qr, expM=expM, sA=sA):
                    t.wait_ge(self.sQB[qr].sem, qb_ready[qr])
                    for kt in range(kmax):
                        sb_ = kt % 4
                        if kt >= 4:
                            t.wait_ge(cC.sem, expM[kt - 4])
                        else:
                            c, v0 = s_free[sb_]
                            t.wait_ge(c.sem, v0)
                        last = t.matmul(
                            sbk[:, 512 * sb_:512 * (sb_ + 1)],
                            self.kfeat[:, 2048 * g + 128 * kt:2048 * g + 128 * (kt + 1)],
                            qbuf[qr][:, 512 * qb:512 * (qb + 1)],
                            start=True, stop=True, skip_group_check=True)
                        cP.inc(last)
                        sA[kt] = cP.n
                self.T(_)

                def _(sc, qb=qb, kmax=kmax, sA=sA, slot0=slot0):
                    for j in range(kmax // 4):
                        k0 = 4 * j
                        sc.wait_ge(cP.sem, sA[k0 + 3])
                        for kk in range(k0, k0 + 4):
                            c, v0 = probs_free[slot0 + kk]
                            sc.wait_ge(c.sem, v0)
                        sl = slot0 + k0
                        sc.activation(probs[:, 512 * sl:512 * (sl + 4)],
                                      sbk[:, :],
                                      AF.Exp, bias=self.negb_col[:, 0:1],
                                      scale=SCALE)
                self.C(_)
                assert cC.n == expM[-1], (cC.n, expM[-1])
                for kt in range(kmax):
                    s_free[kt % 4] = (cC, expM[kt])
                pready = [(cC, expM[kt]) for kt in range(kmax)]

                def _(v, qb=qb, kmax=kmax, mr=masks_ready, expM=expM,
                      pready=pready, slot0=slot0):
                    first = True
                    for kt in range(kmax):
                        if (qb == 0) or (kt >= 8):
                            v.wait_ge(cC.sem, expM[kt])
                            if first:
                                v.wait_ge(self.sMask.sem, mr)
                                first = False
                            sl = slot0 + kt
                            v.tensor_tensor(probs[:, 512 * sl:512 * (sl + 1)],
                                            probs[:, 512 * sl:512 * (sl + 1)],
                                            masks[:, 512 * kt:512 * (kt + 1)],
                                            ALU.mult)
                            pready[kt] = (cV, cV.n)
                self.V(_)

                if pending is not None:
                    emit_B(pending)
                pending = (h, qb, kmax, rr, g, pready, {})
            qb_free[qr] = (cP, cP.n)
        emit_B(pending)

        if self.debug >= 3:
            def _(sy):
                for c in self.sCS:
                    sy.wait_ge(c.sem, c.n)
                self.sDbg.inc(sy.dma_start(out=self.dbg["ctx"][:, :],
                                           in_=self.ctx_hbm[:, :]))
            self.S(_)
        self.fence()
        fr.close()

    def _p4(self):
        cP, cV, cC = self.cP, self.cV, self.cC
        fr = contextlib.ExitStack()
        ctxbuf = self.sb("p4_ctx", [128, 8 * D], BF16, fr)
        woslab = [self.sb(f"p4_wo{r}", [128, NCH * 512], BF16, fr) for r in range(2)]
        self.b2 = self.sb("p4_b2", [3, D], F32, fr)
        xcol = [self.sb(f"p4_x{r}", [128, 512], F32, fr) for r in range(3)]
        outsb = [self.sb(f"p4_o{r}", [128, 512], F32, fr) for r in range(2)]
        ob = [self.ps(f"p4b{r}", [128, 512], F32, fr) for r in range(4)]

        def _(sy):
            self.sMask.inc(sy.dma_start(out=self.b2[:, :], in_=self.b2in[:, :]))
            for c in self.sCS:
                sy.wait_ge(c.sem, c.n)
            self.sMask.inc(sy.dma_start(out=ctxbuf[:, :], in_=self.ctx_hbm[:, :]))
        self.S(_)
        b2_ready = self.sMask.n

        slab_free = [(cP, 0), (cP, 0)]
        slab_ready = [0, 0]
        x_free = [(cV, 0)] * 3
        x_ready = [0] * 3
        o_free = [(cV, 0)] * 4
        os_free = [(self.sOS[0], 0), (self.sOS[1], 0)]

        jobs = [(db, tau) for db in range(8) for tau in range(8)]
        for j, (db, tau) in enumerate(jobs):
            r = db % 2
            cr = j % 3
            bk = j % 4
            if tau == 0:
                def _(gp2, db=db, r=r):
                    c, v0 = slab_free[r]
                    gp2.wait_ge(c.sem, v0)
                    last = gp2.dma_start(
                        out=woslab[r][:, :].rearrange("p (c n) -> p c n", c=NCH),
                        in_=self.wo[:, 512 * db:512 * (db + 1)].rearrange(
                            "(c p) n -> p c n", p=128))
                    self.sWS[r].inc(last)
                    slab_ready[r] = self.sWS[r].n
                self.G(_)

            def _(sy, db=db, tau=tau, cr=cr):
                c, v0 = x_free[cr]
                sy.wait_ge(c.sem, v0)
                last = sy.dma_start(out=xcol[cr][:, :],
                                    in_=self.xo[128 * tau:128 * (tau + 1),
                                                512 * db:512 * (db + 1)])
                self.sX[cr].inc(last)
                x_ready[cr] = self.sX[cr].n
            self.S(_)

            em = {}

            def _(t, db=db, tau=tau, cr=cr, r=r, bk=bk, em=em):
                t.wait_ge(self.sMask.sem, b2_ready)
                t.wait_ge(self.sWS[r].sem, slab_ready[r])
                c, v0 = o_free[bk]
                t.wait_ge(c.sem, v0)
                for hh in range(NCH):
                    t.matmul(ob[bk][:, :],
                             ctxbuf[:, D * tau + 128 * hh:D * tau + 128 * (hh + 1)],
                             woslab[r][:, 512 * hh:512 * (hh + 1)],
                             start=(hh == 0), stop=False, skip_group_check=True)
                last = t.matmul(ob[bk][:, :], self.coefT[:, 128 * tau:128 * (tau + 1)],
                                self.b2[:, 512 * db:512 * (db + 1)],
                                start=False, stop=True, skip_group_check=True)
                cP.inc(last)
                em["mm"] = cP.n
                slab_free[r] = (cP, cP.n)
            self.T(_)

            def _(v, cr=cr, bk=bk, em=em):
                v.wait_ge(cP.sem, em["mm"])
                v.wait_ge(self.sX[cr].sem, x_ready[cr])
                c, v0 = os_free[j % 2]
                v.wait_ge(c.sem, v0)
                last = v.tensor_tensor(outsb[j % 2][:, :], ob[bk][:, :],
                                       xcol[cr][:, :], ALU.add)
                cV.inc(last)
                em["add"] = cV.n
            self.V(_)
            o_free[bk] = (cV, cV.n)
            x_free[cr] = (cV, cV.n)

            def _(sc, db=db, tau=tau, em=em):
                sc.wait_ge(cV.sem, em["add"])
                last = sc.dma_start(out=self.scr[128 * tau:128 * (tau + 1),
                                                 512 * db:512 * (db + 1)],
                                    in_=outsb[j % 2][:, :])
                self.sOS[j % 2].inc(last)
                os_free[j % 2] = (self.sOS[j % 2], self.sOS[j % 2].n)
            self.C(_)
        if self.debug >= 4:
            def _(sy):
                for c in self.sOS:
                    sy.wait_ge(c.sem, c.n)
                self.sDbg.inc(sy.dma_start(out=self.dbg["scr"][:, :],
                                           in_=self.scr[:, :]))
            self.S(_)
        self.fence()
        fr.close()

    def _p5(self):
        cP, cV, cC = self.cP, self.cV, self.cC
        fr = contextlib.ExitStack()
        act_sb = self.sb("p5_act", [128, NCH * TOWN], BF16, fr)
        coefb = self.sb("p5_coefb", [128, 3 * TOWN], BF16, fr)
        tnf = [self.sb(f"p5_tn{r}", [128, 512], F32, fr) for r in range(4)]
        tn_free = [(cV, 0)] * 4

        cbp = contextlib.ExitStack()
        cb = [self.ps(f"p5cb{r}", [128, 512], F32, cbp) for r in range(2)]
        em0 = {}
        cb_free = (cC, 0)
        for e in range(3):
            def _(t, e=e, cbf=cb_free):
                c, v0 = cbf
                t.wait_ge(c.sem, v0)
                for tb in range(2):
                    last = t.matmul(cb[tb][:, :], self.sel3[:, 128 * e:128 * (e + 1)],
                                    self.coefT[0:3, 512 * tb:512 * (tb + 1)],
                                    start=True, stop=True, skip_group_check=True)
                cP.inc(last)
                em0[e] = cP.n
            self.T(_)

            def _(sc, e=e):
                sc.wait_ge(cP.sem, em0[e])
                for tb in range(2):
                    last = sc.activation(
                        coefb[:, 1024 * e + 512 * tb:1024 * e + 512 * (tb + 1)],
                        cb[tb][:, :], AF.Identity)
                cC.inc(last)
            self.C(_)
            cb_free = (cC, cC.n)
        coefb_ready = self.cC.n
        self.fence()
        cbp.close()

        act_free = (cP, 0)
        for e in range(3):
            frp = contextlib.ExitStack()
            xnt5 = self.sb(f"p5_xnt{e}", [128, NCH * TOWN], BF16, frp)
            w1slab = [self.sb(f"p5_w1{e}_{r}", [128, NCH * 256], BF16, frp)
                      for r in range(2)]
            pp = [self.ps(f"p5p{e}_{r}", [128, 512], F32, frp) for r in range(8)]

            def _(gp2):
                gp2.wait_ge(self.sXnt.sem, self.sXnt.n)
                last = gp2.dma_start(out=xnt5[:, :], in_=self.xnt_hbm[:, :])
                self.sXnt.inc(last)
            self.G(_)
            xnt5_ready = self.sXnt.n

            slab_free = [(cP, 0), (cP, 0)]
            slab_ready = [0, 0]
            pp_free = [(cV, 0)] * 8
            for mg in range(16):
                r = mg % 2

                def _(gp2, e=e, mg=mg, r=r):
                    c, v0 = slab_free[r]
                    gp2.wait_ge(c.sem, v0)
                    last = gp2.dma_start(
                        out=w1slab[r][:, :].rearrange("p (c n) -> p c n", c=NCH),
                        in_=self.w1[e][:, 256 * mg:256 * (mg + 1)].rearrange(
                            "(c p) n -> p c n", p=128))
                    self.sWS[r].inc(last)
                    slab_ready[r] = self.sWS[r].n
                self.G(_)

                em = {}

                def _(t, mg=mg, r=r, em=em, xr=xnt5_ready):
                    t.wait_ge(self.sWS[r].sem, slab_ready[r])
                    t.wait_ge(self.sXnt.sem, xr)
                    base = 4 * (mg % 2)
                    for q in range(4):
                        c, v0 = pp_free[base + q]
                        t.wait_ge(c.sem, v0)
                    for ch in range(NCH):
                        for jj in range(2):
                            for tb in range(2):
                                last = t.matmul(
                                    pp[base + 2 * jj + tb][:, :],
                                    w1slab[r][:, 256 * ch + 128 * jj:
                                              256 * ch + 128 * (jj + 1)],
                                    xnt5[:, 1024 * ch + 512 * tb:
                                         1024 * ch + 512 * (tb + 1)],
                                    start=(ch == 0), stop=(ch == NCH - 1),
                                    skip_group_check=True)
                    cP.inc(last)
                    em["mm"] = cP.n
                    slab_free[r] = (cP, cP.n)
                self.T(_)

                emx = {}

                def _(sc, e=e, mg=mg, em=em, emx=emx):
                    sc.wait_ge(cP.sem, em["mm"])
                    afunc = [AF.Tanh, AF.Gelu, AF.Silu][e]
                    base = 4 * (mg % 2)
                    for q in range(4):
                        jj, tb = q // 2, q % 2
                        fc = 2 * mg + jj
                        c, v0 = tn_free[q]
                        sc.wait_ge(c.sem, v0)
                        last = sc.activation(tnf[q][:, :], pp[base + q][:, :],
                                             afunc,
                                             bias=self.b1t[:, 32 * e + fc:
                                                           32 * e + fc + 1])
                        cC.inc(last)
                        emx[q] = cC.n
                self.C(_)
                base = 4 * (mg % 2)
                for q in range(4):
                    pp_free[base + q] = (cC, emx[q])

                def _(v, e=e, mg=mg, emx=emx, afm=act_free):
                    c, v0 = afm
                    v.wait_ge(c.sem, v0)
                    for q in range(4):
                        jj, tb = q // 2, q % 2
                        fc = 2 * mg + jj
                        v.wait_ge(cC.sem, emx[q])
                        dst = act_sb[:, 1024 * fc + 512 * tb:
                                     1024 * fc + 512 * (tb + 1)]
                        cslice = coefb[:, 1024 * e + 512 * tb:
                                       1024 * e + 512 * (tb + 1)]
                        last = v.tensor_tensor(dst, tnf[q][:, :], cslice, ALU.mult)
                        cV.inc(last)
                        tn_free[q] = (cV, cV.n)
                self.V(_)
            act_ready = cV.n
            self.fence()
            frp.close()

            fro = contextlib.ExitStack()
            w2slab = [self.sb(f"p5_w2{e}_{r}", [128, NCH * 512], BF16, fro)
                      for r in range(2)]
            scrin = [self.sb(f"p5_si{e}_{r}", [128, 512], F32, fro) for r in range(2)]
            outsb = [self.sb(f"p5_ou{e}_{r}", [128, 512], F32, fro) for r in range(2)]
            ob = [self.ps(f"p5o{e}_{r}", [128, 512], F32, fro) for r in range(4)]

            slab_free = [(cP, 0), (cP, 0)]
            slab_ready = [0, 0]
            o_free = [(cV, 0)] * 4
            si_free = [(cV, 0), (cV, 0)]
            si_ready = [0, 0]
            os_free = [(self.sOS[0], self.sOS[0].n), (self.sOS[1], self.sOS[1].n)]
            dstT = self.scr if e < 2 else self.out

            for db in range(8):
                r = db % 2

                def _(gp2, e=e, db=db, r=r):
                    c, v0 = slab_free[r]
                    gp2.wait_ge(c.sem, v0)
                    last = gp2.dma_start(
                        out=w2slab[r][:, :].rearrange("p (c n) -> p c n", c=NCH),
                        in_=self.w2[e][:, 512 * db:512 * (db + 1)].rearrange(
                            "(c p) n -> p c n", p=128))
                    self.sWS[r].inc(last)
                    slab_ready[r] = self.sWS[r].n
                self.G(_)

                for tau in range(8):
                    cr = tau % 2
                    bk = (8 * db + tau) % 4
                    em = {}

                    def _(sy, db=db, tau=tau, cr=cr):
                        c, v0 = si_free[cr]
                        sy.wait_ge(c.sem, v0)
                        last = sy.dma_start(out=scrin[cr][:, :],
                                            in_=self.scr[128 * tau:128 * (tau + 1),
                                                         512 * db:512 * (db + 1)])
                        self.sSI[cr].inc(last)
                        si_ready[cr] = self.sSI[cr].n
                    self.S(_)

                    def _(t, tau=tau, r=r, bk=bk, em=em, ar=act_ready):
                        t.wait_ge(self.sWS[r].sem, slab_ready[r])
                        t.wait_ge(cV.sem, ar)
                        c, v0 = o_free[bk]
                        t.wait_ge(c.sem, v0)
                        for fc in range(NCH):
                            last = t.matmul(
                                ob[bk][:, :],
                                act_sb[:, 1024 * fc + 128 * tau:
                                       1024 * fc + 128 * (tau + 1)],
                                w2slab[r][:, 512 * fc:512 * (fc + 1)],
                                start=(fc == 0), stop=(fc == NCH - 1),
                                skip_group_check=True)
                        cP.inc(last)
                        em["mm"] = cP.n
                        slab_free[r] = (cP, cP.n)
                    self.T(_)

                    def _(v, cr=cr, bk=bk, em=em):
                        v.wait_ge(cP.sem, em["mm"])
                        v.wait_ge(self.sSI[cr].sem, si_ready[cr])
                        c, v0 = os_free[cr]
                        v.wait_ge(c.sem, v0)
                        last = v.tensor_tensor(outsb[cr][:, :], ob[bk][:, :],
                                               scrin[cr][:, :], ALU.add)
                        cV.inc(last)
                        em["add"] = cV.n
                    self.V(_)
                    o_free[bk] = (cV, cV.n)
                    si_free[cr] = (cV, cV.n)

                    def _(sc, db=db, tau=tau, cr=cr, em=em, dstT=dstT):
                        sc.wait_ge(cV.sem, em["add"])
                        last = sc.dma_start(out=dstT[128 * tau:128 * (tau + 1),
                                                     512 * db:512 * (db + 1)],
                                            in_=outsb[cr][:, :])
                        self.sOS[cr].inc(last)
                        os_free[cr] = (self.sOS[cr], self.sOS[cr].n)
                    self.C(_)
            act_free = (cP, cP.n)
            self.fence()
            fro.close()
        fr.close()

    def _finish(self):
        def _(sy):
            for c in self._alldsems:
                if c.n > 0:
                    sy.wait_ge(c.sem, c.n)
        self.S(_)


_CACHE = {}


def get_prog(debug=0, phases=5, **kw):
    key = (debug, phases, tuple(sorted(kw.items())))
    if key not in _CACHE:
        _CACHE[key] = Prog(debug, phases, **kw).build()
    return _CACHE[key]


def host_inputs(inputs, core):
    beta, par = core // 2, core % 2
    x = np.ascontiguousarray(np.asarray(inputs["L0_attn_x"], dtype=np.float32)
                             .reshape(4, S, D)[beta])
    rows = np.concatenate([np.arange(128 * (2 * i + par), 128 * (2 * i + par) + 128)
                           for i in range(8)])
    xo = np.ascontiguousarray(x[rows])
    gw = np.asarray(inputs["L0_ffn_gate_w"], dtype=np.float32)
    gwpc = np.ascontiguousarray(
        gw[:4096].reshape(32, 128, 3).transpose(1, 0, 2).reshape(128, 96))
    b1 = np.asarray(inputs["L0_ffn_b1"], dtype=np.float32)
    b1pc = np.ascontiguousarray(
        b1.reshape(3, 32, 128).transpose(2, 0, 1).reshape(128, 96))
    masks = np.zeros((16, 128, 512), dtype=np.float32)
    for j in range(16):
        qb = 0 if j < 8 else 1
        kt = j
        for sl in range(4):
            g = 2 * (4 * qb + sl) + par
            qpos = 128 * g + np.arange(128)[None, :]
            kpos = 128 * kt + np.arange(128)[:, None]
            masks[j][:, 128 * sl:128 * (sl + 1)] = (kpos <= qpos)
    selvt = np.zeros((128, 32), dtype=np.float32)
    for t in range(16):
        tok = 128 * t + np.arange(128)
        selvt[:, t] = (tok < VISLEN)
        selvt[:, 16 + t] = (tok >= VISLEN)
    return {
        "xf": x, "xo": xo,
        "wq": np.asarray(inputs["L0_attn_wq"], dtype=np.float32),
        "wk": np.asarray(inputs["L0_attn_wk"], dtype=np.float32),
        "wv": np.asarray(inputs["L0_attn_wv"], dtype=np.float32),
        "wo": np.asarray(inputs["L0_attn_wo"], dtype=np.float32),
        "w1": np.asarray(inputs["L0_ffn_w1"], dtype=np.float32),
        "w2": np.asarray(inputs["L0_ffn_w2"], dtype=np.float32),
        "gwpc": gwpc,
        "gwlast": np.ascontiguousarray(gw[4096:4097]),
        "gbin": np.asarray(inputs["L0_ffn_gate_b"], dtype=np.float32).reshape(1, 3),
        "b1pc": b1pc,
        "b2in": np.asarray(inputs["L0_ffn_b2"], dtype=np.float32),
        "masksin": masks.astype(ml_dtypes.bfloat16),
        "selvtin": selvt,
        "parcolin": np.full((128, 1), 128.0 * par, dtype=np.float32),
        "idfin": np.eye(128, dtype=np.float32),
        "idbin": np.eye(128, dtype=ml_dtypes.bfloat16),
    }


def kernel(**inputs):
    prog = get_prog(0)
    in_maps = [host_inputs(inputs, c) for c in range(8)]
    res = run_bass_kernel_spmd(prog.nc, in_maps, core_ids=list(range(8)))
    full = np.zeros((4, S, D), dtype=np.float32)
    for c in range(8):
        beta, par = c // 2, c % 2
        o = np.asarray(res.results[c]["out"]).reshape(8, 128, D)
        for i in range(8):
            full[beta, 128 * (2 * i + par):128 * (2 * i + par) + 128] = o[i]
    return full


# version 05
# speedup vs baseline: 1.0201x; 1.0201x over previous
import contextlib
import numpy as np
import ml_dtypes
import concourse.bass as bass
import concourse.mybir as mybir
from concourse.bass_utils import run_bass_kernel_spmd

F32 = mybir.dt.float32
BF16 = mybir.dt.bfloat16
I32 = mybir.dt.int32
AF = mybir.ActivationFunctionType
ALU = mybir.AluOpType
AX = mybir.AxisListType

D = 4096
S = 2048
TOWN = 1024
NCH = 32
NH = 32
NG = 8
SCALE = float(1.0 / np.sqrt(128.0))
EXPB = -30.0
VISLEN = 614
ROPE_BASE = 10000.0
TWO_PI = float(2.0 * np.pi)
INVSQRT2 = float(1.0 / np.sqrt(2.0))


class Ctr:
    def __init__(self, sem, step=1):
        self.sem = sem
        self.n = 0
        self.step = step
        self._last = None

    def inc(self, inst, k=1):
        if inst is self._last:
            return self.n
        inst.then_inc(self.sem, k * self.step)
        self._last = inst
        self.n += k * self.step
        return self.n


_SYNCED_OPS = {
    "tensor_tensor", "tensor_add", "tensor_sub", "tensor_mul", "tensor_max",
    "tensor_scalar", "tensor_single_scalar", "tensor_scalar_add",
    "tensor_scalar_sub", "tensor_scalar_mul", "tensor_scalar_max",
    "tensor_scalar_min", "scalar_tensor_tensor", "tensor_copy", "tensor_reduce",
    "reduce_sum", "reduce_max", "reciprocal", "memset", "memzero", "iota",
    "activation", "copy", "add", "mul", "square", "sqrt", "select",
    "tensor_relu", "affine_select", "sign",
}


class SEng:

    def __init__(self, eng, ctr):
        self._e = eng
        self._c = ctr

    def __getattr__(self, name):
        f = getattr(self._e, name)
        if name in _SYNCED_OPS:
            def wrapped(*a, **k):
                if self._c.n > 0:
                    self._e.wait_ge(self._c.sem, self._c.n)
                inst = f(*a, **k)
                self._c.inc(inst)
                return inst
            return wrapped
        return f


class Prog:
    def __init__(self, debug=0, phases=5, p1_quarters=4, p1_groups=16,
                 p1_tiles=4, p1_skip=""):
        self.phases = phases
        self.p1_quarters = p1_quarters
        self.p1_groups = p1_groups
        self.p1_tiles = p1_tiles
        self.p1_skip = set(p1_skip.split(",")) if p1_skip else set()
        self.debug = debug
        self.nc = bass.Bass("TRN2", target_bir_lowering=False)
        self.es = contextlib.ExitStack()
        self.dbg = {}

    def T(self, fn):
        self.blk.tensor(fn)

    def V(self, fn):
        self.blk.vector(lambda e: fn(SEng(e, self.cV)))

    def C(self, fn):
        self.blk.scalar(lambda e: fn(SEng(e, self.cC)))

    def G(self, fn):
        self.blk.gpsimd(lambda e: fn(SEng(e, self.cG)))

    def S(self, fn):
        self.blk.sync(fn)

    def fence(self):
        marks = [(c, c.n) for c in
                 ([self.cP, self.cV, self.cC, self.cG] + self._alldsems)
                 if c.n > 0]

        def _(e):
            for c, v in marks:
                e.wait_ge(c.sem, v)
        for emit in (self.T, self.V, self.C, self.S, self.G):
            emit(_)

    def sb(self, name, shape, dt, stack=None):
        return (stack or self.es).enter_context(self.nc.sbuf_tensor(name, shape, dt))

    def ps(self, name, shape, dt, stack):
        return stack.enter_context(self.nc.psum_tensor(name, shape, dt))

    def build(self):
        nc = self.nc

        def dram(name, shape, dt, kind):
            return nc.dram_tensor(name, shape, dt, kind=kind).ap()

        self.xf = dram("xf", [S, D], F32, "ExternalInput")
        self.xo = dram("xo", [TOWN, D], F32, "ExternalInput")
        self.wq = dram("wq", [D, D], F32, "ExternalInput")
        self.wk = dram("wk", [D, 1024], F32, "ExternalInput")
        self.wv = dram("wv", [D, 1024], F32, "ExternalInput")
        self.wo = dram("wo", [D, D], F32, "ExternalInput")
        self.w1 = dram("w1", [3, D, D], F32, "ExternalInput")
        self.w2 = dram("w2", [3, D, D], F32, "ExternalInput")
        self.gwpc = dram("gwpc", [128, 96], F32, "ExternalInput")
        self.gwlast = dram("gwlast", [1, 3], F32, "ExternalInput")
        self.gbin = dram("gbin", [1, 3], F32, "ExternalInput")
        self.b1pc = dram("b1pc", [128, 96], F32, "ExternalInput")
        self.b2in = dram("b2in", [3, D], F32, "ExternalInput")
        self.masksin = dram("masksin", [16, 128, 512], BF16, "ExternalInput")
        self.selvtin = dram("selvtin", [128, 32], F32, "ExternalInput")
        self.parcolin = dram("parcolin", [128, 1], F32, "ExternalInput")
        self.idfin = dram("idfin", [128, 128], F32, "ExternalInput")
        self.idbin = dram("idbin", [128, 128], BF16, "ExternalInput")

        self.xnt_hbm = dram("xnt_hbm", [128, NCH * TOWN], BF16, "Internal")
        self.q_hbm = dram("q_hbm", [128, NH * TOWN], BF16, "Internal")
        self.ctx_hbm = dram("ctx_hbm", [128, 8 * D], BF16, "Internal")
        self.scr = dram("scr", [TOWN, D], F32, "Internal")
        self.out = dram("out", [TOWN, D], F32, "ExternalOutput")
        if self.debug >= 1:
            self.dbg["xntq"] = dram("d_xntq", [128, NCH * 512], BF16, "ExternalOutput")
            self.dbg["kfeat"] = dram("d_kfeat", [128, NG * S], BF16, "ExternalOutput")
            self.dbg["vtok"] = dram("d_vtok", [128, 16 * 1024], BF16, "ExternalOutput")
            self.dbg["misc"] = dram("d_misc", [128, 64], F32, "ExternalOutput")
        if self.debug >= 2:
            self.dbg["xnt"] = dram("d_xnt", [128, NCH * TOWN], BF16, "ExternalOutput")
            self.dbg["qh"] = dram("d_qh", [128, NH * TOWN], BF16, "ExternalOutput")
            self.dbg["coef"] = dram("d_coef", [128, 32], F32, "ExternalOutput")
        if self.debug >= 3:
            self.dbg["ctx"] = dram("d_ctx", [128, 8 * D], BF16, "ExternalOutput")
        if self.debug >= 4:
            self.dbg["scr"] = dram("d_scr", [TOWN, D], F32, "ExternalOutput")

        sems = {k: self.es.enter_context(nc.semaphore(k)) for k in
                ["sp", "sv", "sc", "sg"]}
        self.cP = Ctr(sems["sp"])
        self.cV = Ctr(sems["sv"])
        self.cC = Ctr(sems["sc"])
        self.cG = Ctr(sems["sg"])
        self._alldsems = []

        def dsem(name):
            c = Ctr(self.es.enter_context(nc.semaphore(name)), 16)
            self._alldsems.append(c)
            return c
        self.dsem = dsem
        self.sConst = dsem("dconst")
        self.sX = [dsem(f"dx{r}") for r in range(3)]
        self.sWS = [dsem(f"dws{r}") for r in range(2)]
        self.sQS = [dsem(f"dqs{r}") for r in range(2)]
        self.sQB = [dsem(f"dqb{r}") for r in range(3)]
        self.sCS = [dsem(f"dcs{r}") for r in range(2)]
        self.sCX = [dsem(f"dcx{r}") for r in range(2)]
        self.sOS = [dsem(f"dos{r}") for r in range(2)]
        self.sSI = [dsem(f"dsi{r}") for r in range(2)]
        self.sXnt = dsem("dxnt")
        self.sMask = dsem("dmask")
        self.sDbg = dsem("ddbg")

        sb = self.sb
        self.idf = sb("s_idf", [128, 128], F32)
        self.idb = sb("s_idb", [128, 128], BF16)
        self.gw = sb("s_gw", [128, 96], F32)
        self.b1t = sb("s_b1t", [128, 96], F32)
        self.selvt = sb("s_selvt", [128, 32], F32)
        self.parcol = sb("s_parcol", [128, 1], F32)
        self.gwl = sb("s_gwl", [1, 3], F32)
        self.gb = sb("s_gb", [1, 3], F32)
        self.ones_row = sb("s_ones_row", [1, 128], F32)
        self.ones_colf = sb("s_ones_colf", [128, 1], F32)
        self.ones_colb = sb("s_ones_colb", [128, 1], BF16)
        self.iota_col = sb("s_iota_col", [128, 1], F32)
        self.eps_col = sb("s_eps_col", [128, 1], F32)
        self.negb_col = sb("s_negb_col", [128, 1], F32)
        self.invfreq = sb("s_invfreq", [128, 64], F32)
        self.ctabf = sb("s_ctabf", [128, 1024], F32)
        self.stabf = sb("s_stabf", [128, 1024], F32)
        self.ctabo = sb("s_ctabo", [128, 512], F32)
        self.stabo = sb("s_stabo", [128, 512], F32)
        self.absstage = sb("s_absstage", [128, 16], F32)
        self.coefT = sb("s_coefT", [3, TOWN], F32)
        self.ohb = sb("s_ohb", [128, 8], F32)
        self.sel3 = sb("s_sel3", [3, 384], F32)
        self.ohrow = sb("s_ohrow", [1, 8], F32)
        self.ratio = sb("s_ratio", [1, 8], F32)
        self.gcorr = sb("s_gcorr", [1, 4], F32)
        self.stat = sb("s_stat", [128, 16], F32)

        self.blk = self.es.enter_context(nc.Block())
        self._init_consts()

        kvf = contextlib.ExitStack()
        self.kfeat = sb("s_kfeat", [128, NG * S], BF16, kvf)
        self.vtok = sb("s_vtok", [128, 16 * 1024], BF16, kvf)
        if self.phases >= 1:
            self._p1()
            self.fence()
        if self.phases >= 2:
            self._p2()
            self.fence()
        if self.phases >= 3:
            self._p3()
            self.fence()
        kvf.close()
        if self.phases >= 4:
            self._p4()
            self.fence()
        if self.phases >= 5:
            self._p5()
        self._finish()
        self.es.close()
        return self

    def _init_consts(self):
        cA = self.sConst
        cV, cC, cG = self.cV, self.cC, self.cG
        fr = contextlib.ExitStack()
        i32b = self.sb("i_i32b", [128, 1024], I32, fr)
        fb = self.sb("i_fb", [128, 1024], F32, fr)
        ub = self.sb("i_ub", [128, 1024], F32, fr)

        def _(sy):
            for dst, src in [(self.idf, self.idfin), (self.idb, self.idbin),
                             (self.gw, self.gwpc), (self.b1t, self.b1pc),
                             (self.selvt, self.selvtin),
                             (self.parcol, self.parcolin), (self.gwl, self.gwlast),
                             (self.gb, self.gbin)]:
                cA.inc(sy.dma_start(out=dst[:, :], in_=src[:, :]))
        self.S(_)
        w_const = cA.n

        def _(g):
            g.memset(self.ones_row[:, :], 1.0)
            g.memset(self.ones_colf[:, :], 1.0)
            g.memset(self.ones_colb[:, :], 1.0)
            g.memset(self.eps_col[:, :], 1e-5)
            g.memset(self.negb_col[:, :], EXPB)
            g.memset(self.ohb[:, :], 0.0)
            g.memset(self.ohrow[:, :], 0.0)
            g.memset(self.ratio[:, :], 0.0)
            g.iota(self.iota_col[:, :], [[0, 1]], base=0, channel_multiplier=1,
                   allow_small_or_imprecise_dtypes=True)
            g.iota(self.sel3[:, :], [[0, 384]], base=0, channel_multiplier=1,
                   allow_small_or_imprecise_dtypes=True)
            last = g.iota(self.invfreq[:, :], [[1, 64]], base=0, channel_multiplier=0,
                          allow_small_or_imprecise_dtypes=True)
            cG.inc(last)
        self.G(_)
        w_ms = cG.n

        def _(sc):
            sc.wait_ge(cG.sem, w_ms)
            last = sc.activation(self.invfreq[:, :], self.invfreq[:, :], AF.Exp,
                                 scale=float(-np.log(ROPE_BASE) / 64.0))
            cC.inc(last)
        self.C(_)

        def _(v):
            v.wait_ge(cG.sem, w_ms)
            for e in range(3):
                v.tensor_single_scalar(self.sel3[:, 128 * e:128 * (e + 1)],
                                       self.sel3[:, 128 * e:128 * (e + 1)],
                                       float(e), ALU.is_equal)
        self.V(_)
        w_if = cC.n
        waits = {}

        def gen_tab(name, ctab, stab, bases, use_par, nf):
            def _(v):
                v.wait_ge(cC.sem, w_if)
                v.wait_ge(cA.sem, w_const)
                for j, b in enumerate(bases):
                    v.tensor_scalar(fb[:, 1020:1021], self.iota_col[:, 0:1],
                                    float(b), None, ALU.add)
                    if use_par:
                        v.tensor_tensor(fb[:, 1020:1021], fb[:, 1020:1021],
                                        self.parcol[:, 0:1], ALU.add)
                    v.tensor_scalar(ub[:, 64 * j:64 * j + 64], self.invfreq[:, :],
                                    fb[:, 1020:1021], float(1.0 / TWO_PI),
                                    ALU.mult, ALU.mult)
                def frac_pm_half(dst):
                    v.tensor_copy(i32b[:, 0:nf], ub[:, 0:nf])
                    v.tensor_copy(fb[:, 0:nf], i32b[:, 0:nf])
                    v.tensor_tensor(dst[:, 0:nf], ub[:, 0:nf], fb[:, 0:nf],
                                    ALU.subtract)
                    v.tensor_single_scalar(fb[:, 0:nf], dst[:, 0:nf], 0.5,
                                           ALU.is_gt)
                    return v.tensor_tensor(dst[:, 0:nf], dst[:, 0:nf],
                                           fb[:, 0:nf], ALU.subtract)
                cV.inc(frac_pm_half(stab))
                waits[name + "s"] = cV.n
                v.tensor_scalar(ub[:, 0:nf], ub[:, 0:nf], 0.25, None, ALU.add)
                cV.inc(frac_pm_half(ctab))
                waits[name + "c"] = cV.n
            self.V(_)

            def _(sc):
                sc.wait_ge(cV.sem, waits[name + "s"])
                sc.activation(stab[:, 0:nf], stab[:, 0:nf], AF.Sin, scale=TWO_PI)
                sc.wait_ge(cV.sem, waits[name + "c"])
                last = sc.activation(ctab[:, 0:nf], ctab[:, 0:nf], AF.Sin, scale=TWO_PI)
                cC.inc(last)
            self.C(_)

        gen_tab("f", self.ctabf, self.stabf, [128 * t for t in range(16)], False, 1024)
        gen_tab("o", self.ctabo, self.stabo, [256 * t for t in range(8)], True, 512)
        self.w_tabs = cC.n
        self.w_const = w_const
        self.fence()
        fr.close()

    def _ln_tile(self, src_ap, r, xn_dst, abs_col, marks_out):
        cV, cC = self.cV, self.cC
        cA = self.sX[r]
        xbuf, st = self.xbuf, self.stat
        sc0 = 8 * r
        m = {}

        def _(sy):
            c, v0 = self.xb_free[r]
            sy.wait_ge(c.sem, v0)
            cA.inc(sy.dma_start(out=xbuf[r][:, :], in_=src_ap))
            self.xb_ready[r] = cA.n
        self.S(_)

        def _(v):
            v.wait_ge(cA.sem, self.xb_ready[r])
            c, v0 = self.stat_free[r]
            v.wait_ge(c.sem, v0)
            v.reduce_sum(st[:, sc0:sc0 + 1], xbuf[r][:, :], AX.X)
            v.tensor_scalar(st[:, sc0 + 1:sc0 + 2], st[:, sc0:sc0 + 1],
                            float(1.0 / D), None, ALU.mult)
            v.tensor_tensor(st[:, sc0 + 2:sc0 + 3], st[:, sc0 + 1:sc0 + 2],
                            st[:, sc0 + 1:sc0 + 2], ALU.mult)
            c, v0 = self.xn_free[r]
            v.wait_ge(c.sem, v0)
            c, v0 = self.absmark[r]
            if c is not None:
                v.wait_ge(c.sem, v0)
            v.scalar_tensor_tensor(xn_dst[:, :], xbuf[r][:, :], 1.0, xbuf[r][:, :],
                                   ALU.mult, ALU.mult,
                                   accum_out=st[:, sc0 + 3:sc0 + 4])
            last = v.scalar_tensor_tensor(st[:, sc0 + 4:sc0 + 5], st[:, sc0 + 3:sc0 + 4],
                                          float(1.0 / D), st[:, sc0 + 2:sc0 + 3],
                                          ALU.mult, ALU.subtract)
            cV.inc(last)
            m["var"] = cV.n
        self.V(_)

        def _(sc):
            sc.wait_ge(cV.sem, m["var"])
            sc.activation(st[:, sc0 + 5:sc0 + 6], st[:, sc0 + 4:sc0 + 5], AF.Ln,
                          bias=self.eps_col[:, 0:1])
            last = sc.activation(st[:, sc0 + 6:sc0 + 7], st[:, sc0 + 5:sc0 + 6],
                                 AF.Exp, scale=-0.5)
            cC.inc(last)
            m["rstd"] = cC.n
        self.C(_)

        def _(v):
            v.wait_ge(cC.sem, m["rstd"])
            v.scalar_tensor_tensor(st[:, sc0 + 7:sc0 + 8], st[:, sc0 + 1:sc0 + 2],
                                   -1.0, st[:, sc0 + 6:sc0 + 7],
                                   ALU.mult, ALU.mult)
            last = v.tensor_scalar(xn_dst[:, :], xbuf[r][:, :],
                                   st[:, sc0 + 6:sc0 + 7], st[:, sc0 + 7:sc0 + 8],
                                   ALU.mult, ALU.add)
            cV.inc(last)
            marks_out["xn"] = (cV, cV.n)
            self.xb_free[r] = (cV, cV.n)
            self.stat_free[r] = (cV, cV.n)
            if abs_col is not None:
                last = v.reduce_sum(self.absstage[:, abs_col:abs_col + 1],
                                    xn_dst[:, :], AX.X, apply_absolute_value=True)
                cV.inc(last)
                self.absmark[r] = (cV, cV.n)
        self.V(_)

    def _p1(self):
        cP, cV, cC = self.cP, self.cV, self.cC
        fr = contextlib.ExitStack()
        self.xbuf = [self.sb(f"p1_x{r}", [128, D], F32, fr) for r in range(2)]
        xnb = [self.sb(f"p1_xn{r}", [128, D], BF16, fr) for r in range(2)]
        xntq = self.sb("p1_xntq", [128, NCH * 512], BF16, fr)
        wslab = [self.sb(f"p1_w{r}", [128, NCH * 128], BF16, fr) for r in range(2)]
        ktmp = self.sb("p1_ktmp", [128, 512], F32, fr)
        ktok = self.sb("p1_ktok", [128, 512], F32, fr)
        krot = self.sb("p1_krot", [128, 512], BF16, fr)
        vtmp = self.sb("p1_vtmp", [128, 512], BF16, fr)
        rt1 = self.sb("p1_rt1", [128, 256], F32, fr)
        rt2 = self.sb("p1_rt2", [128, 256], F32, fr)
        kb = [self.ps(f"p1kb{r}", [128, 512], F32, fr) for r in range(2)]
        vb = [self.ps(f"p1vb{r}", [128, 512], F32, fr) for r in range(2)]
        ft = self.ps("p1ft", [128, 512], F32, fr)
        bfb = [self.ps(f"p1bf{r}", [128, 1024], BF16, fr) for r in range(2)]

        self.xb_free = [(self.sX[0], 0), (self.sX[1], 0)]
        self.xb_ready = [0, 0]
        self.xn_free = [(cP, 0), (cP, 0)]
        self.stat_free = [(cV, 0), (cV, 0)]
        self.absmark = [(None, 0), (None, 0)]
        bfb_free = [(cV, 0), (cV, 0)]
        slab_free = [(cP, 0), (cP, 0)]
        slab_ready = [0, 0]
        kb_free = [(cV, 0), (cV, 0)]
        vb_free = [(cV, 0), (cV, 0)]
        ft_free = (cV, 0)
        ktmp_free = (cP, 0)
        ktok_free = (cV, 0)
        krot_free = (cP, 0)
        vtmp_free = (cP, 0)
        prev_mm = 0
        pending = None

        def emit_chain(job):
            nonlocal ft_free, ktmp_free, ktok_free, krot_free, vtmp_free
            is_k, g, Q, em = job
            if is_k:
                def _(v):
                    nonlocal ktmp_free
                    v.wait_ge(cP.sem, em["mm"])
                    c, v0 = ktmp_free
                    v.wait_ge(c.sem, v0)
                    last = v.tensor_copy(ktmp[:, :], kb[g % 2][:, :])
                    cV.inc(last)
                    em["ktmp"] = cV.n
                self.V(_)
                kb_free[g % 2] = (cV, cV.n)

                def _(t):
                    nonlocal ft_free
                    t.wait_ge(cV.sem, em["ktmp"])
                    c, v0 = ft_free
                    t.wait_ge(c.sem, v0)
                    for u in range(4):
                        last = t.transpose(ft[:, 128 * u:128 * (u + 1)],
                                           ktmp[:, 128 * u:128 * (u + 1)],
                                           self.idf[:, :])
                    cP.inc(last)
                    em["ftp"] = cP.n
                self.T(_)
                ktmp_free = (cP, cP.n)

                def _(v):
                    nonlocal ktok_free, krot_free
                    v.wait_ge(cP.sem, em["ftp"])
                    c, v0 = ktok_free
                    v.wait_ge(c.sem, v0)
                    last = v.tensor_copy(ktok[:, :], ft[:, :])
                    cV.inc(last)
                    em["ktok"] = cV.n
                    v.wait_ge(cC.sem, self.w_tabs)
                    c, v0 = krot_free
                    v.wait_ge(c.sem, v0)
                    ce = self.ctabf[:, 256 * Q:256 * Q + 256]
                    se = self.stabf[:, 256 * Q:256 * Q + 256]
                    ke = ktok[:, 0:512:2]
                    ko = ktok[:, 1:512:2]
                    v.tensor_tensor(rt1[:, :], ke, ce, ALU.mult)
                    v.tensor_tensor(rt2[:, :], ko, se, ALU.mult)
                    v.tensor_tensor(krot[:, 0:512:2], rt1[:, :], rt2[:, :],
                                    ALU.subtract)
                    v.tensor_tensor(rt1[:, :], ke, se, ALU.mult)
                    v.tensor_tensor(rt2[:, :], ko, ce, ALU.mult)
                    last = v.tensor_tensor(krot[:, 1:512:2], rt1[:, :], rt2[:, :],
                                           ALU.add)
                    cV.inc(last)
                    em["krot"] = cV.n
                self.V(_)
                ft_free = (cV, cV.n)
                ktok_free = (cV, cV.n)

                def _(t):
                    nonlocal krot_free
                    t.wait_ge(cV.sem, em["krot"])
                    c, v0 = bfb_free[0]
                    t.wait_ge(c.sem, v0)
                    for u in range(4):
                        last = t.transpose(bfb[0][:, 128 * u:128 * (u + 1)],
                                           krot[:, 128 * u:128 * (u + 1)],
                                           self.idb[:, :])
                    cP.inc(last)
                    em["kbt"] = cP.n
                self.T(_)
                krot_free = (cP, cP.n)

                def _(v):
                    v.wait_ge(cP.sem, em["kbt"])
                    last = v.tensor_copy(
                        self.kfeat[:, 2048 * g + 512 * Q:2048 * g + 512 * Q + 512],
                        bfb[0][:, 0:512])
                    cV.inc(last)
                    bfb_free[0] = (cV, cV.n)
                self.V(_)
            else:
                def _(v):
                    nonlocal vtmp_free
                    v.wait_ge(cP.sem, em["mm"])
                    c, v0 = vtmp_free
                    v.wait_ge(c.sem, v0)
                    last = v.tensor_copy(vtmp[:, :], vb[g % 2][:, :])
                    cV.inc(last)
                    em["vtmp"] = cV.n
                self.V(_)
                vb_free[g % 2] = (cV, cV.n)

                def _(t):
                    nonlocal vtmp_free
                    t.wait_ge(cV.sem, em["vtmp"])
                    c, v0 = bfb_free[1]
                    t.wait_ge(c.sem, v0)
                    for u in range(4):
                        last = t.transpose(bfb[1][:, 128 * u:128 * (u + 1)],
                                           vtmp[:, 128 * u:128 * (u + 1)],
                                           self.idb[:, :])
                    cP.inc(last)
                    em["vbt"] = cP.n
                self.T(_)
                vtmp_free = (cP, cP.n)

                def _(v):
                    v.wait_ge(cP.sem, em["vbt"])
                    dst = self.vtok[:, :].rearrange("p (t n) -> p t n", t=16)[
                        :, 4 * Q:4 * Q + 4, 128 * g:128 * g + 128]
                    src = bfb[1][:, 0:512].rearrange("p (u n) -> p u n", u=4)
                    last = v.tensor_copy(dst, src)
                    cV.inc(last)
                    bfb_free[1] = (cV, cV.n)
                self.V(_)

        for Q in range(self.p1_quarters):
            for j, tg in enumerate(range(4 * Q, 4 * Q + self.p1_tiles)):
                r = tg % 2
                tq = tg % 4
                lm = {}
                self._ln_tile(self.xf[128 * tg:128 * (tg + 1), :], r, xnb[r],
                              None if "abs" in self.p1_skip else tg, lm)
                if "tp" in self.p1_skip:
                    continue
                tmarks = {}
                for half in range(2):
                    def _(t, r=r, lm=lm, half=half, tmarks=tmarks):
                        if half == 0:
                            c, v0 = lm["xn"]
                            t.wait_ge(c.sem, v0)
                        for bb in range(2):
                            c, v0 = bfb_free[bb]
                            t.wait_ge(c.sem, v0)
                            for u in range(8):
                                ch = 16 * half + 8 * bb + u
                                last = t.transpose(bfb[bb][:, 128 * u:128 * (u + 1)],
                                                   xnb[r][:, 128 * ch:128 * (ch + 1)],
                                                   self.idb[:, :])
                            cP.inc(last)
                            tmarks[(half, bb)] = cP.n
                    self.T(_)
                    if "evac" in self.p1_skip:
                        continue

                    def _(sc, tq=tq, half=half, tmarks=tmarks, pm=prev_mm):
                        if half == 0:
                            sc.wait_ge(cP.sem, pm)
                        for bb in range(2):
                            sc.wait_ge(cP.sem, tmarks[(half, bb)])
                            c0 = 16 * half + 8 * bb
                            dst = xntq[:, :].rearrange("p (c n) -> p c n", c=NCH)[
                                :, c0:c0 + 8, 128 * tq:128 * tq + 128]
                            src = bfb[bb][:, :].rearrange("p (u n) -> p u n", u=8)
                            last = sc.copy(out=dst, in_=src)
                            cC.inc(last)
                            bfb_free[bb] = (cC, cC.n)
                    self.C(_)
                self.xn_free[r] = (cP, cP.n)
            evac_last = cC.n

            for gi in range(self.p1_groups):
                r = gi % 2
                g = gi % 8
                is_k = gi < 8

                def _(gp, r=r, g=g, is_k=is_k):
                    c, v0 = slab_free[r]
                    gp.wait_ge(c.sem, v0)
                    src = (self.wk if is_k else self.wv)[:, 128 * g:128 * (g + 1)]
                    last = gp.dma_start(
                        out=wslab[r][:, :].rearrange("p (c n) -> p c n", c=NCH),
                        in_=src.rearrange("(c p) n -> p c n", p=128))
                    self.sWS[r].inc(last)
                    slab_ready[r] = self.sWS[r].n
                self.G(_)

                em = {}

                def _(t, r=r, g=g, is_k=is_k, el=evac_last, em=em):
                    t.wait_ge(self.sWS[r].sem, slab_ready[r])
                    t.wait_ge(cC.sem, el)
                    bank = (kb if is_k else vb)[g % 2]
                    c, v0 = (kb_free if is_k else vb_free)[g % 2]
                    t.wait_ge(c.sem, v0)
                    for ch in range(NCH):
                        last = t.matmul(bank[:, :], wslab[r][:, 128 * ch:128 * (ch + 1)],
                                        xntq[:, 512 * ch:512 * (ch + 1)],
                                        start=(ch == 0), stop=(ch == NCH - 1),
                                        skip_group_check=True)
                    cP.inc(last)
                    em["mm"] = cP.n
                    slab_free[r] = (cP, cP.n)
                self.T(_)

                if pending is not None:
                    emit_chain(pending)
                pending = (is_k, g, Q, em)
            prev_mm = cP.n

        if pending is not None:
            emit_chain(pending)
            pending = None

        if self.debug >= 1:
            def _(sy):
                sy.wait_ge(cC.sem, cC.n)
                sy.wait_ge(cP.sem, cP.n)
                sy.wait_ge(cV.sem, cV.n)
                self.sDbg.inc(sy.dma_start(out=self.dbg["xntq"][:, :], in_=xntq[:, :]))
                self.sDbg.inc(sy.dma_start(out=self.dbg["kfeat"][:, :], in_=self.kfeat[:, :]))
                self.sDbg.inc(sy.dma_start(out=self.dbg["vtok"][:, :], in_=self.vtok[:, :]))
            self.S(_)
        self.fence()
        fr.close()

    def _p2(self):
        cP, cV, cC = self.cP, self.cV, self.cC
        fr2 = contextlib.ExitStack()
        xnt_own = self.sb("p2_xnt", [128, NCH * TOWN], BF16, fr2)
        fra = contextlib.ExitStack()
        self.xbuf = [self.sb(f"p2_x{r}", [128, D], F32, fra) for r in range(2)]
        xnf = [self.sb("p2_xnf0", [128, D], F32, fra)]
        xnf = [xnf[0], xnf[0]]
        gsc = [self.sb(f"p2_gsc{r}", [128, 512], F32, fra) for r in range(2)]
        g3 = [self.sb(f"p2_g3{r}", [128, 4], F32, fra) for r in range(2)]
        coef3 = [self.sb(f"p2_c3{r}", [128, 4], F32, fra) for r in range(2)]
        sm = self.sb("p2_sm", [128, 24], F32, fra)
        amask = self.sb("p2_amask", [128, 16], F32, fra)
        ft2 = [self.ps(f"p2ft{r}", [128, 512], F32, fra) for r in range(2)]
        gp = self.ps("p2gp", [128, 512], F32, fra)
        cp = self.ps("p2cp", [128, 512], F32, fra)

        self.xb_free = [(self.sX[0], self.sX[0].n), (self.sX[1], self.sX[1].n)]
        self.xb_ready = [0, 0]
        self.xn_free = [(cP, 0), (cP, 0)]
        self.stat_free = [(cV, 0), (cV, 0)]
        self.absmark = [(None, 0), (None, 0)]

        rm = {}

        def _(v):
            v.tensor_tensor(amask[:, :], self.absstage[:, :], self.selvt[:, 0:16],
                            ALU.mult)
            v.reduce_sum(self.stat[:, 0:1], amask[:, :], AX.X)
            v.tensor_tensor(amask[:, :], self.absstage[:, :], self.selvt[:, 16:32],
                            ALU.mult)
            last = v.reduce_sum(self.stat[:, 1:2], amask[:, :], AX.X)
            cV.inc(last)
            rm["cols"] = cV.n
        self.V(_)

        def _(t):
            t.wait_ge(cV.sem, rm["cols"])
            t.matmul(gp[0:1, 0:1], self.stat[:, 0:1], self.ones_colf[:, :],
                     start=True, stop=True, skip_group_check=True)
            last = t.matmul(gp[0:1, 1:2], self.stat[:, 1:2], self.ones_colf[:, :],
                            start=True, stop=True, skip_group_check=True)
            cP.inc(last)
            rm["sums"] = cP.n
        self.T(_)

        def _(v):
            v.wait_ge(cP.sem, rm["sums"])
            r = self.ratio
            v.tensor_scalar(r[0:1, 1:2], gp[0:1, 0:1], float(1.0 / (VISLEN * D)),
                            1e-8, ALU.mult, ALU.add)
            v.reciprocal(r[0:1, 2:3], r[0:1, 1:2])
            v.tensor_scalar(r[0:1, 3:4], gp[0:1, 1:2], float(1.0 / ((S - VISLEN) * D)),
                            None, ALU.mult)
            v.tensor_tensor(r[0:1, 0:1], r[0:1, 3:4], r[0:1, 2:3], ALU.mult)
            v.tensor_scalar(self.gcorr[0:1, 0:3], self.gwl[0:1, 0:3], r[0:1, 0:1],
                            None, ALU.mult)
            last = v.tensor_tensor(self.gcorr[0:1, 0:3], self.gcorr[0:1, 0:3],
                                   self.gb[0:1, 0:3], ALU.add)
            cV.inc(last)
            rm["gcorr"] = cV.n
        self.V(_)
        gcorr_ready = rm["gcorr"]
        gp_free = (cV, rm["gcorr"])
        self.stat_free = [(cV, rm["gcorr"]), (cV, rm["gcorr"])]

        ft_free = [(cV, 0), (cV, 0)]
        gsc_free = [(cP, 0), (cP, 0)]
        xnt_marks = []
        for it in range(9):
            isG = (it == 0)
            i = it - 1
            r = it % 2
            src = self.xf[0:128, :] if isG else self.xo[128 * i:128 * (i + 1), :]
            lm = {}
            self._ln_tile(src, r, xnf[r], None, lm)
            tm = {}
            for grp in range(8):
                bb = grp % 2

                def _(t, r=r, lm=lm, tm=tm, grp=grp, bb=bb):
                    if grp == 0:
                        c, v0 = lm["xn"]
                        t.wait_ge(c.sem, v0)
                        t.wait_ge(cV.sem, gcorr_ready)
                    c, v0 = ft_free[bb]
                    t.wait_ge(c.sem, v0)
                    for u in range(4):
                        ch = 4 * grp + u
                        last = t.transpose(ft2[bb][:, 128 * u:128 * (u + 1)],
                                           xnf[r][:, 128 * ch:128 * (ch + 1)],
                                           self.idf[:, :])
                    cP.inc(last)
                    tm[("tp", grp)] = cP.n
                self.T(_)

                def _(v, tm=tm, grp=grp, bb=bb):
                    v.wait_ge(cP.sem, tm[("tp", grp)])
                    c, v0 = gsc_free[bb]
                    v.wait_ge(c.sem, v0)
                    last = v.tensor_copy(gsc[bb][:, :], ft2[bb][:, :])
                    cV.inc(last)
                    tm[("gsc", grp)] = cV.n
                self.V(_)

                if not isG:
                    def _(sc, i=i, tm=tm, grp=grp, bb=bb):
                        sc.wait_ge(cP.sem, tm[("tp", grp)])
                        dst = xnt_own[:, :].rearrange("p (c n) -> p c n", c=NCH)[
                            :, 4 * grp:4 * grp + 4, 128 * i:128 * i + 128]
                        src2 = ft2[bb][:, 0:512].rearrange("p (u n) -> p u n", u=4)
                        last = sc.copy(out=dst, in_=src2)
                        cC.inc(last)
                        tm[("ev", grp)] = cC.n
                    self.C(_)
                    ft_free[bb] = (cC, tm[("ev", grp)])
                else:
                    ft_free[bb] = (cV, tm[("gsc", grp)])

                def _(t, tm=tm, grp=grp, bb=bb, gpf=gp_free, isG=isG):
                    if grp == 0:
                        c, v0 = gpf
                        t.wait_ge(c.sem, v0)
                    t.wait_ge(cV.sem, tm[("gsc", grp)])
                    for u in range(4):
                        ch = 4 * grp + u
                        last = t.matmul(gp[:, 0:3], gsc[bb][:, 128 * u:128 * (u + 1)],
                                        self.gw[:, 3 * ch:3 * ch + 3],
                                        start=(ch == 0), stop=False,
                                        skip_group_check=True)
                    if grp == 7:
                        last = t.matmul(gp[:, 0:3], self.ones_row[:, :],
                                        self.gcorr[0:1, 0:3],
                                        start=False, stop=True, skip_group_check=True)
                        cP.inc(last)
                        tm["gdone"] = cP.n
                self.T(_)
                gsc_free[bb] = (cP, cP.n)
            self.xn_free[r] = (cP, cP.n)
            if not isG:
                xnt_marks.append(tm[("ev", 7)])

            smm = {}

            def _(v, r=r, tm=tm):
                v.wait_ge(cP.sem, tm["gdone"])
                last = v.reduce_max(sm[:, 12 * r:12 * r + 1], gp[:, 0:3], AX.X,
                                    negate=True)
                cV.inc(last)
                smm["negmax"] = cV.n
            self.V(_)

            def _(sc, r=r):
                sc.wait_ge(cV.sem, smm["negmax"])
                last = sc.activation(g3[r][:, 0:3], gp[:, 0:3], AF.Exp,
                                     bias=sm[:, 12 * r:12 * r + 1])
                cC.inc(last)
                smm["exp"] = cC.n
            self.C(_)
            gp_free = (cC, cC.n)

            def _(v, r=r, isG=isG, i=i):
                b = 12 * r
                v.wait_ge(cC.sem, smm["exp"])
                v.reduce_sum(sm[:, b + 1:b + 2], g3[r][:, 0:3], AX.X)
                v.reciprocal(sm[:, b + 2:b + 3], sm[:, b + 1:b + 2])
                v.tensor_scalar(g3[r][:, 0:3], g3[r][:, 0:3], sm[:, b + 2:b + 3],
                                None, ALU.mult)
                v.reduce_max(sm[:, b + 3:b + 4], g3[r][:, 0:3], AX.X)
                v.tensor_reduce(sm[:, b + 4:b + 5], g3[r][:, 0:3], AX.X, ALU.min)
                v.reduce_sum(sm[:, b + 5:b + 6], g3[r][:, 0:3], AX.X)
                v.tensor_tensor(sm[:, b + 6:b + 7], sm[:, b + 5:b + 6],
                                sm[:, b + 3:b + 4], ALU.subtract)
                v.tensor_tensor(sm[:, b + 6:b + 7], sm[:, b + 6:b + 7],
                                sm[:, b + 4:b + 5], ALU.subtract)
                v.tensor_tensor(sm[:, b + 7:b + 8], sm[:, b + 3:b + 4],
                                sm[:, b + 6:b + 7], ALU.add)
                v.tensor_scalar(sm[:, b + 7:b + 8], sm[:, b + 7:b + 8], 1e-8,
                                None, ALU.add)
                v.reciprocal(sm[:, b + 8:b + 9], sm[:, b + 7:b + 8])
                v.tensor_tensor(sm[:, b + 9:b + 10], sm[:, b + 3:b + 4],
                                sm[:, b + 8:b + 9], ALU.mult)
                last = v.tensor_tensor(sm[:, b + 10:b + 11], sm[:, b + 6:b + 7],
                                       sm[:, b + 8:b + 9], ALU.mult)
                cV.inc(last)
                if isG:
                    p3 = g3[r][0:1, 0:3]
                    oh = self.ohrow
                    v.reduce_max(oh[0:1, 6:7], p3, AX.X)
                    v.tensor_scalar(oh[0:1, 0:3], p3, oh[0:1, 6:7], None, ALU.is_ge)
                    v.tensor_scalar(oh[0:1, 7:8], oh[0:1, 0:1], -1.0, 1.0,
                                    ALU.mult, ALU.add)
                    v.tensor_tensor(oh[0:1, 1:2], oh[0:1, 1:2], oh[0:1, 7:8], ALU.mult)
                    v.tensor_scalar(oh[0:1, 6:7], oh[0:1, 1:2], -1.0, 1.0,
                                    ALU.mult, ALU.add)
                    v.tensor_tensor(oh[0:1, 7:8], oh[0:1, 7:8], oh[0:1, 6:7], ALU.mult)
                    v.tensor_tensor(oh[0:1, 2:3], oh[0:1, 2:3], oh[0:1, 7:8], ALU.mult)
                    v.scalar_tensor_tensor(self.ratio[0:1, 4:7], oh[0:1, 0:3], -1e30,
                                           p3, ALU.mult, ALU.add)
                    v.reduce_max(oh[0:1, 6:7], self.ratio[0:1, 4:7], AX.X)
                    v.tensor_scalar(oh[0:1, 3:6], self.ratio[0:1, 4:7], oh[0:1, 6:7],
                                    None, ALU.is_ge)
                    v.tensor_scalar(oh[0:1, 7:8], oh[0:1, 3:4], -1.0, 1.0,
                                    ALU.mult, ALU.add)
                    v.tensor_tensor(oh[0:1, 4:5], oh[0:1, 4:5], oh[0:1, 7:8], ALU.mult)
                    v.tensor_scalar(oh[0:1, 6:7], oh[0:1, 4:5], -1.0, 1.0,
                                    ALU.mult, ALU.add)
                    v.tensor_tensor(oh[0:1, 7:8], oh[0:1, 7:8], oh[0:1, 6:7], ALU.mult)
                    last = v.tensor_tensor(oh[0:1, 5:6], oh[0:1, 5:6], oh[0:1, 7:8],
                                           ALU.mult)
                    cV.inc(last)
                    smm["oh"] = cV.n
                else:
                    for e in range(3):
                        v.tensor_tensor(sm[:, b + 11:b + 12], self.ohb[:, 3 + e:4 + e],
                                        sm[:, b + 10:b + 11], ALU.mult)
                        v.scalar_tensor_tensor(coef3[r][:, e:e + 1],
                                               self.ohb[:, e:e + 1],
                                               sm[:, b + 9:b + 10],
                                               sm[:, b + 11:b + 12],
                                               ALU.mult, ALU.add)
                    last = v.tensor_copy(sm[:, b + 11:b + 12], coef3[r][:, 0:1])
                    cV.inc(last)
                    smm["coef"] = cV.n
            self.V(_)

            if isG:
                def _(t):
                    t.wait_ge(cV.sem, smm["oh"])
                    last = t.matmul(cp[:, 0:6], self.ones_row[:, :],
                                    self.ohrow[0:1, 0:6],
                                    start=True, stop=True, skip_group_check=True)
                    cP.inc(last)
                    smm["ohmm"] = cP.n
                self.T(_)

                def _(v):
                    v.wait_ge(cP.sem, smm["ohmm"])
                    last = v.tensor_copy(self.ohb[:, 0:6], cp[:, 0:6])
                    cV.inc(last)
                self.V(_)
                cp_free = (cV, cV.n)
            else:
                def _(t, i=i):
                    t.wait_ge(cV.sem, smm["coef"])
                    c, v0 = cp_free
                    t.wait_ge(c.sem, v0)
                    last = t.transpose(cp[0:3, 0:128], coef3[r][:, 0:3], self.idf[:, :])
                    cP.inc(last)
                    smm["ctp"] = cP.n
                self.T(_)

                def _(v, i=i):
                    v.wait_ge(cP.sem, smm["ctp"])
                    last = v.tensor_copy(self.coefT[:, 128 * i:128 * (i + 1)],
                                         cp[0:3, 0:128])
                    cV.inc(last)
                self.V(_)
                cp_free = (cV, cV.n)

        xnt_ready = xnt_marks[-1]
        self.fence()
        fra.close()

        frb = contextlib.ExitStack()
        wqslab = [self.sb(f"p2_wq{r}", [128, NCH * 128], BF16, frb) for r in range(2)]
        qtmp = self.sb("p2_qtmp", [128, 1024], F32, frb)
        qtok = self.sb("p2_qtok", [128, 1024], F32, frb)
        qrot = self.sb("p2_qrot", [128, 1024], BF16, frb)
        qstage = [self.sb(f"p2_qs{r}", [128, 1024], BF16, frb) for r in range(2)]
        qa = [self.ps(f"p2qa{r}", [128, 512], F32, frb) for r in range(4)]
        ftq = [self.ps(f"p2ftq{r}", [128, 512], F32, frb) for r in range(2)]
        bfq = self.ps("p2bfq", [128, 1024], BF16, frb)

        slab_free = [(cP, 0), (cP, 0)]
        slab_ready = [0, 0]
        qa_free = [(cV, 0), (cV, 0)]
        qtmp_free = (cP, 0)
        ftq_free = [(cV, 0), (cV, 0)]
        qtok_free = (cV, 0)
        qrot_free = (cP, 0)
        bfq_free = (cV, 0)
        qs_free = [(self.sQS[0], 0), (self.sQS[1], 0)]
        pending = None

        def emit_qchain(job):
            nonlocal qtmp_free, qtok_free, qrot_free, bfq_free
            h, em = job
            p = h % 2

            def _(v):
                nonlocal qtmp_free
                v.wait_ge(cP.sem, em["mm"])
                c, v0 = qtmp_free
                v.wait_ge(c.sem, v0)
                v.tensor_copy(qtmp[:, 0:512], qa[2 * p][:, :])
                last = v.tensor_copy(qtmp[:, 512:1024], qa[2 * p + 1][:, :])
                cV.inc(last)
                em["qtmp"] = cV.n
            self.V(_)
            qa_free[p] = (cV, cV.n)

            def _(t):
                t.wait_ge(cV.sem, em["qtmp"])
                for half in range(2):
                    c, v0 = ftq_free[half]
                    t.wait_ge(c.sem, v0)
                    for u in range(4):
                        last = t.transpose(ftq[half][:, 128 * u:128 * (u + 1)],
                                           qtmp[:, 512 * half + 128 * u:
                                                512 * half + 128 * (u + 1)],
                                           self.idf[:, :])
                    cP.inc(last)
                    em[("ftp", half)] = cP.n
            self.T(_)
            qtmp_free0 = (cP, cP.n)

            def _(v):
                nonlocal qtok_free, qtmp_free, qrot_free
                c, v0 = qtok_free
                v.wait_ge(c.sem, v0)
                for half in range(2):
                    v.wait_ge(cP.sem, em[("ftp", half)])
                    last = v.tensor_copy(qtok[:, 512 * half:512 * (half + 1)],
                                         ftq[half][:, :])
                    cV.inc(last)
                    ftq_free[half] = (cV, cV.n)
                v.wait_ge(cC.sem, self.w_tabs)
                c, v0 = qrot_free
                v.wait_ge(c.sem, v0)
                qe = qtok[:, 0:1024:2]
                qo = qtok[:, 1:1024:2]
                v.tensor_tensor(qtmp[:, 0:512], qe, self.ctabo[:, :], ALU.mult)
                v.tensor_tensor(qtmp[:, 512:1024], qo, self.stabo[:, :], ALU.mult)
                v.tensor_tensor(qrot[:, 0:1024:2], qtmp[:, 0:512], qtmp[:, 512:1024],
                                ALU.subtract)
                v.tensor_tensor(qtmp[:, 0:512], qe, self.stabo[:, :], ALU.mult)
                v.tensor_tensor(qtmp[:, 512:1024], qo, self.ctabo[:, :], ALU.mult)
                last = v.tensor_tensor(qrot[:, 1:1024:2], qtmp[:, 0:512],
                                       qtmp[:, 512:1024], ALU.add)
                cV.inc(last)
                em["rope"] = cV.n
                qtok_free = (cV, cV.n)
                qtmp_free = (cV, cV.n)
            self.V(_)

            def _(t):
                t.wait_ge(cV.sem, em["rope"])
                c, v0 = bfq_free
                t.wait_ge(c.sem, v0)
                for u in range(8):
                    last = t.transpose(bfq[:, 128 * u:128 * (u + 1)],
                                       qrot[:, 128 * u:128 * (u + 1)], self.idb[:, :])
                cP.inc(last)
                em["bft"] = cP.n
            self.T(_)
            qrot_free = (cP, cP.n)

            def _(v):
                nonlocal bfq_free
                v.wait_ge(cP.sem, em["bft"])
                c, v0 = qs_free[p]
                v.wait_ge(c.sem, v0)
                last = v.tensor_copy(qstage[p][:, :], bfq[:, :])
                cV.inc(last)
                em["qs"] = cV.n
                bfq_free = (cV, cV.n)
            self.V(_)

            def _(sy):
                sy.wait_ge(cV.sem, em["qs"])
                last = sy.dma_start(out=self.q_hbm[:, 1024 * h:1024 * (h + 1)],
                                    in_=qstage[p][:, :])
                self.sQS[p].inc(last)
                qs_free[p] = (self.sQS[p], self.sQS[p].n)
            self.S(_)

        for h in range(NH):
            r = h % 2
            p = h % 2

            def _(gp2, h=h, r=r):
                c, v0 = slab_free[r]
                gp2.wait_ge(c.sem, v0)
                last = gp2.dma_start(
                    out=wqslab[r][:, :].rearrange("p (c n) -> p c n", c=NCH),
                    in_=self.wq[:, 128 * h:128 * (h + 1)].rearrange(
                        "(c p) n -> p c n", p=128))
                self.sWS[r].inc(last)
                slab_ready[r] = self.sWS[r].n
            self.G(_)

            hm = {}

            def _(t, h=h, r=r, p=p, hm=hm, xr=xnt_ready):
                t.wait_ge(self.sWS[r].sem, slab_ready[r])
                t.wait_ge(cC.sem, xr)
                c, v0 = qa_free[p]
                t.wait_ge(c.sem, v0)
                for ch in range(NCH):
                    t.matmul(qa[2 * p][:, :], wqslab[r][:, 128 * ch:128 * (ch + 1)],
                             xnt_own[:, 1024 * ch:1024 * ch + 512],
                             start=(ch == 0), stop=(ch == NCH - 1),
                             skip_group_check=True)
                    last = t.matmul(qa[2 * p + 1][:, :],
                                    wqslab[r][:, 128 * ch:128 * (ch + 1)],
                                    xnt_own[:, 1024 * ch + 512:1024 * ch + 1024],
                                    start=(ch == 0), stop=(ch == NCH - 1),
                                    skip_group_check=True)
                cP.inc(last)
                hm["mm"] = cP.n
                slab_free[r] = (cP, cP.n)
            self.T(_)

            if pending is not None:
                emit_qchain(pending)
            pending = (h, hm)
        emit_qchain(pending)
        pending = None

        def _(gp2):
            gp2.wait_ge(cP.sem, cP.n)
            last = gp2.dma_start(out=self.xnt_hbm[:, :], in_=xnt_own[:, :])
            self.sXnt.inc(last)
        self.G(_)
        if self.debug >= 2:
            def _(sy):
                sy.wait_ge(self.sXnt.sem, self.sXnt.n)
                for c in (self.sQS[0], self.sQS[1]):
                    sy.wait_ge(c.sem, c.n)
                self.sDbg.inc(sy.dma_start(out=self.dbg["xnt"][:, :], in_=xnt_own[:, :]))
                self.sDbg.inc(sy.dma_start(out=self.dbg["qh"][:, :], in_=self.q_hbm[:, :]))
            self.S(_)
        if self.debug >= 1:
            def _(sy):
                sy.wait_ge(cV.sem, cV.n)
                self.sDbg.inc(sy.dma_start(out=self.dbg["misc"][:, 0:8], in_=self.ohb[:, :]))
                self.sDbg.inc(sy.dma_start(out=self.dbg["misc"][0:1, 8:16],
                                           in_=self.ratio[0:1, :]))
                self.sDbg.inc(sy.dma_start(out=self.dbg["misc"][0:3, 16:32],
                                           in_=self.coefT[:, 0:16]))
            self.S(_)
        if self.debug >= 2:
            def _(sy):
                sy.wait_ge(cV.sem, cV.n)
                for i in range(8):
                    self.sDbg.inc(sy.dma_start(out=self.dbg["coef"][0:3, 4 * i:4 * i + 4],
                                               in_=self.coefT[:, 128 * i:128 * i + 4]))
            self.S(_)
        self.fence()
        frb.close()
        fr2.close()

    def _p3(self):
        cP, cV, cC = self.cP, self.cV, self.cC
        fr = contextlib.ExitStack()
        masks = self.sb("p3_masks", [128, 16 * 512], BF16, fr)
        qbuf = [self.sb(f"p3_q{r}", [128, 1024], BF16, fr) for r in range(3)]
        probs = self.sb("p3_probs", [128, 24 * 512], BF16, fr)
        denrow = self.sb("p3_den", [1, 1024], F32, fr)
        rbb = [self.sb(f"p3_rb{r}", [128, 512], F32, fr) for r in range(2)]
        cstage = [self.sb(f"p3_cs{r}", [128, 512], BF16, fr) for r in range(2)]
        sbk = self.ps("p3s", [128, 2048], F32, fr)
        cxb = [self.ps(f"p3c{r}", [128, 512], F32, fr) for r in range(2)]
        dnb = [self.ps(f"p3d{r}", [128, 512], F32, fr) for r in range(2)]

        def _(sy):
            last = sy.dma_start(out=masks[:, :].rearrange("p (j n) -> p j n", j=16),
                                in_=self.masksin[:, :, :].rearrange("j p n -> p j n"))
            self.sMask.inc(last)
        self.S(_)
        masks_ready = self.sMask.n

        qb_free = [(cP, 0)] * 3
        qb_ready = [0, 0, 0]
        s_free = [(cC, 0)] * 4
        probs_free = [(cP, 0)] * 24
        cx_free = [(cV, 0), (cV, 0)]
        dn_free = [(cV, 0), (cV, 0)]
        rb_free = (cV, 0)
        rbb_free = [(cV, 0), (cV, 0)]
        cs_free = [(self.sCS[0], 0), (self.sCS[1], 0)]

        pending = None

        def emit_B(job):
            h, qb, kmax, rr, g, pready, em = job
            slot0 = 16 if qb == 0 else 0

            def _(t):
                best = {}
                for c, v0 in pready:
                    best[id(c)] = (c, max(v0, best.get(id(c), (c, 0))[1]))
                for c, v0 in best.values():
                    t.wait_ge(c.sem, v0)
                c, v0 = cx_free[rr]
                t.wait_ge(c.sem, v0)
                c, v0 = dn_free[rr]
                t.wait_ge(c.sem, v0)
                for kt in range(kmax):
                    sl = slot0 + kt
                    t.matmul(cxb[rr][:, :],
                             self.vtok[:, 1024 * kt + 128 * g:
                                       1024 * kt + 128 * (g + 1)],
                             probs[:, 512 * sl:512 * (sl + 1)],
                             start=(kt == 0), stop=(kt == kmax - 1),
                             skip_group_check=True)
                    last = t.matmul(dnb[rr][0:1, :], self.ones_colb[:, :],
                                    probs[:, 512 * sl:512 * (sl + 1)],
                                    start=(kt == 0), stop=(kt == kmax - 1),
                                    skip_group_check=True)
                    cP.inc(last)
                    probs_free[sl] = (cP, cP.n)
                em["cd"] = cP.n
            self.T(_)

            def _(v):
                v.wait_ge(cP.sem, em["cd"])
                v.tensor_copy(denrow[0:1, 512 * rr:512 * (rr + 1)],
                              dnb[rr][0:1, :])
                last = v.reciprocal(denrow[0:1, 512 * rr:512 * (rr + 1)],
                                    denrow[0:1, 512 * rr:512 * (rr + 1)])
                cV.inc(last)
                em["rec"] = cV.n
            self.V(_)

            def _(t):
                t.wait_ge(cV.sem, em["rec"])
                last = t.matmul(dnb[rr][:, :], self.ones_row[:, :],
                                denrow[0:1, 512 * rr:512 * (rr + 1)],
                                start=True, stop=True, skip_group_check=True)
                cP.inc(last)
                em["rbmm"] = cP.n
            self.T(_)

            def _(v):
                v.wait_ge(cP.sem, em["rbmm"])
                c, v0 = rbb_free[rr]
                v.wait_ge(c.sem, v0)
                v.tensor_copy(rbb[rr][:, :], dnb[rr][:, :])
                c, v0 = cs_free[rr]
                v.wait_ge(c.sem, v0)
                last = v.tensor_tensor(cstage[rr][:, :], cxb[rr][:, :],
                                       rbb[rr][:, :], ALU.mult)
                cV.inc(last)
                em["cs"] = cV.n
            self.V(_)
            dn_free[rr] = (cV, cV.n)
            cx_free[rr] = (cV, cV.n)
            rbb_free[rr] = (cV, cV.n)

            def _(sy):
                sy.wait_ge(cV.sem, em["cs"])
                dst = self.ctx_hbm[:, :].rearrange("p (t n) -> p t n", t=8)[
                    :, 4 * qb:4 * qb + 4, 128 * h:128 * (h + 1)]
                src2 = cstage[rr][:, :].rearrange("p (u n) -> p u n", u=4)
                last = sy.dma_start(out=dst, in_=src2)
                self.sCS[rr].inc(last)
                cs_free[rr] = (self.sCS[rr], self.sCS[rr].n)
            self.S(_)

        for h in range(NH):
            qr = h % 3
            g = h // 4

            def _(gq, h=h, qr=qr):
                c, v0 = qb_free[qr]
                gq.wait_ge(c.sem, v0)
                gq.wait_ge(self.sXnt.sem, self.sXnt.n)
                last = gq.dma_start(out=qbuf[qr][:, :],
                                    in_=self.q_hbm[:, 1024 * h:1024 * (h + 1)])
                self.sQB[qr].inc(last)
                qb_ready[qr] = self.sQB[qr].n
            self.G(_)

            for qb in range(2):
                kmax = 8 if qb == 0 else 16
                rr = (2 * h + qb) % 2
                slot0 = 16 if qb == 0 else 0
                expM = [cC.n + (j // 4) + 1 for j in range(kmax)]
                sA = {}

                def _(t, h=h, qb=qb, kmax=kmax, g=g, qr=qr, expM=expM, sA=sA):
                    t.wait_ge(self.sQB[qr].sem, qb_ready[qr])
                    c, v0 = max((cv for cv in s_free), key=lambda cv: cv[1])
                    t.wait_ge(c.sem, v0)
                    lastw = None
                    for kt in range(kmax):
                        sb_ = kt % 4
                        if kt >= 4 and expM[kt - 4] != lastw:
                            t.wait_ge(cC.sem, expM[kt - 4])
                            lastw = expM[kt - 4]
                        last = t.matmul(
                            sbk[:, 512 * sb_:512 * (sb_ + 1)],
                            self.kfeat[:, 2048 * g + 128 * kt:2048 * g + 128 * (kt + 1)],
                            qbuf[qr][:, 512 * qb:512 * (qb + 1)],
                            start=True, stop=True, skip_group_check=True)
                        cP.inc(last)
                        sA[kt] = cP.n
                self.T(_)

                def _(sc, qb=qb, kmax=kmax, sA=sA, slot0=slot0):
                    for j in range(kmax // 4):
                        k0 = 4 * j
                        sc.wait_ge(cP.sem, sA[k0 + 3])
                        for kk in range(k0, k0 + 4):
                            c, v0 = probs_free[slot0 + kk]
                            sc.wait_ge(c.sem, v0)
                        sl = slot0 + k0
                        sc.activation(probs[:, 512 * sl:512 * (sl + 4)],
                                      sbk[:, :],
                                      AF.Exp, bias=self.negb_col[:, 0:1],
                                      scale=SCALE)
                self.C(_)
                assert cC.n == expM[-1], (cC.n, expM[-1])
                for kt in range(kmax):
                    s_free[kt % 4] = (cC, expM[kt])
                pready = [(cC, expM[kt]) for kt in range(kmax)]

                def _(v, qb=qb, kmax=kmax, mr=masks_ready, expM=expM,
                      pready=pready, slot0=slot0):
                    first = True
                    for kt in range(kmax):
                        if (qb == 0) or (kt >= 8):
                            v.wait_ge(cC.sem, expM[kt])
                            if first:
                                v.wait_ge(self.sMask.sem, mr)
                                first = False
                            sl = slot0 + kt
                            v.tensor_tensor(probs[:, 512 * sl:512 * (sl + 1)],
                                            probs[:, 512 * sl:512 * (sl + 1)],
                                            masks[:, 512 * kt:512 * (kt + 1)],
                                            ALU.mult)
                            pready[kt] = (cV, cV.n)
                self.V(_)

                if pending is not None:
                    emit_B(pending)
                pending = (h, qb, kmax, rr, g, pready, {})
            qb_free[qr] = (cP, cP.n)
        emit_B(pending)

        if self.debug >= 3:
            def _(sy):
                for c in self.sCS:
                    sy.wait_ge(c.sem, c.n)
                self.sDbg.inc(sy.dma_start(out=self.dbg["ctx"][:, :],
                                           in_=self.ctx_hbm[:, :]))
            self.S(_)
        self.fence()
        fr.close()

    def _p4(self):
        cP, cV, cC = self.cP, self.cV, self.cC
        fr = contextlib.ExitStack()
        ctxbuf = self.sb("p4_ctx", [128, 8 * D], BF16, fr)
        woslab = [self.sb(f"p4_wo{r}", [128, NCH * 512], BF16, fr) for r in range(2)]
        self.b2 = self.sb("p4_b2", [3, D], F32, fr)
        xcol = [self.sb(f"p4_x{r}", [128, 512], F32, fr) for r in range(3)]
        outsb = [self.sb(f"p4_o{r}", [128, 512], F32, fr) for r in range(2)]
        ob = [self.ps(f"p4b{r}", [128, 512], F32, fr) for r in range(4)]

        def _(sy):
            self.sMask.inc(sy.dma_start(out=self.b2[:, :], in_=self.b2in[:, :]))
            for c in self.sCS:
                sy.wait_ge(c.sem, c.n)
            self.sMask.inc(sy.dma_start(out=ctxbuf[:, :], in_=self.ctx_hbm[:, :]))
        self.S(_)
        b2_ready = self.sMask.n

        slab_free = [(cP, 0), (cP, 0)]
        slab_ready = [0, 0]
        x_free = [(cV, 0)] * 3
        x_ready = [0] * 3
        o_free = [(cV, 0)] * 4
        os_free = [(self.sOS[0], 0), (self.sOS[1], 0)]

        jobs = [(db, tau) for db in range(8) for tau in range(8)]
        for j, (db, tau) in enumerate(jobs):
            r = db % 2
            cr = j % 3
            bk = j % 4
            if tau == 0:
                def _(gp2, db=db, r=r):
                    c, v0 = slab_free[r]
                    gp2.wait_ge(c.sem, v0)
                    last = gp2.dma_start(
                        out=woslab[r][:, :].rearrange("p (c n) -> p c n", c=NCH),
                        in_=self.wo[:, 512 * db:512 * (db + 1)].rearrange(
                            "(c p) n -> p c n", p=128))
                    self.sWS[r].inc(last)
                    slab_ready[r] = self.sWS[r].n
                self.G(_)

            def _(sy, db=db, tau=tau, cr=cr):
                c, v0 = x_free[cr]
                sy.wait_ge(c.sem, v0)
                last = sy.dma_start(out=xcol[cr][:, :],
                                    in_=self.xo[128 * tau:128 * (tau + 1),
                                                512 * db:512 * (db + 1)])
                self.sX[cr].inc(last)
                x_ready[cr] = self.sX[cr].n
            self.S(_)

            em = {}

            def _(t, db=db, tau=tau, cr=cr, r=r, bk=bk, em=em):
                t.wait_ge(self.sMask.sem, b2_ready)
                t.wait_ge(self.sWS[r].sem, slab_ready[r])
                c, v0 = o_free[bk]
                t.wait_ge(c.sem, v0)
                for hh in range(NCH):
                    t.matmul(ob[bk][:, :],
                             ctxbuf[:, D * tau + 128 * hh:D * tau + 128 * (hh + 1)],
                             woslab[r][:, 512 * hh:512 * (hh + 1)],
                             start=(hh == 0), stop=False, skip_group_check=True)
                last = t.matmul(ob[bk][:, :], self.coefT[:, 128 * tau:128 * (tau + 1)],
                                self.b2[:, 512 * db:512 * (db + 1)],
                                start=False, stop=True, skip_group_check=True)
                cP.inc(last)
                em["mm"] = cP.n
                slab_free[r] = (cP, cP.n)
            self.T(_)

            def _(v, cr=cr, bk=bk, em=em):
                v.wait_ge(cP.sem, em["mm"])
                v.wait_ge(self.sX[cr].sem, x_ready[cr])
                c, v0 = os_free[j % 2]
                v.wait_ge(c.sem, v0)
                last = v.tensor_tensor(outsb[j % 2][:, :], ob[bk][:, :],
                                       xcol[cr][:, :], ALU.add)
                cV.inc(last)
                em["add"] = cV.n
            self.V(_)
            o_free[bk] = (cV, cV.n)
            x_free[cr] = (cV, cV.n)

            def _(sc, db=db, tau=tau, em=em):
                sc.wait_ge(cV.sem, em["add"])
                last = sc.dma_start(out=self.scr[128 * tau:128 * (tau + 1),
                                                 512 * db:512 * (db + 1)],
                                    in_=outsb[j % 2][:, :])
                self.sOS[j % 2].inc(last)
                os_free[j % 2] = (self.sOS[j % 2], self.sOS[j % 2].n)
            self.C(_)
        if self.debug >= 4:
            def _(sy):
                for c in self.sOS:
                    sy.wait_ge(c.sem, c.n)
                self.sDbg.inc(sy.dma_start(out=self.dbg["scr"][:, :],
                                           in_=self.scr[:, :]))
            self.S(_)
        self.fence()
        fr.close()

    def _p5(self):
        cP, cV, cC = self.cP, self.cV, self.cC
        fr = contextlib.ExitStack()
        act_sb = self.sb("p5_act", [128, NCH * TOWN], BF16, fr)
        coefb = self.sb("p5_coefb", [128, 3 * TOWN], BF16, fr)
        tnf = [self.sb(f"p5_tn{r}", [128, 512], F32, fr) for r in range(4)]
        tn_free = [(cV, 0)] * 4

        cbp = contextlib.ExitStack()
        cb = [self.ps(f"p5cb{r}", [128, 512], F32, cbp) for r in range(2)]
        em0 = {}
        cb_free = (cC, 0)
        for e in range(3):
            def _(t, e=e, cbf=cb_free):
                c, v0 = cbf
                t.wait_ge(c.sem, v0)
                for tb in range(2):
                    last = t.matmul(cb[tb][:, :], self.sel3[:, 128 * e:128 * (e + 1)],
                                    self.coefT[0:3, 512 * tb:512 * (tb + 1)],
                                    start=True, stop=True, skip_group_check=True)
                cP.inc(last)
                em0[e] = cP.n
            self.T(_)

            def _(sc, e=e):
                sc.wait_ge(cP.sem, em0[e])
                for tb in range(2):
                    last = sc.activation(
                        coefb[:, 1024 * e + 512 * tb:1024 * e + 512 * (tb + 1)],
                        cb[tb][:, :], AF.Identity)
                cC.inc(last)
            self.C(_)
            cb_free = (cC, cC.n)
        coefb_ready = self.cC.n
        self.fence()
        cbp.close()

        act_free = (cP, 0)
        for e in range(3):
            frp = contextlib.ExitStack()
            xnt5 = self.sb(f"p5_xnt{e}", [128, NCH * TOWN], BF16, frp)
            w1slab = [self.sb(f"p5_w1{e}_{r}", [128, NCH * 256], BF16, frp)
                      for r in range(2)]
            pp = [self.ps(f"p5p{e}_{r}", [128, 512], F32, frp) for r in range(8)]

            def _(gp2):
                gp2.wait_ge(self.sXnt.sem, self.sXnt.n)
                last = gp2.dma_start(out=xnt5[:, :], in_=self.xnt_hbm[:, :])
                self.sXnt.inc(last)
            self.G(_)
            xnt5_ready = self.sXnt.n

            slab_free = [(cP, 0), (cP, 0)]
            slab_ready = [0, 0]
            pp_free = [(cV, 0)] * 8
            for mg in range(16):
                r = mg % 2

                def _(gp2, e=e, mg=mg, r=r):
                    c, v0 = slab_free[r]
                    gp2.wait_ge(c.sem, v0)
                    last = gp2.dma_start(
                        out=w1slab[r][:, :].rearrange("p (c n) -> p c n", c=NCH),
                        in_=self.w1[e][:, 256 * mg:256 * (mg + 1)].rearrange(
                            "(c p) n -> p c n", p=128))
                    self.sWS[r].inc(last)
                    slab_ready[r] = self.sWS[r].n
                self.G(_)

                em = {}

                def _(t, mg=mg, r=r, em=em, xr=xnt5_ready):
                    t.wait_ge(self.sWS[r].sem, slab_ready[r])
                    t.wait_ge(self.sXnt.sem, xr)
                    base = 4 * (mg % 2)
                    for q in range(4):
                        c, v0 = pp_free[base + q]
                        t.wait_ge(c.sem, v0)
                    for ch in range(NCH):
                        for jj in range(2):
                            for tb in range(2):
                                last = t.matmul(
                                    pp[base + 2 * jj + tb][:, :],
                                    w1slab[r][:, 256 * ch + 128 * jj:
                                              256 * ch + 128 * (jj + 1)],
                                    xnt5[:, 1024 * ch + 512 * tb:
                                         1024 * ch + 512 * (tb + 1)],
                                    start=(ch == 0), stop=(ch == NCH - 1),
                                    skip_group_check=True)
                    cP.inc(last)
                    em["mm"] = cP.n
                    slab_free[r] = (cP, cP.n)
                self.T(_)

                emx = {}

                def _(sc, e=e, mg=mg, em=em, emx=emx):
                    sc.wait_ge(cP.sem, em["mm"])
                    afunc = [AF.Tanh, AF.Gelu, AF.Silu][e]
                    base = 4 * (mg % 2)
                    for q in range(4):
                        jj, tb = q // 2, q % 2
                        fc = 2 * mg + jj
                        c, v0 = tn_free[q]
                        sc.wait_ge(c.sem, v0)
                        last = sc.activation(tnf[q][:, :], pp[base + q][:, :],
                                             afunc,
                                             bias=self.b1t[:, 32 * e + fc:
                                                           32 * e + fc + 1])
                        cC.inc(last)
                        emx[q] = cC.n
                self.C(_)
                base = 4 * (mg % 2)
                for q in range(4):
                    pp_free[base + q] = (cC, emx[q])

                def _(v, e=e, mg=mg, emx=emx, afm=act_free):
                    c, v0 = afm
                    v.wait_ge(c.sem, v0)
                    for q in range(4):
                        jj, tb = q // 2, q % 2
                        fc = 2 * mg + jj
                        v.wait_ge(cC.sem, emx[q])
                        dst = act_sb[:, 1024 * fc + 512 * tb:
                                     1024 * fc + 512 * (tb + 1)]
                        cslice = coefb[:, 1024 * e + 512 * tb:
                                       1024 * e + 512 * (tb + 1)]
                        last = v.tensor_tensor(dst, tnf[q][:, :], cslice, ALU.mult)
                        cV.inc(last)
                        tn_free[q] = (cV, cV.n)
                self.V(_)
            act_ready = cV.n
            self.fence()
            frp.close()

            fro = contextlib.ExitStack()
            w2slab = [self.sb(f"p5_w2{e}_{r}", [128, NCH * 512], BF16, fro)
                      for r in range(2)]
            scrin = [self.sb(f"p5_si{e}_{r}", [128, 512], F32, fro) for r in range(2)]
            outsb = [self.sb(f"p5_ou{e}_{r}", [128, 512], F32, fro) for r in range(2)]
            ob = [self.ps(f"p5o{e}_{r}", [128, 512], F32, fro) for r in range(4)]

            slab_free = [(cP, 0), (cP, 0)]
            slab_ready = [0, 0]
            o_free = [(cV, 0)] * 4
            si_free = [(cV, 0), (cV, 0)]
            si_ready = [0, 0]
            os_free = [(self.sOS[0], self.sOS[0].n), (self.sOS[1], self.sOS[1].n)]
            dstT = self.scr if e < 2 else self.out

            for db in range(8):
                r = db % 2

                def _(gp2, e=e, db=db, r=r):
                    c, v0 = slab_free[r]
                    gp2.wait_ge(c.sem, v0)
                    last = gp2.dma_start(
                        out=w2slab[r][:, :].rearrange("p (c n) -> p c n", c=NCH),
                        in_=self.w2[e][:, 512 * db:512 * (db + 1)].rearrange(
                            "(c p) n -> p c n", p=128))
                    self.sWS[r].inc(last)
                    slab_ready[r] = self.sWS[r].n
                self.G(_)

                for tau in range(8):
                    cr = tau % 2
                    bk = (8 * db + tau) % 4
                    em = {}

                    def _(sy, db=db, tau=tau, cr=cr):
                        c, v0 = si_free[cr]
                        sy.wait_ge(c.sem, v0)
                        last = sy.dma_start(out=scrin[cr][:, :],
                                            in_=self.scr[128 * tau:128 * (tau + 1),
                                                         512 * db:512 * (db + 1)])
                        self.sSI[cr].inc(last)
                        si_ready[cr] = self.sSI[cr].n
                    self.S(_)

                    def _(t, tau=tau, r=r, bk=bk, em=em, ar=act_ready):
                        t.wait_ge(self.sWS[r].sem, slab_ready[r])
                        t.wait_ge(cV.sem, ar)
                        c, v0 = o_free[bk]
                        t.wait_ge(c.sem, v0)
                        for fc in range(NCH):
                            last = t.matmul(
                                ob[bk][:, :],
                                act_sb[:, 1024 * fc + 128 * tau:
                                       1024 * fc + 128 * (tau + 1)],
                                w2slab[r][:, 512 * fc:512 * (fc + 1)],
                                start=(fc == 0), stop=(fc == NCH - 1),
                                skip_group_check=True)
                        cP.inc(last)
                        em["mm"] = cP.n
                        slab_free[r] = (cP, cP.n)
                    self.T(_)

                    def _(v, cr=cr, bk=bk, em=em):
                        v.wait_ge(cP.sem, em["mm"])
                        v.wait_ge(self.sSI[cr].sem, si_ready[cr])
                        c, v0 = os_free[cr]
                        v.wait_ge(c.sem, v0)
                        last = v.tensor_tensor(outsb[cr][:, :], ob[bk][:, :],
                                               scrin[cr][:, :], ALU.add)
                        cV.inc(last)
                        em["add"] = cV.n
                    self.V(_)
                    o_free[bk] = (cV, cV.n)
                    si_free[cr] = (cV, cV.n)

                    def _(sc, db=db, tau=tau, cr=cr, em=em, dstT=dstT):
                        sc.wait_ge(cV.sem, em["add"])
                        last = sc.dma_start(out=dstT[128 * tau:128 * (tau + 1),
                                                     512 * db:512 * (db + 1)],
                                            in_=outsb[cr][:, :])
                        self.sOS[cr].inc(last)
                        os_free[cr] = (self.sOS[cr], self.sOS[cr].n)
                    self.C(_)
            act_free = (cP, cP.n)
            self.fence()
            fro.close()
        fr.close()

    def _finish(self):
        def _(sy):
            for c in self._alldsems:
                if c.n > 0:
                    sy.wait_ge(c.sem, c.n)
        self.S(_)


_CACHE = {}


def get_prog(debug=0, phases=5, **kw):
    key = (debug, phases, tuple(sorted(kw.items())))
    if key not in _CACHE:
        _CACHE[key] = Prog(debug, phases, **kw).build()
    return _CACHE[key]


def host_inputs(inputs, core):
    beta, par = core // 2, core % 2
    x = np.ascontiguousarray(np.asarray(inputs["L0_attn_x"], dtype=np.float32)
                             .reshape(4, S, D)[beta])
    rows = np.concatenate([np.arange(128 * (2 * i + par), 128 * (2 * i + par) + 128)
                           for i in range(8)])
    xo = np.ascontiguousarray(x[rows])
    gw = np.asarray(inputs["L0_ffn_gate_w"], dtype=np.float32)
    gwpc = np.ascontiguousarray(
        gw[:4096].reshape(32, 128, 3).transpose(1, 0, 2).reshape(128, 96))
    b1 = np.asarray(inputs["L0_ffn_b1"], dtype=np.float32)
    b1pc = np.ascontiguousarray(
        b1.reshape(3, 32, 128).transpose(2, 0, 1).reshape(128, 96))
    masks = np.zeros((16, 128, 512), dtype=np.float32)
    for j in range(16):
        qb = 0 if j < 8 else 1
        kt = j
        for sl in range(4):
            g = 2 * (4 * qb + sl) + par
            qpos = 128 * g + np.arange(128)[None, :]
            kpos = 128 * kt + np.arange(128)[:, None]
            masks[j][:, 128 * sl:128 * (sl + 1)] = (kpos <= qpos)
    selvt = np.zeros((128, 32), dtype=np.float32)
    for t in range(16):
        tok = 128 * t + np.arange(128)
        selvt[:, t] = (tok < VISLEN)
        selvt[:, 16 + t] = (tok >= VISLEN)
    return {
        "xf": x, "xo": xo,
        "wq": np.asarray(inputs["L0_attn_wq"], dtype=np.float32),
        "wk": np.asarray(inputs["L0_attn_wk"], dtype=np.float32),
        "wv": np.asarray(inputs["L0_attn_wv"], dtype=np.float32),
        "wo": np.asarray(inputs["L0_attn_wo"], dtype=np.float32),
        "w1": np.asarray(inputs["L0_ffn_w1"], dtype=np.float32),
        "w2": np.asarray(inputs["L0_ffn_w2"], dtype=np.float32),
        "gwpc": gwpc,
        "gwlast": np.ascontiguousarray(gw[4096:4097]),
        "gbin": np.asarray(inputs["L0_ffn_gate_b"], dtype=np.float32).reshape(1, 3),
        "b1pc": b1pc,
        "b2in": np.asarray(inputs["L0_ffn_b2"], dtype=np.float32),
        "masksin": masks.astype(ml_dtypes.bfloat16),
        "selvtin": selvt,
        "parcolin": np.full((128, 1), 128.0 * par, dtype=np.float32),
        "idfin": np.eye(128, dtype=np.float32),
        "idbin": np.eye(128, dtype=ml_dtypes.bfloat16),
    }


def kernel(**inputs):
    prog = get_prog(0)
    in_maps = [host_inputs(inputs, c) for c in range(8)]
    res = run_bass_kernel_spmd(prog.nc, in_maps, core_ids=list(range(8)))
    full = np.zeros((4, S, D), dtype=np.float32)
    for c in range(8):
        beta, par = c // 2, c % 2
        o = np.asarray(res.results[c]["out"]).reshape(8, 128, D)
        for i in range(8):
            full[beta, 128 * (2 * i + par):128 * (2 * i + par) + 128] = o[i]
    return full


# version 06 (final)
# speedup vs baseline: 1.0294x; 1.0092x over previous
import contextlib
import numpy as np
import ml_dtypes
import concourse.bass as bass
import concourse.mybir as mybir
from concourse.bass_utils import run_bass_kernel_spmd

F32 = mybir.dt.float32
BF16 = mybir.dt.bfloat16
I32 = mybir.dt.int32
AF = mybir.ActivationFunctionType
ALU = mybir.AluOpType
AX = mybir.AxisListType

D = 4096
S = 2048
TOWN = 1024
NCH = 32
NH = 32
NG = 8
SCALE = float(1.0 / np.sqrt(128.0))
EXPB = -30.0
VISLEN = 614
ROPE_BASE = 10000.0
TWO_PI = float(2.0 * np.pi)
INVSQRT2 = float(1.0 / np.sqrt(2.0))


class Ctr:
    def __init__(self, sem, step=1):
        self.sem = sem
        self.n = 0
        self.step = step
        self._last = None

    def inc(self, inst, k=1):
        if inst is self._last:
            return self.n
        inst.then_inc(self.sem, k * self.step)
        self._last = inst
        self.n += k * self.step
        return self.n


_SYNCED_OPS = {
    "tensor_tensor", "tensor_add", "tensor_sub", "tensor_mul", "tensor_max",
    "tensor_scalar", "tensor_single_scalar", "tensor_scalar_add",
    "tensor_scalar_sub", "tensor_scalar_mul", "tensor_scalar_max",
    "tensor_scalar_min", "scalar_tensor_tensor", "tensor_copy", "tensor_reduce",
    "reduce_sum", "reduce_max", "reciprocal", "memset", "memzero", "iota",
    "activation", "copy", "add", "mul", "square", "sqrt", "select",
    "tensor_relu", "affine_select", "sign",
}


class SEng:

    def __init__(self, eng, ctr):
        self._e = eng
        self._c = ctr

    def __getattr__(self, name):
        f = getattr(self._e, name)
        if name in _SYNCED_OPS:
            def wrapped(*a, **k):
                if self._c.n > 0:
                    self._e.wait_ge(self._c.sem, self._c.n)
                inst = f(*a, **k)
                self._c.inc(inst)
                return inst
            return wrapped
        return f


class Prog:
    def __init__(self, debug=0, phases=5, p1_quarters=4, p1_groups=16,
                 p1_tiles=4, p1_skip=""):
        self.phases = phases
        self.p1_quarters = p1_quarters
        self.p1_groups = p1_groups
        self.p1_tiles = p1_tiles
        self.p1_skip = set(p1_skip.split(",")) if p1_skip else set()
        self.debug = debug
        self.nc = bass.Bass("TRN2", target_bir_lowering=False)
        self.es = contextlib.ExitStack()
        self.dbg = {}

    def T(self, fn):
        self.blk.tensor(fn)

    def V(self, fn):
        self.blk.vector(lambda e: fn(SEng(e, self.cV)))

    def C(self, fn):
        self.blk.scalar(lambda e: fn(SEng(e, self.cC)))

    def G(self, fn):
        self.blk.gpsimd(lambda e: fn(SEng(e, self.cG)))

    def S(self, fn):
        self.blk.sync(fn)

    def fence(self):
        marks = [(c, c.n) for c in
                 ([self.cP, self.cV, self.cC, self.cG] + self._alldsems)
                 if c.n > 0]

        def _(e):
            for c, v in marks:
                e.wait_ge(c.sem, v)
        for emit in (self.T, self.V, self.C, self.S, self.G):
            emit(_)

    def sb(self, name, shape, dt, stack=None):
        return (stack or self.es).enter_context(self.nc.sbuf_tensor(name, shape, dt))

    def ps(self, name, shape, dt, stack):
        return stack.enter_context(self.nc.psum_tensor(name, shape, dt))

    def build(self):
        nc = self.nc

        def dram(name, shape, dt, kind):
            return nc.dram_tensor(name, shape, dt, kind=kind).ap()

        self.xf = dram("xf", [S, D], F32, "ExternalInput")
        self.xo = dram("xo", [TOWN, D], F32, "ExternalInput")
        self.wq = dram("wq", [NH * 128, D], F32, "ExternalInput")
        self.wk = dram("wk", [NG * 128, D], F32, "ExternalInput")
        self.wv = dram("wv", [NG * 128, D], F32, "ExternalInput")
        self.wo = dram("wo", [8 * 128, NCH * 512], F32, "ExternalInput")
        self.w1 = dram("w1", [3, 16 * 128, NCH * 256], F32, "ExternalInput")
        self.w2 = dram("w2", [3, 8 * 128, NCH * 512], F32, "ExternalInput")
        self.gwpc = dram("gwpc", [128, 96], F32, "ExternalInput")
        self.gwlast = dram("gwlast", [1, 3], F32, "ExternalInput")
        self.gbin = dram("gbin", [1, 3], F32, "ExternalInput")
        self.b1pc = dram("b1pc", [128, 96], F32, "ExternalInput")
        self.b2in = dram("b2in", [3, D], F32, "ExternalInput")
        self.masksin = dram("masksin", [16, 128, 512], BF16, "ExternalInput")
        self.selvtin = dram("selvtin", [128, 32], F32, "ExternalInput")
        self.parcolin = dram("parcolin", [128, 1], F32, "ExternalInput")
        self.idfin = dram("idfin", [128, 128], F32, "ExternalInput")
        self.idbin = dram("idbin", [128, 128], BF16, "ExternalInput")

        self.xnt_hbm = dram("xnt_hbm", [128, NCH * TOWN], BF16, "Internal")
        self.q_hbm = dram("q_hbm", [128, NH * TOWN], BF16, "Internal")
        self.ctx_hbm = dram("ctx_hbm", [128, 8 * D], BF16, "Internal")
        self.scr = dram("scr", [TOWN, D], F32, "Internal")
        self.out = dram("out", [TOWN, D], F32, "ExternalOutput")
        if self.debug >= 1:
            self.dbg["xntq"] = dram("d_xntq", [128, NCH * 512], BF16, "ExternalOutput")
            self.dbg["kfeat"] = dram("d_kfeat", [128, NG * S], BF16, "ExternalOutput")
            self.dbg["vtok"] = dram("d_vtok", [128, 16 * 1024], BF16, "ExternalOutput")
            self.dbg["misc"] = dram("d_misc", [128, 64], F32, "ExternalOutput")
        if self.debug >= 2:
            self.dbg["xnt"] = dram("d_xnt", [128, NCH * TOWN], BF16, "ExternalOutput")
            self.dbg["qh"] = dram("d_qh", [128, NH * TOWN], BF16, "ExternalOutput")
            self.dbg["coef"] = dram("d_coef", [128, 32], F32, "ExternalOutput")
        if self.debug >= 3:
            self.dbg["ctx"] = dram("d_ctx", [128, 8 * D], BF16, "ExternalOutput")
        if self.debug >= 4:
            self.dbg["scr"] = dram("d_scr", [TOWN, D], F32, "ExternalOutput")

        sems = {k: self.es.enter_context(nc.semaphore(k)) for k in
                ["sp", "sv", "sc", "sg"]}
        self.cP = Ctr(sems["sp"])
        self.cV = Ctr(sems["sv"])
        self.cC = Ctr(sems["sc"])
        self.cG = Ctr(sems["sg"])
        self._alldsems = []

        def dsem(name):
            c = Ctr(self.es.enter_context(nc.semaphore(name)), 16)
            self._alldsems.append(c)
            return c
        self.dsem = dsem
        self.sConst = dsem("dconst")
        self.sX = [dsem(f"dx{r}") for r in range(3)]
        self.sWS = [dsem(f"dws{r}") for r in range(2)]
        self.sQS = [dsem(f"dqs{r}") for r in range(2)]
        self.sQB = [dsem(f"dqb{r}") for r in range(3)]
        self.sCS = [dsem(f"dcs{r}") for r in range(2)]
        self.sCX = [dsem(f"dcx{r}") for r in range(2)]
        self.sOS = [dsem(f"dos{r}") for r in range(2)]
        self.sSI = [dsem(f"dsi{r}") for r in range(2)]
        self.sXnt = dsem("dxnt")
        self.sMask = dsem("dmask")
        self.sDbg = dsem("ddbg")

        sb = self.sb
        self.idf = sb("s_idf", [128, 128], F32)
        self.idb = sb("s_idb", [128, 128], BF16)
        self.gw = sb("s_gw", [128, 96], F32)
        self.b1t = sb("s_b1t", [128, 96], F32)
        self.selvt = sb("s_selvt", [128, 32], F32)
        self.parcol = sb("s_parcol", [128, 1], F32)
        self.gwl = sb("s_gwl", [1, 3], F32)
        self.gb = sb("s_gb", [1, 3], F32)
        self.ones_row = sb("s_ones_row", [1, 128], F32)
        self.ones_colf = sb("s_ones_colf", [128, 1], F32)
        self.ones_colb = sb("s_ones_colb", [128, 1], BF16)
        self.iota_col = sb("s_iota_col", [128, 1], F32)
        self.eps_col = sb("s_eps_col", [128, 1], F32)
        self.negb_col = sb("s_negb_col", [128, 1], F32)
        self.invfreq = sb("s_invfreq", [128, 64], F32)
        self.ctabf = sb("s_ctabf", [128, 1024], F32)
        self.stabf = sb("s_stabf", [128, 1024], F32)
        self.ctabo = sb("s_ctabo", [128, 512], F32)
        self.stabo = sb("s_stabo", [128, 512], F32)
        self.absstage = sb("s_absstage", [128, 16], F32)
        self.coefT = sb("s_coefT", [3, TOWN], F32)
        self.ohb = sb("s_ohb", [128, 8], F32)
        self.sel3 = sb("s_sel3", [3, 384], F32)
        self.ohrow = sb("s_ohrow", [1, 8], F32)
        self.ratio = sb("s_ratio", [1, 8], F32)
        self.gcorr = sb("s_gcorr", [1, 4], F32)
        self.stat = sb("s_stat", [128, 16], F32)

        self.blk = self.es.enter_context(nc.Block())
        self._init_consts()

        kvf = contextlib.ExitStack()
        self.kfeat = sb("s_kfeat", [128, NG * S], BF16, kvf)
        self.vtok = sb("s_vtok", [128, 16 * 1024], BF16, kvf)
        if self.phases >= 1:
            self._p1()
            self.fence()
        if self.phases >= 2:
            self._p2()
            self.fence()
        if self.phases >= 3:
            self._p3()
            self.fence()
        kvf.close()
        if self.phases >= 4:
            self._p4()
            self.fence()
        if self.phases >= 5:
            self._p5()
        self._finish()
        self.es.close()
        return self

    def _init_consts(self):
        cA = self.sConst
        cV, cC, cG = self.cV, self.cC, self.cG
        fr = contextlib.ExitStack()
        i32b = self.sb("i_i32b", [128, 1024], I32, fr)
        fb = self.sb("i_fb", [128, 1024], F32, fr)
        ub = self.sb("i_ub", [128, 1024], F32, fr)

        def _(sy):
            for dst, src in [(self.idf, self.idfin), (self.idb, self.idbin),
                             (self.gw, self.gwpc), (self.b1t, self.b1pc),
                             (self.selvt, self.selvtin),
                             (self.parcol, self.parcolin), (self.gwl, self.gwlast),
                             (self.gb, self.gbin)]:
                cA.inc(sy.dma_start(out=dst[:, :], in_=src[:, :]))
        self.S(_)
        w_const = cA.n

        def _(g):
            g.memset(self.ones_row[:, :], 1.0)
            g.memset(self.ones_colf[:, :], 1.0)
            g.memset(self.ones_colb[:, :], 1.0)
            g.memset(self.eps_col[:, :], 1e-5)
            g.memset(self.negb_col[:, :], EXPB)
            g.memset(self.ohb[:, :], 0.0)
            g.memset(self.ohrow[:, :], 0.0)
            g.memset(self.ratio[:, :], 0.0)
            g.iota(self.iota_col[:, :], [[0, 1]], base=0, channel_multiplier=1,
                   allow_small_or_imprecise_dtypes=True)
            g.iota(self.sel3[:, :], [[0, 384]], base=0, channel_multiplier=1,
                   allow_small_or_imprecise_dtypes=True)
            last = g.iota(self.invfreq[:, :], [[1, 64]], base=0, channel_multiplier=0,
                          allow_small_or_imprecise_dtypes=True)
            cG.inc(last)
        self.G(_)
        w_ms = cG.n

        def _(sc):
            sc.wait_ge(cG.sem, w_ms)
            last = sc.activation(self.invfreq[:, :], self.invfreq[:, :], AF.Exp,
                                 scale=float(-np.log(ROPE_BASE) / 64.0))
            cC.inc(last)
        self.C(_)

        def _(v):
            v.wait_ge(cG.sem, w_ms)
            for e in range(3):
                v.tensor_single_scalar(self.sel3[:, 128 * e:128 * (e + 1)],
                                       self.sel3[:, 128 * e:128 * (e + 1)],
                                       float(e), ALU.is_equal)
        self.V(_)
        w_if = cC.n
        waits = {}

        def gen_tab(name, ctab, stab, bases, use_par, nf):
            def _(v):
                v.wait_ge(cC.sem, w_if)
                v.wait_ge(cA.sem, w_const)
                for j, b in enumerate(bases):
                    v.tensor_scalar(fb[:, 1020:1021], self.iota_col[:, 0:1],
                                    float(b), None, ALU.add)
                    if use_par:
                        v.tensor_tensor(fb[:, 1020:1021], fb[:, 1020:1021],
                                        self.parcol[:, 0:1], ALU.add)
                    v.tensor_scalar(ub[:, 64 * j:64 * j + 64], self.invfreq[:, :],
                                    fb[:, 1020:1021], float(1.0 / TWO_PI),
                                    ALU.mult, ALU.mult)
                def frac_pm_half(dst):
                    v.tensor_copy(i32b[:, 0:nf], ub[:, 0:nf])
                    v.tensor_copy(fb[:, 0:nf], i32b[:, 0:nf])
                    v.tensor_tensor(dst[:, 0:nf], ub[:, 0:nf], fb[:, 0:nf],
                                    ALU.subtract)
                    v.tensor_single_scalar(fb[:, 0:nf], dst[:, 0:nf], 0.5,
                                           ALU.is_gt)
                    return v.tensor_tensor(dst[:, 0:nf], dst[:, 0:nf],
                                           fb[:, 0:nf], ALU.subtract)
                cV.inc(frac_pm_half(stab))
                waits[name + "s"] = cV.n
                v.tensor_scalar(ub[:, 0:nf], ub[:, 0:nf], 0.25, None, ALU.add)
                cV.inc(frac_pm_half(ctab))
                waits[name + "c"] = cV.n
            self.V(_)

            def _(sc):
                sc.wait_ge(cV.sem, waits[name + "s"])
                sc.activation(stab[:, 0:nf], stab[:, 0:nf], AF.Sin, scale=TWO_PI)
                sc.wait_ge(cV.sem, waits[name + "c"])
                last = sc.activation(ctab[:, 0:nf], ctab[:, 0:nf], AF.Sin, scale=TWO_PI)
                cC.inc(last)
            self.C(_)

        gen_tab("f", self.ctabf, self.stabf, [128 * t for t in range(16)], False, 1024)
        gen_tab("o", self.ctabo, self.stabo, [256 * t for t in range(8)], True, 512)
        self.w_tabs = cC.n
        self.w_const = w_const
        self.fence()
        fr.close()

    def _ln_tile(self, src_ap, r, xn_dst, abs_col, marks_out):
        cV, cC = self.cV, self.cC
        cA = self.sX[r]
        xbuf, st = self.xbuf, self.stat
        sc0 = 8 * r
        m = {}

        def _(sy):
            c, v0 = self.xb_free[r]
            sy.wait_ge(c.sem, v0)
            cA.inc(sy.dma_start(out=xbuf[r][:, :], in_=src_ap))
            self.xb_ready[r] = cA.n
        self.S(_)

        def _(v):
            v.wait_ge(cA.sem, self.xb_ready[r])
            c, v0 = self.stat_free[r]
            v.wait_ge(c.sem, v0)
            v.reduce_sum(st[:, sc0:sc0 + 1], xbuf[r][:, :], AX.X)
            v.tensor_scalar(st[:, sc0 + 1:sc0 + 2], st[:, sc0:sc0 + 1],
                            float(1.0 / D), None, ALU.mult)
            v.tensor_tensor(st[:, sc0 + 2:sc0 + 3], st[:, sc0 + 1:sc0 + 2],
                            st[:, sc0 + 1:sc0 + 2], ALU.mult)
            c, v0 = self.xn_free[r]
            v.wait_ge(c.sem, v0)
            c, v0 = self.absmark[r]
            if c is not None:
                v.wait_ge(c.sem, v0)
            v.scalar_tensor_tensor(xn_dst[:, :], xbuf[r][:, :], 1.0, xbuf[r][:, :],
                                   ALU.mult, ALU.mult,
                                   accum_out=st[:, sc0 + 3:sc0 + 4])
            last = v.scalar_tensor_tensor(st[:, sc0 + 4:sc0 + 5], st[:, sc0 + 3:sc0 + 4],
                                          float(1.0 / D), st[:, sc0 + 2:sc0 + 3],
                                          ALU.mult, ALU.subtract)
            cV.inc(last)
            m["var"] = cV.n
        self.V(_)

        def _(sc):
            sc.wait_ge(cV.sem, m["var"])
            sc.activation(st[:, sc0 + 5:sc0 + 6], st[:, sc0 + 4:sc0 + 5], AF.Ln,
                          bias=self.eps_col[:, 0:1])
            last = sc.activation(st[:, sc0 + 6:sc0 + 7], st[:, sc0 + 5:sc0 + 6],
                                 AF.Exp, scale=-0.5)
            cC.inc(last)
            m["rstd"] = cC.n
        self.C(_)

        def _(v):
            v.wait_ge(cC.sem, m["rstd"])
            v.scalar_tensor_tensor(st[:, sc0 + 7:sc0 + 8], st[:, sc0 + 1:sc0 + 2],
                                   -1.0, st[:, sc0 + 6:sc0 + 7],
                                   ALU.mult, ALU.mult)
            last = v.tensor_scalar(xn_dst[:, :], xbuf[r][:, :],
                                   st[:, sc0 + 6:sc0 + 7], st[:, sc0 + 7:sc0 + 8],
                                   ALU.mult, ALU.add)
            cV.inc(last)
            marks_out["xn"] = (cV, cV.n)
            self.xb_free[r] = (cV, cV.n)
            self.stat_free[r] = (cV, cV.n)
            if abs_col is not None:
                last = v.reduce_sum(self.absstage[:, abs_col:abs_col + 1],
                                    xn_dst[:, :], AX.X, apply_absolute_value=True)
                cV.inc(last)
                self.absmark[r] = (cV, cV.n)
        self.V(_)

    def _p1(self):
        cP, cV, cC = self.cP, self.cV, self.cC
        fr = contextlib.ExitStack()
        self.xbuf = [self.sb(f"p1_x{r}", [128, D], F32, fr) for r in range(2)]
        xnb = [self.sb(f"p1_xn{r}", [128, D], BF16, fr) for r in range(2)]
        xntq = self.sb("p1_xntq", [128, NCH * 512], BF16, fr)
        wslab = [self.sb(f"p1_w{r}", [128, NCH * 128], BF16, fr) for r in range(2)]
        ktmp = self.sb("p1_ktmp", [128, 512], F32, fr)
        ktok = self.sb("p1_ktok", [128, 512], F32, fr)
        krot = self.sb("p1_krot", [128, 512], BF16, fr)
        vtmp = self.sb("p1_vtmp", [128, 512], BF16, fr)
        rt1 = self.sb("p1_rt1", [128, 256], F32, fr)
        rt2 = self.sb("p1_rt2", [128, 256], F32, fr)
        kb = [self.ps(f"p1kb{r}", [128, 512], F32, fr) for r in range(2)]
        vb = [self.ps(f"p1vb{r}", [128, 512], F32, fr) for r in range(2)]
        ft = self.ps("p1ft", [128, 512], F32, fr)
        bfb = [self.ps(f"p1bf{r}", [128, 1024], BF16, fr) for r in range(2)]

        self.xb_free = [(self.sX[0], 0), (self.sX[1], 0)]
        self.xb_ready = [0, 0]
        self.xn_free = [(cP, 0), (cP, 0)]
        self.stat_free = [(cV, 0), (cV, 0)]
        self.absmark = [(None, 0), (None, 0)]
        bfb_free = [(cV, 0), (cV, 0)]
        slab_free = [(cP, 0), (cP, 0)]
        slab_ready = [0, 0]
        kb_free = [(cV, 0), (cV, 0)]
        vb_free = [(cV, 0), (cV, 0)]
        ft_free = (cV, 0)
        ktmp_free = (cP, 0)
        ktok_free = (cV, 0)
        krot_free = (cP, 0)
        vtmp_free = (cP, 0)
        prev_mm = 0
        pending = None

        def emit_chain(job):
            nonlocal ft_free, ktmp_free, ktok_free, krot_free, vtmp_free
            is_k, g, Q, em = job
            if is_k:
                def _(v):
                    nonlocal ktmp_free
                    v.wait_ge(cP.sem, em["mm"])
                    c, v0 = ktmp_free
                    v.wait_ge(c.sem, v0)
                    last = v.tensor_copy(ktmp[:, :], kb[g % 2][:, :])
                    cV.inc(last)
                    em["ktmp"] = cV.n
                self.V(_)
                kb_free[g % 2] = (cV, cV.n)

                def _(t):
                    nonlocal ft_free
                    t.wait_ge(cV.sem, em["ktmp"])
                    c, v0 = ft_free
                    t.wait_ge(c.sem, v0)
                    for u in range(4):
                        last = t.transpose(ft[:, 128 * u:128 * (u + 1)],
                                           ktmp[:, 128 * u:128 * (u + 1)],
                                           self.idf[:, :])
                    cP.inc(last)
                    em["ftp"] = cP.n
                self.T(_)
                ktmp_free = (cP, cP.n)

                def _(v):
                    nonlocal ktok_free, krot_free
                    v.wait_ge(cP.sem, em["ftp"])
                    c, v0 = ktok_free
                    v.wait_ge(c.sem, v0)
                    last = v.tensor_copy(ktok[:, :], ft[:, :])
                    cV.inc(last)
                    em["ktok"] = cV.n
                    v.wait_ge(cC.sem, self.w_tabs)
                    c, v0 = krot_free
                    v.wait_ge(c.sem, v0)
                    ce = self.ctabf[:, 256 * Q:256 * Q + 256]
                    se = self.stabf[:, 256 * Q:256 * Q + 256]
                    ke = ktok[:, 0:512:2]
                    ko = ktok[:, 1:512:2]
                    v.tensor_tensor(rt1[:, :], ke, ce, ALU.mult)
                    v.tensor_tensor(rt2[:, :], ko, se, ALU.mult)
                    v.tensor_tensor(krot[:, 0:512:2], rt1[:, :], rt2[:, :],
                                    ALU.subtract)
                    v.tensor_tensor(rt1[:, :], ke, se, ALU.mult)
                    v.tensor_tensor(rt2[:, :], ko, ce, ALU.mult)
                    last = v.tensor_tensor(krot[:, 1:512:2], rt1[:, :], rt2[:, :],
                                           ALU.add)
                    cV.inc(last)
                    em["krot"] = cV.n
                self.V(_)
                ft_free = (cV, cV.n)
                ktok_free = (cV, cV.n)

                def _(t):
                    nonlocal krot_free
                    t.wait_ge(cV.sem, em["krot"])
                    c, v0 = bfb_free[0]
                    t.wait_ge(c.sem, v0)
                    for u in range(4):
                        last = t.transpose(bfb[0][:, 128 * u:128 * (u + 1)],
                                           krot[:, 128 * u:128 * (u + 1)],
                                           self.idb[:, :])
                    cP.inc(last)
                    em["kbt"] = cP.n
                self.T(_)
                krot_free = (cP, cP.n)

                def _(v):
                    v.wait_ge(cP.sem, em["kbt"])
                    last = v.tensor_copy(
                        self.kfeat[:, 2048 * g + 512 * Q:2048 * g + 512 * Q + 512],
                        bfb[0][:, 0:512])
                    cV.inc(last)
                    bfb_free[0] = (cV, cV.n)
                self.V(_)
            else:
                def _(v):
                    nonlocal vtmp_free
                    v.wait_ge(cP.sem, em["mm"])
                    c, v0 = vtmp_free
                    v.wait_ge(c.sem, v0)
                    last = v.tensor_copy(vtmp[:, :], vb[g % 2][:, :])
                    cV.inc(last)
                    em["vtmp"] = cV.n
                self.V(_)
                vb_free[g % 2] = (cV, cV.n)

                def _(t):
                    nonlocal vtmp_free
                    t.wait_ge(cV.sem, em["vtmp"])
                    c, v0 = bfb_free[1]
                    t.wait_ge(c.sem, v0)
                    for u in range(4):
                        last = t.transpose(bfb[1][:, 128 * u:128 * (u + 1)],
                                           vtmp[:, 128 * u:128 * (u + 1)],
                                           self.idb[:, :])
                    cP.inc(last)
                    em["vbt"] = cP.n
                self.T(_)
                vtmp_free = (cP, cP.n)

                def _(v):
                    v.wait_ge(cP.sem, em["vbt"])
                    dst = self.vtok[:, :].rearrange("p (t n) -> p t n", t=16)[
                        :, 4 * Q:4 * Q + 4, 128 * g:128 * g + 128]
                    src = bfb[1][:, 0:512].rearrange("p (u n) -> p u n", u=4)
                    last = v.tensor_copy(dst, src)
                    cV.inc(last)
                    bfb_free[1] = (cV, cV.n)
                self.V(_)

        for Q in range(self.p1_quarters):
            for j, tg in enumerate(range(4 * Q, 4 * Q + self.p1_tiles)):
                r = tg % 2
                tq = tg % 4
                lm = {}
                self._ln_tile(self.xf[128 * tg:128 * (tg + 1), :], r, xnb[r],
                              None if "abs" in self.p1_skip else tg, lm)
                if "tp" in self.p1_skip:
                    continue
                tmarks = {}
                for half in range(2):
                    def _(t, r=r, lm=lm, half=half, tmarks=tmarks):
                        if half == 0:
                            c, v0 = lm["xn"]
                            t.wait_ge(c.sem, v0)
                        for bb in range(2):
                            c, v0 = bfb_free[bb]
                            t.wait_ge(c.sem, v0)
                            for u in range(8):
                                ch = 16 * half + 8 * bb + u
                                last = t.transpose(bfb[bb][:, 128 * u:128 * (u + 1)],
                                                   xnb[r][:, 128 * ch:128 * (ch + 1)],
                                                   self.idb[:, :])
                            cP.inc(last)
                            tmarks[(half, bb)] = cP.n
                    self.T(_)
                    if "evac" in self.p1_skip:
                        continue

                    def _(sc, tq=tq, half=half, tmarks=tmarks, pm=prev_mm):
                        if half == 0:
                            sc.wait_ge(cP.sem, pm)
                        for bb in range(2):
                            sc.wait_ge(cP.sem, tmarks[(half, bb)])
                            c0 = 16 * half + 8 * bb
                            dst = xntq[:, :].rearrange("p (c n) -> p c n", c=NCH)[
                                :, c0:c0 + 8, 128 * tq:128 * tq + 128]
                            src = bfb[bb][:, :].rearrange("p (u n) -> p u n", u=8)
                            last = sc.copy(out=dst, in_=src)
                            cC.inc(last)
                            bfb_free[bb] = (cC, cC.n)
                    self.C(_)
                self.xn_free[r] = (cP, cP.n)
            evac_last = cC.n

            for gi in range(self.p1_groups):
                r = gi % 2
                g = gi % 8
                is_k = gi < 8

                def _(gp, r=r, g=g, is_k=is_k):
                    c, v0 = slab_free[r]
                    gp.wait_ge(c.sem, v0)
                    src = (self.wk if is_k else self.wv)[128 * g:128 * (g + 1), :]
                    last = gp.dma_start(out=wslab[r][:, :], in_=src)
                    self.sWS[r].inc(last)
                    slab_ready[r] = self.sWS[r].n
                self.G(_)

                em = {}

                def _(t, r=r, g=g, is_k=is_k, el=evac_last, em=em):
                    t.wait_ge(self.sWS[r].sem, slab_ready[r])
                    t.wait_ge(cC.sem, el)
                    bank = (kb if is_k else vb)[g % 2]
                    c, v0 = (kb_free if is_k else vb_free)[g % 2]
                    t.wait_ge(c.sem, v0)
                    for ch in range(NCH):
                        last = t.matmul(bank[:, :], wslab[r][:, 128 * ch:128 * (ch + 1)],
                                        xntq[:, 512 * ch:512 * (ch + 1)],
                                        start=(ch == 0), stop=(ch == NCH - 1),
                                        skip_group_check=True)
                    cP.inc(last)
                    em["mm"] = cP.n
                    slab_free[r] = (cP, cP.n)
                self.T(_)

                if pending is not None:
                    emit_chain(pending)
                pending = (is_k, g, Q, em)
            prev_mm = cP.n

        if pending is not None:
            emit_chain(pending)
            pending = None

        if self.debug >= 1:
            def _(sy):
                sy.wait_ge(cC.sem, cC.n)
                sy.wait_ge(cP.sem, cP.n)
                sy.wait_ge(cV.sem, cV.n)
                self.sDbg.inc(sy.dma_start(out=self.dbg["xntq"][:, :], in_=xntq[:, :]))
                self.sDbg.inc(sy.dma_start(out=self.dbg["kfeat"][:, :], in_=self.kfeat[:, :]))
                self.sDbg.inc(sy.dma_start(out=self.dbg["vtok"][:, :], in_=self.vtok[:, :]))
            self.S(_)
        self.fence()
        fr.close()

    def _p2(self):
        cP, cV, cC = self.cP, self.cV, self.cC
        fr2 = contextlib.ExitStack()
        xnt_own = self.sb("p2_xnt", [128, NCH * TOWN], BF16, fr2)
        fra = contextlib.ExitStack()
        self.xbuf = [self.sb(f"p2_x{r}", [128, D], F32, fra) for r in range(2)]
        xnf = [self.sb("p2_xnf0", [128, D], F32, fra)]
        xnf = [xnf[0], xnf[0]]
        gsc = [self.sb(f"p2_gsc{r}", [128, 512], F32, fra) for r in range(2)]
        g3 = [self.sb(f"p2_g3{r}", [128, 4], F32, fra) for r in range(2)]
        coef3 = [self.sb(f"p2_c3{r}", [128, 4], F32, fra) for r in range(2)]
        sm = self.sb("p2_sm", [128, 24], F32, fra)
        amask = self.sb("p2_amask", [128, 16], F32, fra)
        ft2 = [self.ps(f"p2ft{r}", [128, 512], F32, fra) for r in range(2)]
        gp = self.ps("p2gp", [128, 512], F32, fra)
        cp = self.ps("p2cp", [128, 512], F32, fra)

        self.xb_free = [(self.sX[0], self.sX[0].n), (self.sX[1], self.sX[1].n)]
        self.xb_ready = [0, 0]
        self.xn_free = [(cP, 0), (cP, 0)]
        self.stat_free = [(cV, 0), (cV, 0)]
        self.absmark = [(None, 0), (None, 0)]

        rm = {}

        def _(v):
            v.tensor_tensor(amask[:, :], self.absstage[:, :], self.selvt[:, 0:16],
                            ALU.mult)
            v.reduce_sum(self.stat[:, 0:1], amask[:, :], AX.X)
            v.tensor_tensor(amask[:, :], self.absstage[:, :], self.selvt[:, 16:32],
                            ALU.mult)
            last = v.reduce_sum(self.stat[:, 1:2], amask[:, :], AX.X)
            cV.inc(last)
            rm["cols"] = cV.n
        self.V(_)

        def _(t):
            t.wait_ge(cV.sem, rm["cols"])
            t.matmul(gp[0:1, 0:1], self.stat[:, 0:1], self.ones_colf[:, :],
                     start=True, stop=True, skip_group_check=True)
            last = t.matmul(gp[0:1, 1:2], self.stat[:, 1:2], self.ones_colf[:, :],
                            start=True, stop=True, skip_group_check=True)
            cP.inc(last)
            rm["sums"] = cP.n
        self.T(_)

        def _(v):
            v.wait_ge(cP.sem, rm["sums"])
            r = self.ratio
            v.tensor_scalar(r[0:1, 1:2], gp[0:1, 0:1], float(1.0 / (VISLEN * D)),
                            1e-8, ALU.mult, ALU.add)
            v.reciprocal(r[0:1, 2:3], r[0:1, 1:2])
            v.tensor_scalar(r[0:1, 3:4], gp[0:1, 1:2], float(1.0 / ((S - VISLEN) * D)),
                            None, ALU.mult)
            v.tensor_tensor(r[0:1, 0:1], r[0:1, 3:4], r[0:1, 2:3], ALU.mult)
            v.tensor_scalar(self.gcorr[0:1, 0:3], self.gwl[0:1, 0:3], r[0:1, 0:1],
                            None, ALU.mult)
            last = v.tensor_tensor(self.gcorr[0:1, 0:3], self.gcorr[0:1, 0:3],
                                   self.gb[0:1, 0:3], ALU.add)
            cV.inc(last)
            rm["gcorr"] = cV.n
        self.V(_)
        gcorr_ready = rm["gcorr"]
        gp_free = (cV, rm["gcorr"])
        self.stat_free = [(cV, rm["gcorr"]), (cV, rm["gcorr"])]

        ft_free = [(cV, 0), (cV, 0)]
        gsc_free = [(cP, 0), (cP, 0)]
        xnt_marks = []
        for it in range(9):
            isG = (it == 0)
            i = it - 1
            r = it % 2
            src = self.xf[0:128, :] if isG else self.xo[128 * i:128 * (i + 1), :]
            lm = {}
            self._ln_tile(src, r, xnf[r], None, lm)
            tm = {}
            for grp in range(8):
                bb = grp % 2

                def _(t, r=r, lm=lm, tm=tm, grp=grp, bb=bb):
                    if grp == 0:
                        c, v0 = lm["xn"]
                        t.wait_ge(c.sem, v0)
                        t.wait_ge(cV.sem, gcorr_ready)
                    c, v0 = ft_free[bb]
                    t.wait_ge(c.sem, v0)
                    for u in range(4):
                        ch = 4 * grp + u
                        last = t.transpose(ft2[bb][:, 128 * u:128 * (u + 1)],
                                           xnf[r][:, 128 * ch:128 * (ch + 1)],
                                           self.idf[:, :])
                    cP.inc(last)
                    tm[("tp", grp)] = cP.n
                self.T(_)

                def _(v, tm=tm, grp=grp, bb=bb):
                    v.wait_ge(cP.sem, tm[("tp", grp)])
                    c, v0 = gsc_free[bb]
                    v.wait_ge(c.sem, v0)
                    last = v.tensor_copy(gsc[bb][:, :], ft2[bb][:, :])
                    cV.inc(last)
                    tm[("gsc", grp)] = cV.n
                self.V(_)

                if not isG:
                    def _(sc, i=i, tm=tm, grp=grp, bb=bb):
                        sc.wait_ge(cP.sem, tm[("tp", grp)])
                        dst = xnt_own[:, :].rearrange("p (c n) -> p c n", c=NCH)[
                            :, 4 * grp:4 * grp + 4, 128 * i:128 * i + 128]
                        src2 = ft2[bb][:, 0:512].rearrange("p (u n) -> p u n", u=4)
                        last = sc.copy(out=dst, in_=src2)
                        cC.inc(last)
                        tm[("ev", grp)] = cC.n
                    self.C(_)
                    ft_free[bb] = (cC, tm[("ev", grp)])
                else:
                    ft_free[bb] = (cV, tm[("gsc", grp)])

                def _(t, tm=tm, grp=grp, bb=bb, gpf=gp_free, isG=isG):
                    if grp == 0:
                        c, v0 = gpf
                        t.wait_ge(c.sem, v0)
                    t.wait_ge(cV.sem, tm[("gsc", grp)])
                    for u in range(4):
                        ch = 4 * grp + u
                        last = t.matmul(gp[:, 0:3], gsc[bb][:, 128 * u:128 * (u + 1)],
                                        self.gw[:, 3 * ch:3 * ch + 3],
                                        start=(ch == 0), stop=False,
                                        skip_group_check=True)
                    if grp == 7:
                        last = t.matmul(gp[:, 0:3], self.ones_row[:, :],
                                        self.gcorr[0:1, 0:3],
                                        start=False, stop=True, skip_group_check=True)
                        cP.inc(last)
                        tm["gdone"] = cP.n
                self.T(_)
                gsc_free[bb] = (cP, cP.n)
            self.xn_free[r] = (cP, cP.n)
            if not isG:
                xnt_marks.append(tm[("ev", 7)])

            smm = {}

            def _(v, r=r, tm=tm):
                v.wait_ge(cP.sem, tm["gdone"])
                last = v.reduce_max(sm[:, 12 * r:12 * r + 1], gp[:, 0:3], AX.X,
                                    negate=True)
                cV.inc(last)
                smm["negmax"] = cV.n
            self.V(_)

            def _(sc, r=r):
                sc.wait_ge(cV.sem, smm["negmax"])
                last = sc.activation(g3[r][:, 0:3], gp[:, 0:3], AF.Exp,
                                     bias=sm[:, 12 * r:12 * r + 1])
                cC.inc(last)
                smm["exp"] = cC.n
            self.C(_)
            gp_free = (cC, cC.n)

            def _(v, r=r, isG=isG, i=i):
                b = 12 * r
                v.wait_ge(cC.sem, smm["exp"])
                v.reduce_sum(sm[:, b + 1:b + 2], g3[r][:, 0:3], AX.X)
                v.reciprocal(sm[:, b + 2:b + 3], sm[:, b + 1:b + 2])
                v.tensor_scalar(g3[r][:, 0:3], g3[r][:, 0:3], sm[:, b + 2:b + 3],
                                None, ALU.mult)
                v.reduce_max(sm[:, b + 3:b + 4], g3[r][:, 0:3], AX.X)
                v.tensor_reduce(sm[:, b + 4:b + 5], g3[r][:, 0:3], AX.X, ALU.min)
                v.reduce_sum(sm[:, b + 5:b + 6], g3[r][:, 0:3], AX.X)
                v.tensor_tensor(sm[:, b + 6:b + 7], sm[:, b + 5:b + 6],
                                sm[:, b + 3:b + 4], ALU.subtract)
                v.tensor_tensor(sm[:, b + 6:b + 7], sm[:, b + 6:b + 7],
                                sm[:, b + 4:b + 5], ALU.subtract)
                v.tensor_tensor(sm[:, b + 7:b + 8], sm[:, b + 3:b + 4],
                                sm[:, b + 6:b + 7], ALU.add)
                v.tensor_scalar(sm[:, b + 7:b + 8], sm[:, b + 7:b + 8], 1e-8,
                                None, ALU.add)
                v.reciprocal(sm[:, b + 8:b + 9], sm[:, b + 7:b + 8])
                v.tensor_tensor(sm[:, b + 9:b + 10], sm[:, b + 3:b + 4],
                                sm[:, b + 8:b + 9], ALU.mult)
                last = v.tensor_tensor(sm[:, b + 10:b + 11], sm[:, b + 6:b + 7],
                                       sm[:, b + 8:b + 9], ALU.mult)
                cV.inc(last)
                if isG:
                    p3 = g3[r][0:1, 0:3]
                    oh = self.ohrow
                    v.reduce_max(oh[0:1, 6:7], p3, AX.X)
                    v.tensor_scalar(oh[0:1, 0:3], p3, oh[0:1, 6:7], None, ALU.is_ge)
                    v.tensor_scalar(oh[0:1, 7:8], oh[0:1, 0:1], -1.0, 1.0,
                                    ALU.mult, ALU.add)
                    v.tensor_tensor(oh[0:1, 1:2], oh[0:1, 1:2], oh[0:1, 7:8], ALU.mult)
                    v.tensor_scalar(oh[0:1, 6:7], oh[0:1, 1:2], -1.0, 1.0,
                                    ALU.mult, ALU.add)
                    v.tensor_tensor(oh[0:1, 7:8], oh[0:1, 7:8], oh[0:1, 6:7], ALU.mult)
                    v.tensor_tensor(oh[0:1, 2:3], oh[0:1, 2:3], oh[0:1, 7:8], ALU.mult)
                    v.scalar_tensor_tensor(self.ratio[0:1, 4:7], oh[0:1, 0:3], -1e30,
                                           p3, ALU.mult, ALU.add)
                    v.reduce_max(oh[0:1, 6:7], self.ratio[0:1, 4:7], AX.X)
                    v.tensor_scalar(oh[0:1, 3:6], self.ratio[0:1, 4:7], oh[0:1, 6:7],
                                    None, ALU.is_ge)
                    v.tensor_scalar(oh[0:1, 7:8], oh[0:1, 3:4], -1.0, 1.0,
                                    ALU.mult, ALU.add)
                    v.tensor_tensor(oh[0:1, 4:5], oh[0:1, 4:5], oh[0:1, 7:8], ALU.mult)
                    v.tensor_scalar(oh[0:1, 6:7], oh[0:1, 4:5], -1.0, 1.0,
                                    ALU.mult, ALU.add)
                    v.tensor_tensor(oh[0:1, 7:8], oh[0:1, 7:8], oh[0:1, 6:7], ALU.mult)
                    last = v.tensor_tensor(oh[0:1, 5:6], oh[0:1, 5:6], oh[0:1, 7:8],
                                           ALU.mult)
                    cV.inc(last)
                    smm["oh"] = cV.n
                else:
                    for e in range(3):
                        v.tensor_tensor(sm[:, b + 11:b + 12], self.ohb[:, 3 + e:4 + e],
                                        sm[:, b + 10:b + 11], ALU.mult)
                        v.scalar_tensor_tensor(coef3[r][:, e:e + 1],
                                               self.ohb[:, e:e + 1],
                                               sm[:, b + 9:b + 10],
                                               sm[:, b + 11:b + 12],
                                               ALU.mult, ALU.add)
                    last = v.tensor_copy(sm[:, b + 11:b + 12], coef3[r][:, 0:1])
                    cV.inc(last)
                    smm["coef"] = cV.n
            self.V(_)

            if isG:
                def _(t):
                    t.wait_ge(cV.sem, smm["oh"])
                    last = t.matmul(cp[:, 0:6], self.ones_row[:, :],
                                    self.ohrow[0:1, 0:6],
                                    start=True, stop=True, skip_group_check=True)
                    cP.inc(last)
                    smm["ohmm"] = cP.n
                self.T(_)

                def _(v):
                    v.wait_ge(cP.sem, smm["ohmm"])
                    last = v.tensor_copy(self.ohb[:, 0:6], cp[:, 0:6])
                    cV.inc(last)
                self.V(_)
                cp_free = (cV, cV.n)
            else:
                def _(t, i=i):
                    t.wait_ge(cV.sem, smm["coef"])
                    c, v0 = cp_free
                    t.wait_ge(c.sem, v0)
                    last = t.transpose(cp[0:3, 0:128], coef3[r][:, 0:3], self.idf[:, :])
                    cP.inc(last)
                    smm["ctp"] = cP.n
                self.T(_)

                def _(v, i=i):
                    v.wait_ge(cP.sem, smm["ctp"])
                    last = v.tensor_copy(self.coefT[:, 128 * i:128 * (i + 1)],
                                         cp[0:3, 0:128])
                    cV.inc(last)
                self.V(_)
                cp_free = (cV, cV.n)

        xnt_ready = xnt_marks[-1]
        self.fence()
        fra.close()

        frb = contextlib.ExitStack()
        wqslab = [self.sb(f"p2_wq{r}", [128, NCH * 128], BF16, frb) for r in range(2)]
        qtmp = self.sb("p2_qtmp", [128, 1024], F32, frb)
        qtok = self.sb("p2_qtok", [128, 1024], F32, frb)
        qrot = self.sb("p2_qrot", [128, 1024], BF16, frb)
        qstage = [self.sb(f"p2_qs{r}", [128, 1024], BF16, frb) for r in range(2)]
        qa = [self.ps(f"p2qa{r}", [128, 512], F32, frb) for r in range(4)]
        ftq = [self.ps(f"p2ftq{r}", [128, 512], F32, frb) for r in range(2)]
        bfq = self.ps("p2bfq", [128, 1024], BF16, frb)

        slab_free = [(cP, 0), (cP, 0)]
        slab_ready = [0, 0]
        qa_free = [(cV, 0), (cV, 0)]
        qtmp_free = (cP, 0)
        ftq_free = [(cV, 0), (cV, 0)]
        qtok_free = (cV, 0)
        qrot_free = (cP, 0)
        bfq_free = (cV, 0)
        qs_free = [(self.sQS[0], 0), (self.sQS[1], 0)]
        pending = None

        def emit_qchain(job):
            nonlocal qtmp_free, qtok_free, qrot_free, bfq_free
            h, em = job
            p = h % 2

            def _(v):
                nonlocal qtmp_free
                v.wait_ge(cP.sem, em["mm"])
                c, v0 = qtmp_free
                v.wait_ge(c.sem, v0)
                v.tensor_copy(qtmp[:, 0:512], qa[2 * p][:, :])
                last = v.tensor_copy(qtmp[:, 512:1024], qa[2 * p + 1][:, :])
                cV.inc(last)
                em["qtmp"] = cV.n
            self.V(_)
            qa_free[p] = (cV, cV.n)

            def _(t):
                t.wait_ge(cV.sem, em["qtmp"])
                for half in range(2):
                    c, v0 = ftq_free[half]
                    t.wait_ge(c.sem, v0)
                    for u in range(4):
                        last = t.transpose(ftq[half][:, 128 * u:128 * (u + 1)],
                                           qtmp[:, 512 * half + 128 * u:
                                                512 * half + 128 * (u + 1)],
                                           self.idf[:, :])
                    cP.inc(last)
                    em[("ftp", half)] = cP.n
            self.T(_)
            qtmp_free0 = (cP, cP.n)

            def _(v):
                nonlocal qtok_free, qtmp_free, qrot_free
                c, v0 = qtok_free
                v.wait_ge(c.sem, v0)
                for half in range(2):
                    v.wait_ge(cP.sem, em[("ftp", half)])
                    last = v.tensor_copy(qtok[:, 512 * half:512 * (half + 1)],
                                         ftq[half][:, :])
                    cV.inc(last)
                    ftq_free[half] = (cV, cV.n)
                v.wait_ge(cC.sem, self.w_tabs)
                c, v0 = qrot_free
                v.wait_ge(c.sem, v0)
                qe = qtok[:, 0:1024:2]
                qo = qtok[:, 1:1024:2]
                v.tensor_tensor(qtmp[:, 0:512], qe, self.ctabo[:, :], ALU.mult)
                v.tensor_tensor(qtmp[:, 512:1024], qo, self.stabo[:, :], ALU.mult)
                v.tensor_tensor(qrot[:, 0:1024:2], qtmp[:, 0:512], qtmp[:, 512:1024],
                                ALU.subtract)
                v.tensor_tensor(qtmp[:, 0:512], qe, self.stabo[:, :], ALU.mult)
                v.tensor_tensor(qtmp[:, 512:1024], qo, self.ctabo[:, :], ALU.mult)
                last = v.tensor_tensor(qrot[:, 1:1024:2], qtmp[:, 0:512],
                                       qtmp[:, 512:1024], ALU.add)
                cV.inc(last)
                em["rope"] = cV.n
                qtok_free = (cV, cV.n)
                qtmp_free = (cV, cV.n)
            self.V(_)

            def _(t):
                t.wait_ge(cV.sem, em["rope"])
                c, v0 = bfq_free
                t.wait_ge(c.sem, v0)
                for u in range(8):
                    last = t.transpose(bfq[:, 128 * u:128 * (u + 1)],
                                       qrot[:, 128 * u:128 * (u + 1)], self.idb[:, :])
                cP.inc(last)
                em["bft"] = cP.n
            self.T(_)
            qrot_free = (cP, cP.n)

            def _(v):
                nonlocal bfq_free
                v.wait_ge(cP.sem, em["bft"])
                c, v0 = qs_free[p]
                v.wait_ge(c.sem, v0)
                last = v.tensor_copy(qstage[p][:, :], bfq[:, :])
                cV.inc(last)
                em["qs"] = cV.n
                bfq_free = (cV, cV.n)
            self.V(_)

            def _(sy):
                sy.wait_ge(cV.sem, em["qs"])
                last = sy.dma_start(out=self.q_hbm[:, 1024 * h:1024 * (h + 1)],
                                    in_=qstage[p][:, :])
                self.sQS[p].inc(last)
                qs_free[p] = (self.sQS[p], self.sQS[p].n)
            self.S(_)

        for h in range(NH):
            r = h % 2
            p = h % 2

            def _(gp2, h=h, r=r):
                c, v0 = slab_free[r]
                gp2.wait_ge(c.sem, v0)
                last = gp2.dma_start(out=wqslab[r][:, :],
                                     in_=self.wq[128 * h:128 * (h + 1), :])
                self.sWS[r].inc(last)
                slab_ready[r] = self.sWS[r].n
            self.G(_)

            hm = {}

            def _(t, h=h, r=r, p=p, hm=hm, xr=xnt_ready):
                t.wait_ge(self.sWS[r].sem, slab_ready[r])
                t.wait_ge(cC.sem, xr)
                c, v0 = qa_free[p]
                t.wait_ge(c.sem, v0)
                for ch in range(NCH):
                    t.matmul(qa[2 * p][:, :], wqslab[r][:, 128 * ch:128 * (ch + 1)],
                             xnt_own[:, 1024 * ch:1024 * ch + 512],
                             start=(ch == 0), stop=(ch == NCH - 1),
                             skip_group_check=True)
                    last = t.matmul(qa[2 * p + 1][:, :],
                                    wqslab[r][:, 128 * ch:128 * (ch + 1)],
                                    xnt_own[:, 1024 * ch + 512:1024 * ch + 1024],
                                    start=(ch == 0), stop=(ch == NCH - 1),
                                    skip_group_check=True)
                cP.inc(last)
                hm["mm"] = cP.n
                slab_free[r] = (cP, cP.n)
            self.T(_)

            if pending is not None:
                emit_qchain(pending)
            pending = (h, hm)
        emit_qchain(pending)
        pending = None

        def _(gp2):
            gp2.wait_ge(cP.sem, cP.n)
            last = gp2.dma_start(out=self.xnt_hbm[:, :], in_=xnt_own[:, :])
            self.sXnt.inc(last)
        self.G(_)
        if self.debug >= 2:
            def _(sy):
                sy.wait_ge(self.sXnt.sem, self.sXnt.n)
                for c in (self.sQS[0], self.sQS[1]):
                    sy.wait_ge(c.sem, c.n)
                self.sDbg.inc(sy.dma_start(out=self.dbg["xnt"][:, :], in_=xnt_own[:, :]))
                self.sDbg.inc(sy.dma_start(out=self.dbg["qh"][:, :], in_=self.q_hbm[:, :]))
            self.S(_)
        if self.debug >= 1:
            def _(sy):
                sy.wait_ge(cV.sem, cV.n)
                self.sDbg.inc(sy.dma_start(out=self.dbg["misc"][:, 0:8], in_=self.ohb[:, :]))
                self.sDbg.inc(sy.dma_start(out=self.dbg["misc"][0:1, 8:16],
                                           in_=self.ratio[0:1, :]))
                self.sDbg.inc(sy.dma_start(out=self.dbg["misc"][0:3, 16:32],
                                           in_=self.coefT[:, 0:16]))
            self.S(_)
        if self.debug >= 2:
            def _(sy):
                sy.wait_ge(cV.sem, cV.n)
                for i in range(8):
                    self.sDbg.inc(sy.dma_start(out=self.dbg["coef"][0:3, 4 * i:4 * i + 4],
                                               in_=self.coefT[:, 128 * i:128 * i + 4]))
            self.S(_)
        self.fence()
        frb.close()
        fr2.close()

    def _p3(self):
        cP, cV, cC = self.cP, self.cV, self.cC
        fr = contextlib.ExitStack()
        masks = self.sb("p3_masks", [128, 16 * 512], BF16, fr)
        qbuf = [self.sb(f"p3_q{r}", [128, 1024], BF16, fr) for r in range(3)]
        probs = self.sb("p3_probs", [128, 24 * 512], BF16, fr)
        denrow = self.sb("p3_den", [1, 1024], F32, fr)
        rbb = [self.sb(f"p3_rb{r}", [128, 512], F32, fr) for r in range(2)]
        cstage = [self.sb(f"p3_cs{r}", [128, 512], BF16, fr) for r in range(2)]
        sbk = self.ps("p3s", [128, 2048], F32, fr)
        cxb = [self.ps(f"p3c{r}", [128, 512], F32, fr) for r in range(2)]
        dnb = [self.ps(f"p3d{r}", [128, 512], F32, fr) for r in range(2)]

        def _(sy):
            last = sy.dma_start(out=masks[:, :].rearrange("p (j n) -> p j n", j=16),
                                in_=self.masksin[:, :, :].rearrange("j p n -> p j n"))
            self.sMask.inc(last)
        self.S(_)
        masks_ready = self.sMask.n

        qb_free = [(cP, 0)] * 3
        qb_ready = [0, 0, 0]
        s_free = [(cC, 0)] * 4
        probs_free = [(cP, 0)] * 24
        cx_free = [(cV, 0), (cV, 0)]
        dn_free = [(cV, 0), (cV, 0)]
        rb_free = (cV, 0)
        rbb_free = [(cV, 0), (cV, 0)]
        cs_free = [(self.sCS[0], 0), (self.sCS[1], 0)]

        pending = None

        def emit_B(job):
            h, qb, kmax, rr, g, pready, em = job
            slot0 = 16 if qb == 0 else 0

            def _(t):
                best = {}
                for c, v0 in pready:
                    best[id(c)] = (c, max(v0, best.get(id(c), (c, 0))[1]))
                for c, v0 in best.values():
                    t.wait_ge(c.sem, v0)
                c, v0 = cx_free[rr]
                t.wait_ge(c.sem, v0)
                c, v0 = dn_free[rr]
                t.wait_ge(c.sem, v0)
                for kt in range(kmax):
                    sl = slot0 + kt
                    t.matmul(cxb[rr][:, :],
                             self.vtok[:, 1024 * kt + 128 * g:
                                       1024 * kt + 128 * (g + 1)],
                             probs[:, 512 * sl:512 * (sl + 1)],
                             start=(kt == 0), stop=(kt == kmax - 1),
                             skip_group_check=True)
                    last = t.matmul(dnb[rr][0:1, :], self.ones_colb[:, :],
                                    probs[:, 512 * sl:512 * (sl + 1)],
                                    start=(kt == 0), stop=(kt == kmax - 1),
                                    skip_group_check=True)
                    cP.inc(last)
                    probs_free[sl] = (cP, cP.n)
                em["cd"] = cP.n
            self.T(_)

            def _(v):
                v.wait_ge(cP.sem, em["cd"])
                v.tensor_copy(denrow[0:1, 512 * rr:512 * (rr + 1)],
                              dnb[rr][0:1, :])
                last = v.reciprocal(denrow[0:1, 512 * rr:512 * (rr + 1)],
                                    denrow[0:1, 512 * rr:512 * (rr + 1)])
                cV.inc(last)
                em["rec"] = cV.n
            self.V(_)

            def _(t):
                t.wait_ge(cV.sem, em["rec"])
                last = t.matmul(dnb[rr][:, :], self.ones_row[:, :],
                                denrow[0:1, 512 * rr:512 * (rr + 1)],
                                start=True, stop=True, skip_group_check=True)
                cP.inc(last)
                em["rbmm"] = cP.n
            self.T(_)

            def _(v):
                v.wait_ge(cP.sem, em["rbmm"])
                c, v0 = rbb_free[rr]
                v.wait_ge(c.sem, v0)
                v.tensor_copy(rbb[rr][:, :], dnb[rr][:, :])
                c, v0 = cs_free[rr]
                v.wait_ge(c.sem, v0)
                last = v.tensor_tensor(cstage[rr][:, :], cxb[rr][:, :],
                                       rbb[rr][:, :], ALU.mult)
                cV.inc(last)
                em["cs"] = cV.n
            self.V(_)
            dn_free[rr] = (cV, cV.n)
            cx_free[rr] = (cV, cV.n)
            rbb_free[rr] = (cV, cV.n)

            def _(sy):
                sy.wait_ge(cV.sem, em["cs"])
                dst = self.ctx_hbm[:, :].rearrange("p (t n) -> p t n", t=8)[
                    :, 4 * qb:4 * qb + 4, 128 * h:128 * (h + 1)]
                src2 = cstage[rr][:, :].rearrange("p (u n) -> p u n", u=4)
                last = sy.dma_start(out=dst, in_=src2)
                self.sCS[rr].inc(last)
                cs_free[rr] = (self.sCS[rr], self.sCS[rr].n)
            self.S(_)

        for h in range(NH):
            qr = h % 3
            g = h // 4

            def _(gq, h=h, qr=qr):
                c, v0 = qb_free[qr]
                gq.wait_ge(c.sem, v0)
                gq.wait_ge(self.sXnt.sem, self.sXnt.n)
                last = gq.dma_start(out=qbuf[qr][:, :],
                                    in_=self.q_hbm[:, 1024 * h:1024 * (h + 1)])
                self.sQB[qr].inc(last)
                qb_ready[qr] = self.sQB[qr].n
            self.G(_)

            for qb in range(2):
                kmax = 8 if qb == 0 else 16
                rr = (2 * h + qb) % 2
                slot0 = 16 if qb == 0 else 0
                expM = [cC.n + (j // 4) + 1 for j in range(kmax)]
                sA = {}

                def _(t, h=h, qb=qb, kmax=kmax, g=g, qr=qr, expM=expM, sA=sA):
                    t.wait_ge(self.sQB[qr].sem, qb_ready[qr])
                    c, v0 = max((cv for cv in s_free), key=lambda cv: cv[1])
                    t.wait_ge(c.sem, v0)
                    lastw = None
                    for kt in range(kmax):
                        sb_ = kt % 4
                        if kt >= 4 and expM[kt - 4] != lastw:
                            t.wait_ge(cC.sem, expM[kt - 4])
                            lastw = expM[kt - 4]
                        last = t.matmul(
                            sbk[:, 512 * sb_:512 * (sb_ + 1)],
                            self.kfeat[:, 2048 * g + 128 * kt:2048 * g + 128 * (kt + 1)],
                            qbuf[qr][:, 512 * qb:512 * (qb + 1)],
                            start=True, stop=True, skip_group_check=True)
                        cP.inc(last)
                        sA[kt] = cP.n
                self.T(_)

                def _(sc, qb=qb, kmax=kmax, sA=sA, slot0=slot0):
                    for j in range(kmax // 4):
                        k0 = 4 * j
                        sc.wait_ge(cP.sem, sA[k0 + 3])
                        for kk in range(k0, k0 + 4):
                            c, v0 = probs_free[slot0 + kk]
                            sc.wait_ge(c.sem, v0)
                        sl = slot0 + k0
                        sc.activation(probs[:, 512 * sl:512 * (sl + 4)],
                                      sbk[:, :],
                                      AF.Exp, bias=self.negb_col[:, 0:1],
                                      scale=SCALE)
                self.C(_)
                assert cC.n == expM[-1], (cC.n, expM[-1])
                for kt in range(kmax):
                    s_free[kt % 4] = (cC, expM[kt])
                pready = [(cC, expM[kt]) for kt in range(kmax)]

                def _(v, qb=qb, kmax=kmax, mr=masks_ready, expM=expM,
                      pready=pready, slot0=slot0):
                    first = True
                    for kt in range(kmax):
                        if (qb == 0) or (kt >= 8):
                            v.wait_ge(cC.sem, expM[kt])
                            if first:
                                v.wait_ge(self.sMask.sem, mr)
                                first = False
                            sl = slot0 + kt
                            v.tensor_tensor(probs[:, 512 * sl:512 * (sl + 1)],
                                            probs[:, 512 * sl:512 * (sl + 1)],
                                            masks[:, 512 * kt:512 * (kt + 1)],
                                            ALU.mult)
                            pready[kt] = (cV, cV.n)
                self.V(_)

                if pending is not None:
                    emit_B(pending)
                pending = (h, qb, kmax, rr, g, pready, {})
            qb_free[qr] = (cP, cP.n)
        emit_B(pending)

        if self.debug >= 3:
            def _(sy):
                for c in self.sCS:
                    sy.wait_ge(c.sem, c.n)
                self.sDbg.inc(sy.dma_start(out=self.dbg["ctx"][:, :],
                                           in_=self.ctx_hbm[:, :]))
            self.S(_)
        self.fence()
        fr.close()

    def _p4(self):
        cP, cV, cC = self.cP, self.cV, self.cC
        fr = contextlib.ExitStack()
        ctxbuf = self.sb("p4_ctx", [128, 8 * D], BF16, fr)
        woslab = [self.sb(f"p4_wo{r}", [128, NCH * 512], BF16, fr) for r in range(2)]
        self.b2 = self.sb("p4_b2", [3, D], F32, fr)
        xcol = [self.sb(f"p4_x{r}", [128, 512], F32, fr) for r in range(3)]
        outsb = [self.sb(f"p4_o{r}", [128, 512], F32, fr) for r in range(2)]
        ob = [self.ps(f"p4b{r}", [128, 512], F32, fr) for r in range(4)]

        def _(sy):
            self.sMask.inc(sy.dma_start(out=self.b2[:, :], in_=self.b2in[:, :]))
            for c in self.sCS:
                sy.wait_ge(c.sem, c.n)
            self.sMask.inc(sy.dma_start(out=ctxbuf[:, :], in_=self.ctx_hbm[:, :]))
        self.S(_)
        b2_ready = self.sMask.n

        slab_free = [(cP, 0), (cP, 0)]
        slab_ready = [0, 0]
        x_free = [(cV, 0)] * 3
        x_ready = [0] * 3
        o_free = [(cV, 0)] * 4
        os_free = [(self.sOS[0], 0), (self.sOS[1], 0)]

        jobs = [(db, tau) for db in range(8) for tau in range(8)]
        for j, (db, tau) in enumerate(jobs):
            r = db % 2
            cr = j % 3
            bk = j % 4
            if tau == 0:
                def _(gp2, db=db, r=r):
                    c, v0 = slab_free[r]
                    gp2.wait_ge(c.sem, v0)
                    last = gp2.dma_start(out=woslab[r][:, :],
                                         in_=self.wo[128 * db:128 * (db + 1), :])
                    self.sWS[r].inc(last)
                    slab_ready[r] = self.sWS[r].n
                self.G(_)

            def _(sy, db=db, tau=tau, cr=cr):
                c, v0 = x_free[cr]
                sy.wait_ge(c.sem, v0)
                last = sy.dma_start(out=xcol[cr][:, :],
                                    in_=self.xo[128 * tau:128 * (tau + 1),
                                                512 * db:512 * (db + 1)])
                self.sX[cr].inc(last)
                x_ready[cr] = self.sX[cr].n
            self.S(_)

            em = {}

            def _(t, db=db, tau=tau, cr=cr, r=r, bk=bk, em=em):
                t.wait_ge(self.sMask.sem, b2_ready)
                t.wait_ge(self.sWS[r].sem, slab_ready[r])
                c, v0 = o_free[bk]
                t.wait_ge(c.sem, v0)
                for hh in range(NCH):
                    t.matmul(ob[bk][:, :],
                             ctxbuf[:, D * tau + 128 * hh:D * tau + 128 * (hh + 1)],
                             woslab[r][:, 512 * hh:512 * (hh + 1)],
                             start=(hh == 0), stop=False, skip_group_check=True)
                last = t.matmul(ob[bk][:, :], self.coefT[:, 128 * tau:128 * (tau + 1)],
                                self.b2[:, 512 * db:512 * (db + 1)],
                                start=False, stop=True, skip_group_check=True)
                cP.inc(last)
                em["mm"] = cP.n
                slab_free[r] = (cP, cP.n)
            self.T(_)

            def _(v, cr=cr, bk=bk, em=em):
                v.wait_ge(cP.sem, em["mm"])
                v.wait_ge(self.sX[cr].sem, x_ready[cr])
                c, v0 = os_free[j % 2]
                v.wait_ge(c.sem, v0)
                last = v.tensor_tensor(outsb[j % 2][:, :], ob[bk][:, :],
                                       xcol[cr][:, :], ALU.add)
                cV.inc(last)
                em["add"] = cV.n
            self.V(_)
            o_free[bk] = (cV, cV.n)
            x_free[cr] = (cV, cV.n)

            def _(sc, db=db, tau=tau, em=em):
                sc.wait_ge(cV.sem, em["add"])
                last = sc.dma_start(out=self.scr[128 * tau:128 * (tau + 1),
                                                 512 * db:512 * (db + 1)],
                                    in_=outsb[j % 2][:, :])
                self.sOS[j % 2].inc(last)
                os_free[j % 2] = (self.sOS[j % 2], self.sOS[j % 2].n)
            self.C(_)
        if self.debug >= 4:
            def _(sy):
                for c in self.sOS:
                    sy.wait_ge(c.sem, c.n)
                self.sDbg.inc(sy.dma_start(out=self.dbg["scr"][:, :],
                                           in_=self.scr[:, :]))
            self.S(_)
        self.fence()
        fr.close()

    def _p5(self):
        cP, cV, cC = self.cP, self.cV, self.cC
        fr = contextlib.ExitStack()
        act_sb = self.sb("p5_act", [128, NCH * TOWN], BF16, fr)
        coefb = self.sb("p5_coefb", [128, 3 * TOWN], BF16, fr)
        tnf = [self.sb(f"p5_tn{r}", [128, 512], F32, fr) for r in range(4)]
        tn_free = [(cV, 0)] * 4

        cbp = contextlib.ExitStack()
        cb = [self.ps(f"p5cb{r}", [128, 512], F32, cbp) for r in range(2)]
        em0 = {}
        cb_free = (cC, 0)
        for e in range(3):
            def _(t, e=e, cbf=cb_free):
                c, v0 = cbf
                t.wait_ge(c.sem, v0)
                for tb in range(2):
                    last = t.matmul(cb[tb][:, :], self.sel3[:, 128 * e:128 * (e + 1)],
                                    self.coefT[0:3, 512 * tb:512 * (tb + 1)],
                                    start=True, stop=True, skip_group_check=True)
                cP.inc(last)
                em0[e] = cP.n
            self.T(_)

            def _(sc, e=e):
                sc.wait_ge(cP.sem, em0[e])
                for tb in range(2):
                    last = sc.activation(
                        coefb[:, 1024 * e + 512 * tb:1024 * e + 512 * (tb + 1)],
                        cb[tb][:, :], AF.Identity)
                cC.inc(last)
            self.C(_)
            cb_free = (cC, cC.n)
        coefb_ready = self.cC.n
        self.fence()
        cbp.close()

        act_free = (cP, 0)
        for e in range(3):
            frp = contextlib.ExitStack()
            xnt5 = self.sb(f"p5_xnt{e}", [128, NCH * TOWN], BF16, frp)
            w1slab = [self.sb(f"p5_w1{e}_{r}", [128, NCH * 256], BF16, frp)
                      for r in range(2)]
            pp = [self.ps(f"p5p{e}_{r}", [128, 512], F32, frp) for r in range(8)]

            def _(gp2):
                gp2.wait_ge(self.sXnt.sem, self.sXnt.n)
                last = gp2.dma_start(out=xnt5[:, :], in_=self.xnt_hbm[:, :])
                self.sXnt.inc(last)
            self.G(_)
            xnt5_ready = self.sXnt.n

            slab_free = [(cP, 0), (cP, 0)]
            slab_ready = [0, 0]
            pp_free = [(cV, 0)] * 8
            for mg in range(16):
                r = mg % 2

                def _(gp2, e=e, mg=mg, r=r):
                    c, v0 = slab_free[r]
                    gp2.wait_ge(c.sem, v0)
                    last = gp2.dma_start(out=w1slab[r][:, :],
                                         in_=self.w1[e][128 * mg:128 * (mg + 1), :])
                    self.sWS[r].inc(last)
                    slab_ready[r] = self.sWS[r].n
                self.G(_)

                em = {}

                def _(t, mg=mg, r=r, em=em, xr=xnt5_ready):
                    t.wait_ge(self.sWS[r].sem, slab_ready[r])
                    t.wait_ge(self.sXnt.sem, xr)
                    base = 4 * (mg % 2)
                    for q in range(4):
                        c, v0 = pp_free[base + q]
                        t.wait_ge(c.sem, v0)
                    for ch in range(NCH):
                        for jj in range(2):
                            for tb in range(2):
                                last = t.matmul(
                                    pp[base + 2 * jj + tb][:, :],
                                    w1slab[r][:, 256 * ch + 128 * jj:
                                              256 * ch + 128 * (jj + 1)],
                                    xnt5[:, 1024 * ch + 512 * tb:
                                         1024 * ch + 512 * (tb + 1)],
                                    start=(ch == 0), stop=(ch == NCH - 1),
                                    skip_group_check=True)
                    cP.inc(last)
                    em["mm"] = cP.n
                    slab_free[r] = (cP, cP.n)
                self.T(_)

                emx = {}

                def _(sc, e=e, mg=mg, em=em, emx=emx):
                    sc.wait_ge(cP.sem, em["mm"])
                    afunc = [AF.Tanh, AF.Gelu, AF.Silu][e]
                    base = 4 * (mg % 2)
                    for q in range(4):
                        jj, tb = q // 2, q % 2
                        fc = 2 * mg + jj
                        c, v0 = tn_free[q]
                        sc.wait_ge(c.sem, v0)
                        last = sc.activation(tnf[q][:, :], pp[base + q][:, :],
                                             afunc,
                                             bias=self.b1t[:, 32 * e + fc:
                                                           32 * e + fc + 1])
                        cC.inc(last)
                        emx[q] = cC.n
                self.C(_)
                base = 4 * (mg % 2)
                for q in range(4):
                    pp_free[base + q] = (cC, emx[q])

                def _(v, e=e, mg=mg, emx=emx, afm=act_free):
                    c, v0 = afm
                    v.wait_ge(c.sem, v0)
                    for q in range(4):
                        jj, tb = q // 2, q % 2
                        fc = 2 * mg + jj
                        v.wait_ge(cC.sem, emx[q])
                        dst = act_sb[:, 1024 * fc + 512 * tb:
                                     1024 * fc + 512 * (tb + 1)]
                        cslice = coefb[:, 1024 * e + 512 * tb:
                                       1024 * e + 512 * (tb + 1)]
                        last = v.tensor_tensor(dst, tnf[q][:, :], cslice, ALU.mult)
                        cV.inc(last)
                        tn_free[q] = (cV, cV.n)
                self.V(_)
            act_ready = cV.n
            self.fence()
            frp.close()

            fro = contextlib.ExitStack()
            w2slab = [self.sb(f"p5_w2{e}_{r}", [128, NCH * 512], BF16, fro)
                      for r in range(2)]
            scrin = [self.sb(f"p5_si{e}_{r}", [128, 512], F32, fro) for r in range(2)]
            outsb = [self.sb(f"p5_ou{e}_{r}", [128, 512], F32, fro) for r in range(2)]
            ob = [self.ps(f"p5o{e}_{r}", [128, 512], F32, fro) for r in range(4)]

            slab_free = [(cP, 0), (cP, 0)]
            slab_ready = [0, 0]
            o_free = [(cV, 0)] * 4
            si_free = [(cV, 0), (cV, 0)]
            si_ready = [0, 0]
            os_free = [(self.sOS[0], self.sOS[0].n), (self.sOS[1], self.sOS[1].n)]
            dstT = self.scr if e < 2 else self.out

            for db in range(8):
                r = db % 2

                def _(gp2, e=e, db=db, r=r):
                    c, v0 = slab_free[r]
                    gp2.wait_ge(c.sem, v0)
                    last = gp2.dma_start(out=w2slab[r][:, :],
                                         in_=self.w2[e][128 * db:128 * (db + 1), :])
                    self.sWS[r].inc(last)
                    slab_ready[r] = self.sWS[r].n
                self.G(_)

                for tau in range(8):
                    cr = tau % 2
                    bk = (8 * db + tau) % 4
                    em = {}

                    def _(sy, db=db, tau=tau, cr=cr):
                        c, v0 = si_free[cr]
                        sy.wait_ge(c.sem, v0)
                        last = sy.dma_start(out=scrin[cr][:, :],
                                            in_=self.scr[128 * tau:128 * (tau + 1),
                                                         512 * db:512 * (db + 1)])
                        self.sSI[cr].inc(last)
                        si_ready[cr] = self.sSI[cr].n
                    self.S(_)

                    def _(t, tau=tau, r=r, bk=bk, em=em, ar=act_ready):
                        t.wait_ge(self.sWS[r].sem, slab_ready[r])
                        t.wait_ge(cV.sem, ar)
                        c, v0 = o_free[bk]
                        t.wait_ge(c.sem, v0)
                        for fc in range(NCH):
                            last = t.matmul(
                                ob[bk][:, :],
                                act_sb[:, 1024 * fc + 128 * tau:
                                       1024 * fc + 128 * (tau + 1)],
                                w2slab[r][:, 512 * fc:512 * (fc + 1)],
                                start=(fc == 0), stop=(fc == NCH - 1),
                                skip_group_check=True)
                        cP.inc(last)
                        em["mm"] = cP.n
                        slab_free[r] = (cP, cP.n)
                    self.T(_)

                    def _(v, cr=cr, bk=bk, em=em):
                        v.wait_ge(cP.sem, em["mm"])
                        v.wait_ge(self.sSI[cr].sem, si_ready[cr])
                        c, v0 = os_free[cr]
                        v.wait_ge(c.sem, v0)
                        last = v.tensor_tensor(outsb[cr][:, :], ob[bk][:, :],
                                               scrin[cr][:, :], ALU.add)
                        cV.inc(last)
                        em["add"] = cV.n
                    self.V(_)
                    o_free[bk] = (cV, cV.n)
                    si_free[cr] = (cV, cV.n)

                    def _(sc, db=db, tau=tau, cr=cr, em=em, dstT=dstT):
                        sc.wait_ge(cV.sem, em["add"])
                        last = sc.dma_start(out=dstT[128 * tau:128 * (tau + 1),
                                                     512 * db:512 * (db + 1)],
                                            in_=outsb[cr][:, :])
                        self.sOS[cr].inc(last)
                        os_free[cr] = (self.sOS[cr], self.sOS[cr].n)
                    self.C(_)
            act_free = (cP, cP.n)
            self.fence()
            fro.close()
        fr.close()

    def _finish(self):
        def _(sy):
            for c in self._alldsems:
                if c.n > 0:
                    sy.wait_ge(c.sem, c.n)
        self.S(_)


_CACHE = {}


def get_prog(debug=0, phases=5, **kw):
    key = (debug, phases, tuple(sorted(kw.items())))
    if key not in _CACHE:
        _CACHE[key] = Prog(debug, phases, **kw).build()
    return _CACHE[key]


def _swz(w, nb, bw):
    d = w.shape[0]
    return np.ascontiguousarray(
        w.reshape(d // 128, 128, nb, bw).transpose(2, 1, 0, 3)
        .reshape(nb * 128, (d // 128) * bw))


_WCACHE = {"key": None, "w": None}


def _weights(inputs):
    key = id(inputs.get("L0_attn_wq"))
    if _WCACHE["key"] != key:
        w1f = np.asarray(inputs["L0_ffn_w1"], dtype=np.float32)
        w2f = np.asarray(inputs["L0_ffn_w2"], dtype=np.float32)
        _WCACHE["w"] = {
            "wq": _swz(np.asarray(inputs["L0_attn_wq"], np.float32), 32, 128),
            "wk": _swz(np.asarray(inputs["L0_attn_wk"], np.float32), 8, 128),
            "wv": _swz(np.asarray(inputs["L0_attn_wv"], np.float32), 8, 128),
            "wo": _swz(np.asarray(inputs["L0_attn_wo"], np.float32), 8, 512),
            "w1": np.stack([_swz(w1f[e], 16, 256) for e in range(3)]),
            "w2": np.stack([_swz(w2f[e], 8, 512) for e in range(3)]),
        }
        _WCACHE["key"] = key
    return _WCACHE["w"]


def host_inputs(inputs, core):
    beta, par = core // 2, core % 2
    x = np.ascontiguousarray(np.asarray(inputs["L0_attn_x"], dtype=np.float32)
                             .reshape(4, S, D)[beta])
    rows = np.concatenate([np.arange(128 * (2 * i + par), 128 * (2 * i + par) + 128)
                           for i in range(8)])
    xo = np.ascontiguousarray(x[rows])
    gw = np.asarray(inputs["L0_ffn_gate_w"], dtype=np.float32)
    gwpc = np.ascontiguousarray(
        gw[:4096].reshape(32, 128, 3).transpose(1, 0, 2).reshape(128, 96))
    b1 = np.asarray(inputs["L0_ffn_b1"], dtype=np.float32)
    b1pc = np.ascontiguousarray(
        b1.reshape(3, 32, 128).transpose(2, 0, 1).reshape(128, 96))
    masks = np.zeros((16, 128, 512), dtype=np.float32)
    for j in range(16):
        qb = 0 if j < 8 else 1
        kt = j
        for sl in range(4):
            g = 2 * (4 * qb + sl) + par
            qpos = 128 * g + np.arange(128)[None, :]
            kpos = 128 * kt + np.arange(128)[:, None]
            masks[j][:, 128 * sl:128 * (sl + 1)] = (kpos <= qpos)
    selvt = np.zeros((128, 32), dtype=np.float32)
    for t in range(16):
        tok = 128 * t + np.arange(128)
        selvt[:, t] = (tok < VISLEN)
        selvt[:, 16 + t] = (tok >= VISLEN)
    WS = _weights(inputs)
    return {
        "xf": x, "xo": xo,
        "wq": WS["wq"], "wk": WS["wk"], "wv": WS["wv"], "wo": WS["wo"],
        "w1": WS["w1"], "w2": WS["w2"],
        "gwpc": gwpc,
        "gwlast": np.ascontiguousarray(gw[4096:4097]),
        "gbin": np.asarray(inputs["L0_ffn_gate_b"], dtype=np.float32).reshape(1, 3),
        "b1pc": b1pc,
        "b2in": np.asarray(inputs["L0_ffn_b2"], dtype=np.float32),
        "masksin": masks.astype(ml_dtypes.bfloat16),
        "selvtin": selvt,
        "parcolin": np.full((128, 1), 128.0 * par, dtype=np.float32),
        "idfin": np.eye(128, dtype=np.float32),
        "idbin": np.eye(128, dtype=ml_dtypes.bfloat16),
    }


def kernel(**inputs):
    prog = get_prog(0)
    in_maps = [host_inputs(inputs, c) for c in range(8)]
    res = run_bass_kernel_spmd(prog.nc, in_maps, core_ids=list(range(8)))
    full = np.zeros((4, S, D), dtype=np.float32)
    for c in range(8):
        beta, par = c // 2, c % 2
        o = np.asarray(res.results[c]["out"]).reshape(8, 128, D)
        for i in range(8):
            full[beta, 128 * (2 * i + par):128 * (2 * i + par) + 128] = o[i]
    return full
